# version 03
# baseline (speedup 1.0000x reference)
import contextlib
import numpy as np
import ml_dtypes
import concourse.bass as bass
import concourse.mybir as mybir

F32 = mybir.dt.float32
BF16 = mybir.dt.bfloat16
U8 = mybir.dt.uint8
U32 = mybir.dt.uint32
ALU = mybir.AluOpType
ACT = mybir.ActivationFunctionType
AX = mybir.AxisListType.X

T_TOK, DM, NCHUNK = 4096, 1024, 32
DI, NH, DS, HD, DPROJ = 2048, 32, 64, 64, 8224
E0, CAP0G, LCAP0 = 8, 8192, 1408
E1, CAP1G, LCAP1 = 16, 2560, 512
SLOTS0, SLOTS1, TRASH = E0 * LCAP0, E1 * LCAP1, 128
H1, DH1, FF = 16, 64, 4096
TSEG, NSEG, LC = 256, 16, 128
HALF = 256
NT = 16
LEVELS = (1, 2, 4, 8, 16, 32, 64)


def _bf(x):
    return np.ascontiguousarray(np.asarray(x, dtype=np.float32).astype(ml_dtypes.bfloat16))


def host_inputs(core_id, x_shard, W, ncores=8):
    perm = list(range(0, 4096))
    for h in range(NH):
        perm += [4096 + 129 * h + s for s in range(64)]
    for h in range(NH):
        perm += [4096 + 129 * h + 64 + s for s in range(64)]
    perm += [4096 + 129 * h + 128 for h in range(NH)]
    al = np.empty((128, 16), np.float32)
    dr = np.empty((128, 16), np.float32)
    cw = np.empty((128, 64), np.float32)
    cb = np.empty((128, 16), np.float32)
    for tt in range(16):
        for p in range(128):
            al[p, tt] = W["L0_attn_A_log"][2 * tt + p // 64, p % 64]
            dr[p, tt] = W["L0_attn_D"][2 * tt + p // 64]
            cb[p, tt] = W["L0_attn_conv_b"][128 * tt + p]
            for k in range(4):
                cw[p, 4 * tt + k] = W["L0_attn_conv_w"][128 * tt + p, k]
    hsel = np.zeros((32, 2048), np.float32)
    for tt in range(16):
        for a in range(2):
            hsel[2 * tt + a, 128 * tt + 64 * a:128 * tt + 64 * a + 64] = 1.0
    bones16 = np.zeros((128, 128), np.float32)
    zsel = np.zeros((16, 1024), np.float32)
    for dt_ in range(8):
        for p in range(128):
            h = 2 * dt_ + p // 64
            bones16[p, 16 * dt_ + h] = 1.0
            zsel[h, 128 * dt_ + p] = 1.0
    pref = np.zeros((ncores, 1), np.float32)
    pref[:core_id] = 1.0
    d = {
        "x": np.ascontiguousarray(x_shard.astype(np.float32)),
        "W_inT": np.ascontiguousarray(W["L0_attn_W_in"][perm, :].T),
        "W_outT": np.ascontiguousarray(W["L0_attn_W_out"].T),
        "conv_w_r": cw, "conv_b_r": cb, "A_log_r": al, "D_r": dr,
        "dt_bias_c": np.ascontiguousarray(W["L0_attn_dt_bias"].reshape(32, 1)),
        "wg0": np.ascontiguousarray(W["L0_ffn_wg"]),
        "w_gate0": np.ascontiguousarray(W["L0_ffn_w_gate"]),
        "w_down0": np.ascontiguousarray(W["L0_ffn_w_down"]),
        "Wq": np.ascontiguousarray(W["L1_attn_Wq"]),
        "Wk": np.ascontiguousarray(W["L1_attn_Wk"]),
        "Wv": np.ascontiguousarray(W["L1_attn_Wv"]),
        "Wo": np.ascontiguousarray(W["L1_attn_Wo"]),
        "bo": np.ascontiguousarray(
            np.broadcast_to(W["L1_attn_bo"].reshape(1, DM), (128, DM))),
        "gate_wT": np.ascontiguousarray(W["L1_ffn_gate_w"].T),
        "w1": np.ascontiguousarray(W["L1_ffn_w1"]),
        "w2": np.ascontiguousarray(W["L1_ffn_w2"]),
        "hsel": hsel, "bones16": bones16, "zsel": zsel,
        "eye": np.eye(128, dtype=np.float32),
        "zeros_bf": _bf(np.zeros((128, 128))),
        "triu_bf": _bf(np.triu(np.ones((128, 128)))),
        "ones1_bf": _bf(np.ones((1, 128))),
        "eye_bf": _bf(np.eye(128)),
        "dd_bf": _bf(np.tile(np.eye(64), (2, 2))),
        "ud_bf": _bf(np.block([[np.eye(64), np.eye(64)],
                               [np.zeros((64, 64)), np.eye(64)]])),
        "prefmask": pref, "onesmask": np.ones((ncores, 1), np.float32),
        "ecap0": np.ascontiguousarray((np.arange(E0) * LCAP0).astype(np.float32).reshape(E0, 1)),
        "ecap1": np.ascontiguousarray((np.arange(E1) * LCAP1).astype(np.float32).reshape(E1, 1)),
    }
    for w in LEVELS:
        m = np.zeros((128, 64), np.float32)
        for k in range(64 // w):
            m[2 * k * w: 2 * k * w + w, k * w:(k + 1) * w] = 1
        d[f"mbank{w}"] = _bf(np.tile(m, (1, 8)))
    d["mbankd"] = _bf(np.tile(np.eye(128, dtype=np.float32), (1, 4)))
    return d


class MK:
    def __init__(self, debug_taps=(), ncores=8):
        self.ncores = ncores
        self.nc = bass.Bass("TRN2", target_bir_lowering=False,
                            detect_race_conditions=False)
        self.debug_taps = set(debug_taps)
        self.tap_req = []
        self.es = contextlib.ExitStack()
        self.local_es = None
        self.counts, self.sems, self.t = {}, {}, {}
        self.pb = {}

    def sem(self, name):
        if name not in self.sems:
            self.sems[name] = self.es.enter_context(self.nc.semaphore(name))
            self.counts[name] = 0
        return self.sems[name]

    def inc(self, instr, name, by=1):
        instr.then_inc(self.sem(name), by)
        self.counts[name] += by
        return self.counts[name]

    def dma(self, eng, out, in_, qsem, **kw):
        ins = eng.dma_start(out=out, in_=in_, **kw)
        return self.inc(ins, qsem, 16)

    def idma(self, eng, out, in_, qsem, out_offset=None, in_offset=None):
        ins = eng.indirect_dma_start(out=out, out_offset=out_offset,
                                     in_=in_, in_offset=in_offset)
        return self.inc(ins, qsem, 16)

    def trdma(self, eng, out, in_, qsem):
        ins = eng.dma_start_transpose(out=out, in_=in_)
        return self.inc(ins, qsem, 16)

    def vgap(self, v):
        v.tensor_copy(out=self.t["c_gap"][:], in_=self.t["c_eye"][:, 0:64])

    def snap(self):
        return dict(self.counts)

    def wait(self, eng, snapshot, only=None):
        for name, v in snapshot.items():
            if v <= 0 or (only is not None and name not in only):
                continue
            eng.wait_ge(self.sems[name], v)

    def sb(self, name, shape, dtype):
        tt = self.es.enter_context(self.nc.sbuf_tensor(name, list(shape), dtype))
        self.t[name] = tt
        return tt

    def sbl(self, name, shape, dtype):
        self._uid = getattr(self, "_uid", 0) + 1
        tt = self.local_es.enter_context(
            self.nc.sbuf_tensor(f"{name}_{self._uid}", list(shape), dtype))
        self.t[name] = tt
        return tt

    def psl(self, name, shape=(128, 512)):
        self._uid = getattr(self, "_uid", 0) + 1
        tt = self.local_es.enter_context(
            self.nc.psum_tensor(f"{name}_{self._uid}", list(shape), F32))
        self.t[name] = tt
        return tt

    def dram(self, name, shape, dtype, kind=None):
        tt = (self.nc.dram_tensor(name, list(shape), dtype, kind=kind) if kind
              else self.nc.dram_tensor(name, list(shape), dtype))
        self.t[name] = tt.ap()
        return self.t[name]

    def tap(self, name, src, shape, dtype=F32):
        if name in self.debug_taps:
            out = self.dram("dbg_" + name, shape, dtype, kind="ExternalOutput")
            self.tap_req.append((name, out, src))

    def flush_taps(self):
        if not self.tap_req:
            return
        snapshot, block = self.snap(), self.block

        @block.gpsimd
        def _(g):
            self.wait(g, snapshot)
            for name, out, src in self.tap_req:
                self.dma(g, out, src, "sG")
            g.wait_ge(self.sems["sG"], self.counts["sG"])
        self.tap_req = []

    @contextlib.contextmanager
    def phase(self, name):
        snapshot, block = self.snap(), self.block
        for engn in ("sync", "vector", "scalar", "tensor", "gpsimd"):
            getattr(block, engn)(lambda eng, s=snapshot: self.wait(eng, s))
        self.local_es = contextlib.ExitStack()
        with self.local_es:
            yield
        self.local_es = None

    def build(self):
        with self.es:
            self.declare_io()
            self.block = self.es.enter_context(self.nc.Block())
            for s in ("sQ", "sA", "sG", "eV", "eS", "eT", "eG", "sCC"):
                self.sem(s)
            with self.phase("prep"):
                self.ph_prep()
            with self.phase("ln0"):
                self.ph_ln_park()
            with self.phase("mamba"):
                self.ph_mamba()
            with self.phase("rout0"):
                self.ph_routing(0)
            with self.phase("moe0"):
                self.ph_moe(0)
            with self.phase("comb0"):
                self.ph_combine(0)
            with self.phase("ln1"):
                self.ph_ln_park()
            with self.phase("attn"):
                self.ph_attn()
            with self.phase("rout1"):
                self.ph_routing(1)
            with self.phase("moe1"):
                self.ph_moe(1)
            with self.phase("comb1"):
                self.ph_combine(1)
            self.ph_final_wait()
            self.flush_taps()
        return self.nc

    def declare_io(self):
        dr = self.dram
        dr("x", (T_TOK, DM), F32, kind="ExternalInput")
        for n, sh in [
                ("W_inT", (DM, DPROJ)), ("W_outT", (DI, DM)), ("conv_w_r", (128, 64)),
                ("conv_b_r", (128, 16)), ("A_log_r", (128, 16)), ("D_r", (128, 16)),
                ("dt_bias_c", (32, 1)), ("wg0", (DM, E0)), ("w_gate0", (E0, DM, FF)),
                ("w_down0", (E0, FF, DM)), ("Wq", (DM, DM)), ("Wk", (DM, DM)),
                ("Wv", (DM, DM)), ("Wo", (DM, DM)), ("bo", (128, DM)),
                ("gate_wT", (DM, E1)), ("w1", (E1, DM, FF)), ("w2", (E1, FF, DM)),
                ("bones16", (128, 128)), ("eye", (128, 128)),
                ("prefmask", (self.ncores, 1)),
                ("onesmask", (self.ncores, 1)), ("ecap0", (E0, 1)),
                ("ecap1", (E1, 1))]:
            dr(n, sh, F32, kind="ExternalInput")
        for n, sh in [("hsel", (32, 2048)), ("zsel", (16, 1024))]:
            dr(n, sh, F32, kind="ExternalInput")
        for n, sh in [("zeros_bf", (128, 128)), ("triu_bf", (128, 128)),
                      ("ones1_bf", (1, 128)), ("eye_bf", (128, 128)),
                      ("dd_bf", (128, 128)), ("ud_bf", (128, 128))]:
            dr(n, sh, BF16, kind="ExternalInput")
        for w in LEVELS:
            dr(f"mbank{w}", (128, 512), BF16, kind="ExternalInput")
        dr("mbankd", (128, 512), BF16, kind="ExternalInput")
        dr("out", (T_TOK, DM), F32, kind="ExternalOutput")
        dr("x_dram", (T_TOK, DM), BF16)
        dr("xn_dram", (T_TOK, DM), BF16)
        dr("s_dram", (T_TOK, DM), BF16)
        dr("woutbf_dram", (DI, DM), BF16)
        dr("disp_dram", (SLOTS0 + TRASH, DM), BF16)
        dr("eo_dram", (SLOTS0 + TRASH, DM), BF16)
        dr("parkX_dram", (DI, HALF), BF16)
        dr("parkB_dram", (DI, HALF), BF16)
        dr("parkK_dram", (DM, TSEG), BF16)
        dr("cc_in", (1, 64), F32)
        dr("cc_out", (self.ncores, 64), F32)
        dr("row_dram", (4, 64), F32)
        self.sb("c_eye", (128, 128), F32)
        self.sb("c_eyebf", (128, 128), BF16)
        self.sb("c_zeros", (128, 128), BF16)
        self.sb("c_triu", (128, 128), BF16)
        self.sb("c_ones1", (1, 128), BF16)
        self.sb("c_hsel", (32, 2048), F32)
        self.sb("c_zsel", (16, 1024), F32)
        self.sb("c_bones16", (128, 128), F32)
        self.sb("c_convw", (128, 64), F32)
        self.sb("c_convb", (128, 16), F32)
        self.sb("c_negA", (128, 16), F32)
        self.sb("c_D", (128, 16), F32)
        self.sb("c_dtb", (32, 1), F32)
        self.sb("c_wg0", (128, 8 * E0), BF16)
        self.sb("c_gw1", (128, 8 * E1), BF16)
        self.sb("c_pref", (self.ncores, 1), F32)
        self.sb("c_ones8", (self.ncores, 1), F32)
        self.sb("c_bo", (128, DM), BF16)
        self.sb("c_ecap0", (E0, 1), F32)
        self.sb("c_ecap1", (E1, 1), F32)

        self.sb("c_iota", (128, 1), U32)
        self.sb("c_scr", (1, 2), U32)
        self.sb("c_gap", (128, 64), F32)
        self.sb("c_eps", (128, 1), F32)
        self.sb("c_eps6", (128, 1), F32)
        self.sb("c_trash0", (128, 1), U32)
        self.sb("c_trash1", (128, 1), U32)
        self.sb("c_trash0f", (128, 1), F32)
        self.sb("c_trash1f", (128, 1), F32)
        self.sb("lg_fm", (16, T_TOK), BF16)
        self.sb("f1_col", (128, NCHUNK), U32)
        self.sb("f2_col", (128, NCHUNK), U32)
        self.sb("g1_col", (128, NCHUNK), F32)
        self.sb("g2_col", (128, NCHUNK), F32)

    def ph_prep(self):
        t, block = self.t, self.block
        self.sbl("tmpA", (128, 16), F32)
        self.sbl("tmpZ", (128, DM), BF16)

        @block.sync
        def _(sync):
            for n, dst in [("eye", "c_eye"), ("eye_bf", "c_eyebf"),
                           ("conv_w_r", "c_convw"), ("conv_b_r", "c_convb"),
                           ("D_r", "c_D"),
                           ("A_log_r", "tmpA"), ("dt_bias_c", "c_dtb"),
                           ("zeros_bf", "c_zeros"),
                           ("triu_bf", "c_triu"), ("ones1_bf", "c_ones1"),
                           ("hsel", "c_hsel"), ("zsel", "c_zsel"),
                           ("bones16", "c_bones16"), ("prefmask", "c_pref"),
                           ("onesmask", "c_ones8"), ("ecap0", "c_ecap0"),
                           ("ecap1", "c_ecap1")]:
                self.dma(sync, t[dst][:], t[n], "sQ")


        @block.gpsimd
        def _(g):
            self.dma(g, t["x_dram"], t["x"], "sG")
            self.dma(g, t["woutbf_dram"], t["W_outT"], "sG")
            self.dma(g, t["c_bo"][:], t["bo"], "sG")
            for dtl in range(8):
                self.dma(g, t["c_wg0"][:, dtl * E0:(dtl + 1) * E0],
                         t["wg0"][dtl * 128:(dtl + 1) * 128, :], "sG")
                self.dma(g, t["c_gw1"][:, dtl * E1:(dtl + 1) * E1],
                         t["gate_wT"][dtl * 128:(dtl + 1) * 128, :], "sG")
            ins = g.iota(t["c_iota"][:], [[0, 1]], base=0, channel_multiplier=1)
            self.inc(ins, "eG")

        snap = self.snap()

        @block.scalar
        def _(sc):
            self.wait(sc, snap, only=("sQ",))
            ins = sc.activation(out=t["c_negA"][:], in_=t["tmpA"][:], func=ACT.Exp)
            self.inc(ins, "eS")

        @block.vector
        def _(v):
            v.wait_ge(self.sems["eS"], self.counts["eS"])
            v.wait_ge(self.sems["eG"], self.counts["eG"])
            v.tensor_scalar_mul(t["c_negA"][:], t["c_negA"][:], -1.0)
            v.memset(t["c_eps"][:], 1e-5)
            v.memset(t["c_eps6"][:], 1e-6)
            v.tensor_scalar_add(t["c_trash0"][:], t["c_iota"][:], SLOTS0)
            v.tensor_scalar_add(t["c_trash1"][:], t["c_iota"][:], SLOTS1)
            v.tensor_copy(out=t["c_trash0f"][:], in_=t["c_trash0"][:])
            v.tensor_copy(out=t["c_trash1f"][:], in_=t["c_trash1"][:])
            ins = v.memset(t["tmpZ"][:], 0.0)
            self.inc(ins, "eV")
        z_eV = self.counts["eV"]

        @block.scalar
        def _(sc):
            sc.wait_ge(self.sems["eV"], z_eV)
            self.dma(sc, t["eo_dram"][SLOTS0:SLOTS0 + TRASH, :], t["tmpZ"][:], "sA")
            for c in range((SLOTS0 + TRASH) // 128):
                self.dma(sc, t["disp_dram"][c * 128:(c + 1) * 128, :],
                         t["tmpZ"][:], "sA")

        @block.gpsimd
        def _(g):
            g.wait_ge(self.sems["eV"], z_eV)
            self.dma(g, t["cc_in"], t["tmpZ"][0:1, 0:64], "sG")

    def ph_ln_park(self):
        t, block = self.t, self.block
        self.sbl("ln_x", (128, 2 * DM), BF16)
        self.sbl("ln_xn", (128, 2 * DM), BF16)
        self.sbl("ln_sq", (128, DM), F32)
        self.sbl("ln_cs", (128, DM), F32)
        self.sbl("ln_cq", (128, DM), F32)
        self.sbl("ln_st", (128, 8), F32)
        ld, xn_v = [], {}
        sG0 = self.counts["sG"]
        st = None
        for c in range(NCHUNK):
            b = c % 2

            @block.sync
            def _(sync, c=c, b=b):
                if c >= 2:
                    sync.wait_ge(self.sems["eV"], xn_v[c - 2])
                ld.append(self.dma(sync, t["ln_x"][:, b * DM:(b + 1) * DM],
                                   t["x_dram"][c * 128:(c + 1) * 128, :], "sQ"))

            @block.vector
            def _(v, c=c, b=b):
                xc = t["ln_x"][:, b * DM:(b + 1) * DM]
                v.wait_ge(self.sems["sQ"], ld[c])
                v.tensor_tensor(out=t["ln_sq"][:], in0=xc, in1=xc, op=ALU.mult)
                v.tensor_tensor_scan(out=t["ln_cs"][:], data0=xc, data1=xc,
                                     initial=0.0, op0=ALU.add, op1=ALU.bypass)
                ins = v.tensor_tensor_scan(out=t["ln_cq"][:], data0=t["ln_sq"][:],
                                           data1=t["ln_sq"][:], initial=0.0,
                                           op0=ALU.add, op1=ALU.bypass)
                self.inc(ins, "eV")
            scan_v = self.counts["eV"]

            @block.scalar
            def _(sc, c=c, scan_v=scan_v):
                st = t["ln_st"]
                sc.wait_ge(self.sems["eV"], scan_v)
                sc.activation(out=st[:, 0:1], in_=t["ln_cs"][:, DM - 1:DM],
                              func=ACT.Identity, scale=1.0 / DM)
                ins = sc.activation(out=st[:, 1:2], in_=t["ln_cq"][:, DM - 1:DM],
                                    func=ACT.Identity, scale=1.0 / DM)
                self.inc(ins, "eS")
            ext_s = self.counts["eS"]

            @block.vector
            def _(v, c=c, ext_s=ext_s):
                st = t["ln_st"]
                v.wait_ge(self.sems["eS"], ext_s)
                ins = v.tensor_scalar(out=st[:, 2:3], in0=st[:, 0:1],
                                      scalar1=st[:, 0:1], scalar2=st[:, 1:2],
                                      op0=ALU.mult, op1=ALU.subtract)
                self.inc(ins, "eV")
            var_v = self.counts["eV"]

            @block.scalar
            def _(sc, c=c, var_v=var_v):
                st = t["ln_st"]
                sc.wait_ge(self.sems["eV"], var_v)
                ins = sc.activation(out=st[:, 5:6], in_=st[:, 2:3], func=ACT.Sqrt,
                                    scale=-1.0, bias=t["c_eps"][:, 0:1])
                self.inc(ins, "eS")
            sig_s = self.counts["eS"]

            @block.vector
            def _(v, c=c, sig_s=sig_s):
                st = t["ln_st"]
                v.wait_ge(self.sems["eS"], sig_s)
                ins = v.reciprocal(st[:, 6:7], st[:, 5:6])
                self.inc(ins, "eV")
            rec_v = self.counts["eV"]

            @block.scalar
            def _(sc, c=c, rec_v=rec_v):
                st = t["ln_st"]
                sc.wait_ge(self.sems["eV"], rec_v)
                ins = sc.activation(out=st[:, 7:8], in_=st[:, 6:7],
                                    func=ACT.Identity)
                self.inc(ins, "eS")
            rcp_s = self.counts["eS"]

            @block.vector
            def _(v, c=c, b=b, rcp_s=rcp_s):
                st = t["ln_st"]
                xc = t["ln_x"][:, b * DM:(b + 1) * DM]
                xnc = t["ln_xn"][:, b * DM:(b + 1) * DM]
                v.wait_ge(self.sems["eS"], rcp_s)
                if c >= 2:
                    v.wait_ge(self.sems["sG"], sG0 + 16 * (c - 1))
                ins = v.tensor_scalar(out=xnc, in0=xc, scalar1=st[:, 0:1],
                                      scalar2=st[:, 7:8], op0=ALU.subtract,
                                      op1=ALU.mult)
                xn_v[c] = self.inc(ins, "eV")

            @block.gpsimd
            def _(g, c=c, b=b):
                g.wait_ge(self.sems["eV"], xn_v[c])
                self.dma(g, t["xn_dram"][c * 128:(c + 1) * 128, :],
                         t["ln_xn"][:, b * DM:(b + 1) * DM], "sG")

    def ph_final_wait(self):
        block = self.block
        snap = self.snap()

        @block.sync
        def _(sync):
            self.wait(sync, snap)

    def ph_mamba(self):
        t, block = self.t, self.block
        self.sbl("m_xnf", (128, 2048), BF16)
        self.sbl("m_win", (128, 4096), BF16)
        self.sbl("m_xi", (128, 16 * 260), BF16)
        self.sbl("m_zs", (128, 4096), BF16)
        self.sbl("m_B", (128, 4096), BF16)
        self.sbl("m_C", (128, 4096), BF16)
        self.sbl("m_dtp", (32, 256), F32)
        self.sbl("m_dtt", (32, 256), F32)
        self.sbl("m_dts", (32, 256), F32)
        self.sbl("m_convt", (128, 256), F32)
        self.sbl("m_Xc", (128, 4096), BF16)
        self.sbl("m_S", (128, 4096), F32)
        self.sbl("m_SB", (128, 4096), F32)
        self.sbl("m_Bt", (128, 4096), BF16)
        self.sbl("m_ccC", (128, 2048), BF16)
        self.sbl("m_c128", (128, 4096), BF16)
        self.sbl("m_b128", (128, 4096), BF16)
        self.sbl("m_dcol", (128, 32), F32)
        self.sbl("m_sc", (128, 8192), BF16)
        self.sbl("m_Xtm", (128, 2048), BF16)
        self.sbl("m_Btm", (128, 2048), BF16)
        self.sbl("m_yg", (128, 4096), BF16)
        self.sbl("m_h", (128, 1024), F32)
        self.sbl("m_hbf", (128, 1024), BF16)
        self.sbl("m_stm", (128, 2048), BF16)
        self.sbl("m_wo2", (128, 4096), BF16)
        self.sbl("m_tmp", (128, 512), BF16)
        for w in list(LEVELS) + ["d"]:
            self.sbl(f"m_mb{w}", (128, 512), BF16)
        pb = [self.psl(f"m_pb{i}") for i in range(8)]
        self.tail_v = {}

        @block.sync
        def _(sync):
            for w in list(LEVELS) + ["d"]:
                self.dma(sync, t[f"m_mb{w}"][:], t[f"mbank{w}"], "sQ")

        @block.vector
        def _(v):
            for tl in range(16):
                v.memset(t["m_xi"][:, tl * 260: tl * 260 + 4], 0.0)
            v.memset(t["m_h"][:], 0.0)
            ins = v.memset(t["m_hbf"][:], 0.0)
            self.inc(ins, "eV")

        for seg in range(NSEG):
            self._mamba_segment(seg, pb)

    def _mamba_segment(self, seg, pb):
        t, block = self.t, self.block
        t0 = seg * TSEG
        FG = [(fg * 512, 512) for fg in range(16)] + [(8192, 32)]
        NFG = len(FG)

        seg_snap = {k: self.counts[k] for k in ("eV", "eS", "eT")}
        ld_xnf = [None]
        ld_win = {}
        mm_done = {}
        evac_at = {}

        def emit_win_load(fg):
            flo, width = FG[fg]

            @block.gpsimd
            def _(g):
                if fg < 1:
                    self.wait(g, seg_snap)
                if fg >= 1:
                    g.wait_ge(self.sems["eT"], mm_done[("fg", fg - 1)])
                for dtl in range(8):
                    ld_win[(fg, dtl)] = self.dma(
                        g, t["m_win"][:, dtl * 512: dtl * 512 + width],
                        t["W_inT"][dtl * 128:(dtl + 1) * 128, flo:flo + width], "sG")

        @block.sync
        def _(sync):
            self.wait(sync, seg_snap)
            for dtl in range(8):
                ld_xnf.append(self.trdma(
                    sync, t["m_xnf"][:, dtl * 256:(dtl + 1) * 256],
                    t["xn_dram"][t0:t0 + TSEG, dtl * 128:(dtl + 1) * 128], "sQ"))

        emit_win_load(0)
        fcg = 0
        for fg in range(NFG):
            flo, width = FG[fg]
            par = fg % 2
            nsub = max(1, width // 128)

            @block.tensor
            def _(tn, fg=fg, width=width, nsub=nsub, fcg0=fcg):
                if fg == 0:
                    self.wait(tn, seg_snap)
                tn.wait_ge(self.sems["sQ"], ld_xnf[-1])
                tn.wait_ge(self.sems["sG"], ld_win[(fg, 7)])
                for sub in range(nsub):
                    wq = min(width, 128)
                    bank = pb[(fcg0 + sub) % 4]
                    if fcg0 + sub >= 4:
                        sem, cnt = evac_at[fcg0 + sub - 4]
                        tn.wait_ge(self.sems[sem], cnt)
                    for dtl in range(8):
                        ins = tn.matmul(
                            bank[0:wq, 0:TSEG],
                            t["m_win"][:, dtl * 512 + sub * 128:
                                       dtl * 512 + sub * 128 + wq],
                            t["m_xnf"][:, dtl * 256:(dtl + 1) * 256],
                            start=(dtl == 0), stop=(dtl == 7))
                    mm_done[("fc", fcg0 + sub)] = self.inc(ins, "eT")
                mm_done[("fg", fg)] = self.counts["eT"]

            for sub in range(nsub):
                myfcg = fcg + sub
                bank = pb[myfcg % 4]
                if 16 <= myfcg < 32:
                    @block.scalar
                    def _(sc, myfcg=myfcg, bank=bank):
                        tl = myfcg - 16
                        sc.wait_ge(self.sems["eT"], mm_done[("fc", myfcg)])
                        ins = sc.activation(
                            out=t["m_zs"][:, tl * 256:(tl + 1) * 256],
                            in_=bank[:, 0:TSEG], func=ACT.Silu)
                        evac_at[myfcg] = ("eS", self.inc(ins, "eS"))
                else:
                    @block.vector
                    def _(v, myfcg=myfcg, bank=bank):
                        v.wait_ge(self.sems["eT"], mm_done[("fc", myfcg)])
                        if myfcg < 16:
                            dst = t["m_xi"][:, myfcg * 260 + 4: myfcg * 260 + 260]
                            ins = v.tensor_copy(out=dst, in_=bank[:, 0:TSEG])
                        elif myfcg < 48:
                            tl = myfcg - 32
                            ins = v.tensor_copy(
                                out=t["m_B"][:, tl * 256:(tl + 1) * 256],
                                in_=bank[:, 0:TSEG])
                        elif myfcg < 64:
                            tl = myfcg - 48
                            ins = v.tensor_copy(
                                out=t["m_C"][:, tl * 256:(tl + 1) * 256],
                                in_=bank[:, 0:TSEG])
                        else:
                            ins = v.tensor_copy(out=t["m_dtp"][:],
                                                in_=bank[0:32, 0:TSEG])
                        evac_at[myfcg] = ("eV", self.inc(ins, "eV"))
            fcg += nsub
            if fg + 1 < NFG:
                emit_win_load(fg + 1)

        @block.tensor
        def _(tn):
            for dtl in range(8):
                ins = tn.matmul(pb[4][0:E0, 0:TSEG],
                                t["c_wg0"][:, dtl * E0:(dtl + 1) * E0],
                                t["m_xnf"][:, dtl * 256:(dtl + 1) * 256],
                                start=(dtl == 0), stop=(dtl == 7))
            mm_done[("lg",)] = self.inc(ins, "eT")
            mm_done[("mseg", seg)] = self.counts["eT"]

        @block.vector
        def _(v):
            v.wait_ge(self.sems["eT"], mm_done[("lg",)])
            ins = v.tensor_copy(out=t["lg_fm"][0:E0, t0:t0 + TSEG],
                                in_=pb[4][0:E0, 0:TSEG])
            self.inc(ins, "eV")

        @block.scalar
        def _(sc):
            sem, cnt = evac_at[64]
            sc.wait_ge(self.sems[sem], cnt)
            sc.activation(out=t["m_dtt"][:], in_=t["m_dtp"][:], func=ACT.Abs,
                          bias=t["c_dtb"][:, 0:1])
            sc.activation(out=t["m_dtt"][:], in_=t["m_dtt"][:], func=ACT.Exp,
                          scale=-1.0)
            ins = sc.activation(out=t["m_dtt"][:], in_=t["m_dtt"][:], func=ACT.Ln,
                                bias=1.0)
            self.inc(ins, "eS")
        dt_eS = self.counts["eS"]

        @block.vector
        def _(v):
            v.wait_ge(self.sems["eS"], dt_eS)
            v.tensor_scalar(out=t["m_dts"][:], in0=t["m_dtp"][:],
                            scalar1=t["c_dtb"][:, 0:1], scalar2=0.0,
                            op0=ALU.add, op1=ALU.max)
            ins = v.tensor_add(t["m_dts"][:], t["m_dts"][:], t["m_dtt"][:])
            self.inc(ins, "eV")
        dt_eV = self.counts["eV"]

        conv_v = []
        for tl in range(16):
            @block.vector
            def _(v, tl=tl):
                if tl == 0:
                    sem, cnt = evac_at[15]
                    v.wait_ge(self.sems[sem], cnt)
                base = tl * 260

                def sl(k):
                    return t["m_xi"][:, base + 1 + k: base + 1 + k + TSEG]
                dst = t["m_convt"][:]
                v.tensor_scalar(out=dst, in0=sl(0),
                                scalar1=t["c_convw"][:, 4 * tl: 4 * tl + 1],
                                scalar2=None, op0=ALU.mult)
                for k in (1, 2, 3):
                    v.scalar_tensor_tensor(
                        out=dst, in0=sl(k),
                        scalar=t["c_convw"][:, 4 * tl + k: 4 * tl + k + 1],
                        in1=dst, op0=ALU.mult, op1=ALU.add)
                ins = v.tensor_copy(out=t["m_xi"][:, base + 1: base + 4],
                                    in_=t["m_xi"][:, base + 257: base + 260])
                conv_v.append(self.inc(ins, "eV"))

            @block.scalar
            def _(sc, tl=tl):
                sc.wait_ge(self.sems["eV"], conv_v[tl])
                ins = sc.activation(out=t["m_Xc"][:, tl * 256:(tl + 1) * 256],
                                    in_=t["m_convt"][:], func=ACT.Silu,
                                    bias=t["c_convb"][:, tl:tl + 1])
                self.inc(ins, "eS")
            cv_cnt = self.counts["eS"]

            @block.vector
            def _(v, cv_cnt=cv_cnt):
                v.wait_ge(self.sems["eS"], cv_cnt)
        xc_eS = self.counts["eS"]

        dtrep_v = []
        for tl in range(16):
            @block.tensor
            def _(tn, tl=tl):
                bank = pb[5 + (tl % 2)]
                if tl == 0:
                    tn.wait_ge(self.sems["eV"], dt_eV)
                if tl >= 2:
                    tn.wait_ge(self.sems["eV"], dtrep_v[tl - 2])
                ins = tn.matmul(bank[:, 0:TSEG],
                                t["c_hsel"][:, tl * 128:(tl + 1) * 128],
                                t["m_dts"][:], start=True, stop=True)
                self.inc(ins, "eT")
            mmc = self.counts["eT"]

            @block.vector
            def _(v, tl=tl, mmc=mmc):
                bank = pb[5 + (tl % 2)]
                v.wait_ge(self.sems["eT"], mmc)
                v.tensor_scalar(out=t["m_S"][:, tl * 256:(tl + 1) * 256],
                                in0=bank[:, 0:TSEG],
                                scalar1=t["c_negA"][:, tl:tl + 1],
                                scalar2=None, op0=ALU.mult)
                ins = v.tensor_tensor(out=t["m_B"][:, tl * 256:(tl + 1) * 256],
                                      in0=t["m_B"][:, tl * 256:(tl + 1) * 256],
                                      in1=bank[:, 0:TSEG], op=ALU.mult)
                dtrep_v.append(self.inc(ins, "eV"))

        @block.vector
        def _(v):
            v.wait_ge(self.sems["eT"], self.counts["eT"])
            v.memset(t["m_SB"][:], 0.0)
            v.memset(t["m_sc"][:], 0.0)
            ins = v.tensor_copy(out=t["m_Bt"][:], in_=t["m_B"][:])
            self.inc(ins, "eV")
        chain_eV = self.counts["eV"]

        p0_mm = {}
        for c in range(2):
            for tl in range(16):
                @block.tensor
                def _(tn, c=c, tl=tl):
                    if c == 0 and tl == 0:
                        tn.wait_ge(self.sems["eV"], chain_eV)
                        tn.wait_ge(self.sems["eS"], xc_eS)
                    if c == 1 and tl == 0:
                        tn.wait_ge(self.sems["eV"], self.counts["eV"])
                    for a in range(2):
                        q = tl * 2 + a
                        bank = pb[q % 8]
                        slot = q // 8
                        ins = tn.matmul(
                            bank[:, slot * 128:(slot + 1) * 128],
                            t["m_B"][64 * a:64 * a + 64,
                                     tl * 256 + c * 128: tl * 256 + c * 128 + 128],
                            t["m_C"][64 * a:64 * a + 64,
                                     tl * 256 + c * 128: tl * 256 + c * 128 + 128],
                            start=True, stop=True)
                        p0_mm[(c, q)] = self.inc(ins, "eT")
            for j in range(8):
                @block.vector
                def _(v, c=c, j=j):
                    v.wait_ge(self.sems["eT"], p0_mm[(c, 8 * 3 + j)])
                    bank = pb[j]
                    v.tensor_tensor(out=t["m_tmp"][:], in0=bank[:],
                                    in1=t["m_mbd"][:], op=ALU.mult)
                    dstv = t["m_sc"][:].rearrange("p (s j t) -> p s j t", j=8, t=256)
                    dstv = dstv[:, :, j, c * 128:(c + 1) * 128]
                    ins = v.tensor_tensor(out=dstv, in0=dstv, in1=t["m_tmp"][:],
                                          op=ALU.add)
                    self.inc(ins, "eV")
        p0_eV = self.counts["eV"]

        self._mamba_levels(seg, pb, p0_eV)

        self._mamba_tail(seg, pb)


    def _mamba_levels(self, seg, pb, p0_eV):
        t, block = self.t, self.block

        def parity_view(buf, w, par):
            u = 4096 // (2 * w)
            return buf[:].rearrange("p (u two w) -> p u two w", two=2, w=w)[:, :, par, :]

        def endcol_view(w, par):
            u = 4096 // (2 * w)
            v = t["m_S"][:].rearrange("p (u two w) -> p u two w", two=2, w=w)
            return v[:, :, par, w - 1:w].broadcast_to((128, u, w))

        prev_sel_eV = p0_eV
        for li, w in enumerate(LEVELS):
            u = 4096 // (2 * w)
            @block.scalar
            def _(sc, w=w, u=u):
                sc.wait_ge(self.sems["eV"], self.counts["eV"])
                sc.wait_ge(self.sems["eT"], self.counts["eT"])
                ins = sc.activation(
                    out=t["m_ccC"][:].rearrange("p (u w) -> p u w", w=w),
                    in_=parity_view(t["m_S"], w, 1), func=ACT.Exp)
                self.inc(ins, "eS")
            ccC_eS = self.counts["eS"]

            @block.vector
            def _(v, w=w, u=u, ccC_eS=ccC_eS):
                v.wait_ge(self.sems["eS"], ccC_eS)
                ins = v.tensor_tensor(
                    out=t["m_ccC"][:].rearrange("p (u w) -> p u w", w=w),
                    in0=parity_view(t["m_C"], w, 1),
                    in1=t["m_ccC"][:].rearrange("p (u w) -> p u w", w=w),
                    op=ALU.mult)
                self.inc(ins, "eV")

            if w > 1:
                wv = w // 2
                @block.scalar
                def _(sc, wv=wv):
                    sc.wait_ge(self.sems["eV"], self.counts["eV"])
                    sc.wait_ge(self.sems["eT"], self.counts["eT"])
                    ins = sc.activation(
                        out=parity_view(t["m_Bt"], wv, 0),
                        in_=parity_view(t["m_SB"], wv, 0), func=ACT.Exp)
                    self.inc(ins, "eS")
                ebt_eS = self.counts["eS"]

                @block.vector
                def _(v, wv=wv, ebt_eS=ebt_eS):
                    v.wait_ge(self.sems["eS"], ebt_eS)
                    ins = v.tensor_tensor(
                        out=parity_view(t["m_Bt"], wv, 0),
                        in0=parity_view(t["m_B"], wv, 0),
                        in1=parity_view(t["m_Bt"], wv, 0),
                        op=ALU.mult)
                    self.inc(ins, "eV")
            bt_eV = self.counts["eV"]

            @block.vector
            def _(v, w=w, bt_eV=bt_eV):
                v.wait_ge(self.sems["eV"], bt_eV)
                v.tensor_tensor(out=parity_view(t["m_SB"], w, 0),
                                in0=parity_view(t["m_SB"], w, 0),
                                in1=endcol_view(w, 1), op=ALU.add)
                ins = v.tensor_tensor(out=parity_view(t["m_S"], w, 1),
                                      in0=parity_view(t["m_S"], w, 1),
                                      in1=endcol_view(w, 0), op=ALU.add)
                self.inc(ins, "eV")

            for c in range(2):
                lvl_mm = {}
                for tl in range(16):
                    @block.tensor
                    def _(tn, c=c, tl=tl, w=w, li=li):
                        if tl == 0:
                            tn.wait_ge(self.sems["eV"], prev_sel_eV)
                            tn.wait_ge(self.sems["eV"], self.counts["eV"])
                        for a in range(2):
                            q = tl * 2 + a
                            bank = pb[4 * c + q % 4]
                            r = q // 4
                            ins = tn.matmul(
                                bank[:, r * 64:(r + 1) * 64],
                                t["m_Bt"][64 * a:64 * a + 64,
                                          tl * 256 + c * 128: tl * 256 + c * 128 + 128],
                                t["m_ccC"][64 * a:64 * a + 64,
                                           tl * 128 + c * 64: tl * 128 + c * 64 + 64],
                                start=True, stop=True)
                            lvl_mm[(c, q)] = self.inc(ins, "eT")
                for j in range(4):
                    @block.vector
                    def _(v, c=c, j=j, w=w):
                        v.wait_ge(self.sems["eT"], lvl_mm[(c, 28 + j)])
                        bank = pb[4 * c + j]
                        v.tensor_tensor(out=t["m_tmp"][:], in0=bank[:],
                                        in1=t[f"m_mb{w}"][:], op=ALU.mult)
                        dstv = t["m_sc"][:].rearrange("p (s j t) -> p s j t",
                                                      j=4, t=256)
                        dstv = dstv[:, :, j, c * 128:(c + 1) * 128]
                        dstv = dstv.rearrange("p s (nb two w) -> p s nb two w",
                                              two=2, w=w)[:, :, :, 1, :]
                        ins = v.tensor_tensor(out=dstv, in0=dstv,
                                              in1=t["m_tmp"][:], op=ALU.add)
                        self.inc(ins, "eV")
            prev_sel_eV = self.counts["eV"]

    def _mamba_tail(self, seg, pb):
        t, block = self.t, self.block
        t0 = seg * TSEG

        @block.scalar
        def _(sc):
            sc.wait_ge(self.sems["eV"], self.counts["eV"])
            sc.wait_ge(self.sems["eT"], self.counts["eT"])
            ins = sc.activation(out=t["m_c128"][:], in_=t["m_S"][:], func=ACT.Exp)
            self.inc(ins, "eS")
        c128_eS = self.counts["eS"]

        @block.vector
        def _(v):
            v.wait_ge(self.sems["eS"], c128_eS)
            for tl in range(16):
                for c in range(2):
                    v.tensor_copy(
                        out=t["m_dcol"][:, tl * 2 + c: tl * 2 + c + 1],
                        in_=t["m_c128"][:, tl * 256 + c * 128 + 127:
                                        tl * 256 + c * 128 + 128])
            ins = v.tensor_tensor(out=t["m_c128"][:], in0=t["m_C"][:],
                                  in1=t["m_c128"][:], op=ALU.mult)
            self.inc(ins, "eV")

        @block.scalar
        def _(sc):
            sc.wait_ge(self.sems["eV"], self.counts["eV"])
            sc.wait_ge(self.sems["eT"], self.counts["eT"])
            ins = sc.activation(out=t["m_b128"][:], in_=t["m_SB"][:], func=ACT.Exp)
            self.inc(ins, "eS")
        b128_eS = self.counts["eS"]

        @block.vector
        def _(v):
            v.wait_ge(self.sems["eS"], b128_eS)
            ins = v.tensor_tensor(out=t["m_b128"][:], in0=t["m_B"][:],
                                  in1=t["m_b128"][:], op=ALU.mult)
            self.inc(ins, "eV")
        b128_eV = self.counts["eV"]

        @block.gpsimd
        def _(g):
            g.wait_ge(self.sems["eV"], b128_eV)
            self.dma(g, t["parkX_dram"].rearrange("(tl p) t -> p tl t", p=128),
                     t["m_Xc"][:].rearrange("p (tl t) -> p tl t", t=256), "sG")
            self.dma(g, t["parkB_dram"].rearrange("(tl p) t -> p tl t", p=128),
                     t["m_b128"][:].rearrange("p (tl t) -> p tl t", t=256), "sG")
        park_sG = self.counts["sG"]

        for c in range(2):
            @block.sync
            def _(sync, c=c):
                sync.wait_ge(self.sems["sG"], park_sG)
                if c == 1:
                    sync.wait_ge(self.sems["eT"], self.counts["eT"])
                self.trdma(sync, t["m_Xtm"][:],
                           t["parkX_dram"][:, c * 128:(c + 1) * 128], "sQ")
                self.trdma(sync, t["m_Btm"][:],
                           t["parkB_dram"][:, c * 128:(c + 1) * 128], "sQ")
            tm_sQ = self.counts["sQ"]

            for tl in range(16):
                @block.tensor
                def _(tn, c=c, tl=tl, tm_sQ=tm_sQ):
                    pbF, pbY, pbS = pb[(tl % 2) * 3], pb[(tl % 2) * 3 + 1], pb[(tl % 2) * 3 + 2]
                    if tl == 0:
                        tn.wait_ge(self.sems["sQ"], tm_sQ)
                        tn.wait_ge(self.sems["eV"], self.counts["eV"])
                    if tl >= 2:
                        tn.wait_ge(self.sems["eV"], self.tail_v[(c, tl - 2)])
                    for a in range(2):
                        tn.matmul(pbF[64 * a:64 * a + 64, 0:128],
                                  t["m_hbf"][64 * a:64 * a + 64, tl * 64:(tl + 1) * 64],
                                  t["m_c128"][64 * a:64 * a + 64,
                                              tl * 256 + c * 128: tl * 256 + c * 128 + 128],
                                  start=True, stop=True)
                        tn.matmul(pbY[64 * a:64 * a + 64, 0:128],
                                  t["m_Xtm"][:, tl * 128 + 64 * a: tl * 128 + 64 * a + 64],
                                  t["m_sc"][:, (tl * 2 + a) * 256 + c * 128:
                                            (tl * 2 + a) * 256 + c * 128 + 128],
                                  start=True, stop=True)
                        ins = tn.matmul(pbS[64 * a:64 * a + 64, 0:64],
                                        t["m_Btm"][:, tl * 128 + 64 * a: tl * 128 + 64 * a + 64],
                                        t["m_Xtm"][:, tl * 128 + 64 * a: tl * 128 + 64 * a + 64],
                                        start=True, stop=True)
                    self.inc(ins, "eT")
                mm_cnt = self.counts["eT"]

                @block.vector
                def _(v, c=c, tl=tl, mm_cnt=mm_cnt):
                    pbF, pbY, pbS = pb[(tl % 2) * 3], pb[(tl % 2) * 3 + 1], pb[(tl % 2) * 3 + 2]
                    v.wait_ge(self.sems["eT"], mm_cnt)
                    ygs = t["m_yg"][:, tl * 256 + c * 128: tl * 256 + c * 128 + 128]
                    v.scalar_tensor_tensor(out=ygs,
                                           in0=t["m_Xc"][:, tl * 256 + c * 128:
                                                         tl * 256 + c * 128 + 128],
                                           scalar=t["c_D"][:, tl:tl + 1],
                                           in1=pbF[:, 0:128],
                                           op0=ALU.mult, op1=ALU.add)
                    v.tensor_tensor(out=ygs, in0=ygs, in1=pbY[:, 0:128], op=ALU.add)
                    v.tensor_tensor(out=ygs, in0=ygs,
                                    in1=t["m_zs"][:, tl * 256 + c * 128:
                                                  tl * 256 + c * 128 + 128],
                                    op=ALU.mult)
                    v.scalar_tensor_tensor(out=t["m_h"][:, tl * 64:(tl + 1) * 64],
                                           in0=t["m_h"][:, tl * 64:(tl + 1) * 64],
                                           scalar=t["m_dcol"][:, tl * 2 + c: tl * 2 + c + 1],
                                           in1=pbS[:, 0:64],
                                           op0=ALU.mult, op1=ALU.add)
                    ins = v.tensor_copy(out=t["m_hbf"][:, tl * 64:(tl + 1) * 64],
                                        in_=t["m_h"][:, tl * 64:(tl + 1) * 64])
                    self.tail_v[(c, tl)] = self.inc(ins, "eV")

        for dq in range(4):
            @block.gpsimd
            def _(g, dq=dq):
                g.wait_ge(self.sems["eT"], self.counts["eT"])
                for ft in range(16):
                    self.dma(g, t["m_wo2"][:, ft * 256:(ft + 1) * 256],
                             t["woutbf_dram"][ft * 128:(ft + 1) * 128,
                                              dq * 256:(dq + 1) * 256], "sG")
            wo_sG = self.counts["sG"]
            for c in range(2):
                @block.tensor
                def _(tn, dq=dq, c=c, wo_sG=wo_sG):
                    tn.wait_ge(self.sems["sG"], wo_sG)
                    if dq == 0 and c == 0:
                        tn.wait_ge(self.sems["eV"], self.tail_v[(1, 15)])
                    if (dq, c) != (0, 0):
                        tn.wait_ge(self.sems["eV"], self.counts["eV"])
                    bank = pb[6 + c]
                    for tl in range(16):
                        ins = tn.matmul(bank[:, 0:256],
                                        t["m_yg"][:, tl * 256 + c * 128:
                                                  tl * 256 + c * 128 + 128],
                                        t["m_wo2"][:, tl * 256:(tl + 1) * 256],
                                        start=(tl == 0), stop=(tl == 15))
                    self.inc(ins, "eT")
                op_eT = self.counts["eT"]

                @block.vector
                def _(v, dq=dq, c=c, op_eT=op_eT):
                    v.wait_ge(self.sems["eT"], op_eT)
                    ins = v.tensor_copy(
                        out=t["m_stm"][:, c * 1024 + dq * 256:
                                       c * 1024 + dq * 256 + 256],
                        in_=pb[6 + c][:, 0:256])
                    self.inc(ins, "eV")

        @block.scalar
        def _(sc):
            sc.wait_ge(self.sems["eV"], self.counts["eV"])
            for c in range(2):
                self.dma(sc, t["s_dram"][t0 + c * 128: t0 + (c + 1) * 128, :],
                         t["m_stm"][:, c * 1024:(c + 1) * 1024], "sA")

    def ph_attn(self, upto="full", nseg=NSEG, plainfeat=False, skip=()):
        t, block = self.t, self.block
        for n in ("aw_q", "aw_k", "aw_v", "aw_o"):
            self.sbl(n, (128, 8 * DM), BF16)
        self.sbl("a_dd", (128, 128), BF16)
        self.sbl("a_ud", (128, 128), BF16)
        self.sbl("a_xn", (128, 8 * TSEG), BF16)
        self.sbl("a_q", (128, 8 * TSEG), BF16)
        self.sbl("a_k", (128, 8 * TSEG), BF16)
        self.sbl("a_t1", (128, TSEG), F32)
        self.sbl("a_t2", (128, TSEG), F32)
        self.sbl("a_t3", (128, TSEG), F32)
        self.sbl("a_z1", (128, TSEG), F32)
        self.sbl("a_z2", (128, TSEG), F32)
        self.sbl("a_cum", (128, 8 * TSEG), F32)
        self.sbl("a_zsb", (16, TSEG), F32)
        self.sbl("a_zrep", (128, 8 * TSEG), F32)
        self.sbl("a_ktm", (128, 2 * DM), BF16)
        self.sbl("a_vtm", (128, 2 * DM), BF16)
        self.sbl("a_sc", (128, 16 * 128), BF16)
        self.sbl("a_y", (128, 2 * DM), BF16)
        self.sbl("a_S", (128, 8 * 64), F32)
        self.sbl("a_Sbf", (128, 8 * 128), BF16)
        self.sbl("a_s", (128, DM), BF16)
        pb = [self.psl(f"a_pb{i}") for i in range(8)]
        wmap = {"aw_q": "Wq", "aw_k": "Wk", "aw_v": "Wv", "aw_o": "Wo"}

        @block.gpsimd
        def _(g):
            for dst, src in wmap.items():
                for kt in range(8):
                    self.dma(g, t[dst][:, kt * DM:(kt + 1) * DM],
                             t[src][kt * 128:(kt + 1) * 128, :], "sG")

        @block.sync
        def _(sync):
            self.dma(sync, t["a_dd"][:], t["dd_bf"], "sQ")
            self.dma(sync, t["a_ud"][:], t["ud_bf"], "sQ")

        @block.vector
        def _(v):
            v.memset(t["a_S"][:], 0.0)
            ins = v.memset(t["a_Sbf"][:], 0.0)
            self.inc(ins, "eV")

        sbf = {tt: 0 for tt in range(8)}
        for seg in range(nseg):
            t0 = seg * TSEG
            @block.sync
            def _(sync, seg=seg):
                sync.wait_ge(self.sems["eT"], self.counts["eT"])
                for dtl in range(8):
                    self.trdma(
                        sync, t["a_xn"][:, dtl * TSEG:(dtl + 1) * TSEG],
                        t["xn_dram"][t0:t0 + TSEG, dtl * 128:(dtl + 1) * 128], "sQ")
            xn_sQ = self.counts["sQ"]

            fz = {}
            for st, (wsb, dst, b0) in enumerate(
                    [("aw_q", "a_q", 0), ("aw_k", "a_k", 2)]):
                for m in range(8):
                    @block.tensor
                    def _(tn, st=st, m=m, wsb=wsb, b0=b0, seg=seg, xn_sQ=xn_sQ):
                        bank = pb[b0 + m % 2]
                        if st == 0 and m == 0:
                            tn.wait_ge(self.sems["sQ"], xn_sQ)
                            tn.wait_ge(self.sems["eV"], self.counts["eV"])
                            if seg == 0:
                                tn.wait_ge(self.sems["sG"], self.counts["sG"])
                        if (st, m) >= (0, 2):
                            key = (st, m - 2) if m >= 2 else (st - 1, 6 + m)
                            tn.wait_ge(self.sems["eV"], fz[key][2])
                        for kt in range(8):
                            ins = tn.matmul(
                                bank[:, 0:TSEG],
                                t[wsb][:, kt * DM + m * 128: kt * DM + m * 128 + 128],
                                t["a_xn"][:, kt * TSEG:(kt + 1) * TSEG],
                                start=(kt == 0), stop=(kt == 7))
                        self.inc(ins, "eT")
                    mmc = self.counts["eT"]

                    @block.vector
                    def _(v, st=st, m=m, b0=b0, mmc=mmc):
                        bank = pb[b0 + m % 2]
                        v.wait_ge(self.sems["eT"], mmc)
                        if (st, m) > (0, 0):
                            pk = (st, m - 1) if m >= 1 else (st - 1, 7)
                            v.wait_ge(self.sems["eS"], fz[pk][1])
                        v.tensor_single_scalar(out=t["a_t1"][:], in_=bank[:, 0:TSEG],
                                               scalar=0.0, op=ALU.min)
                        ins = v.tensor_single_scalar(out=t["a_t3"][:], in_=bank[:, 0:TSEG],
                                                     scalar=0.0, op=ALU.max)
                        c1 = self.inc(ins, "eV")
                        fz.setdefault((st, m), [c1, 0, 0])[0] = c1

                    @block.scalar
                    def _(sc, st=st, m=m):
                        sc.wait_ge(self.sems["eV"], fz[(st, m)][0])
                        ins = sc.activation(out=t["a_t2"][:], in_=t["a_t1"][:],
                                            func=ACT.Exp)
                        fz[(st, m)][1] = self.inc(ins, "eS")

                    @block.vector
                    def _(v, st=st, m=m, b0=b0, dst=dst, plainfeat=plainfeat):
                        bank = pb[b0 + m % 2]
                        v.wait_ge(self.sems["eS"], fz[(st, m)][1])
                        if plainfeat:
                            ins = v.tensor_copy(
                                out=t[dst][:, m * TSEG:(m + 1) * TSEG],
                                in_=bank[:, 0:TSEG])
                        else:
                            ins = v.tensor_tensor(
                                out=t[dst][:, m * TSEG:(m + 1) * TSEG],
                                in0=t["a_t3"][:], in1=t["a_t2"][:], op=ALU.add)
                        fz[(st, m)][2] = self.inc(ins, "eV")
            feat_eV = self.counts["eV"]

            v_v = {}
            for c in range(2):
                for dh in range(2):
                    @block.tensor
                    def _(tn, c=c, dh=dh, seg=seg, xn_sQ=xn_sQ):
                        bank = pb[4 + c * 2 + dh]
                        tn.wait_ge(self.sems["sQ"], xn_sQ)
                        tn.wait_ge(self.sems["eV"], self.counts["eV"])
                        for kt in range(8):
                            ins = tn.matmul(
                                bank[:, 0:512],
                                t["a_xn"][:, kt * TSEG + c * 128: kt * TSEG + c * 128 + 128],
                                t["aw_v"][:, kt * DM + dh * 512: kt * DM + dh * 512 + 512],
                                start=(kt == 0), stop=(kt == 7))
                        self.inc(ins, "eT")
                    mmc = self.counts["eT"]

                    @block.vector
                    def _(v, c=c, dh=dh, mmc=mmc):
                        bank = pb[4 + c * 2 + dh]
                        v.wait_ge(self.sems["eT"], mmc)
                        ins = v.tensor_copy(
                            out=t["a_vtm"][:, c * DM + dh * 512: c * DM + dh * 512 + 512],
                            in_=bank[:, 0:512])
                        v_v[(c, dh)] = self.inc(ins, "eV")

            @block.tensor
            def _(tn, seg=seg):
                bank = pb[1]
                tn.wait_ge(self.sems["eV"], feat_eV)
                for kt in range(8):
                    ins = tn.matmul(bank[0:16, 0:TSEG],
                                    t["c_gw1"][:, kt * E1:(kt + 1) * E1],
                                    t["a_xn"][:, kt * TSEG:(kt + 1) * TSEG],
                                    start=(kt == 0), stop=(kt == 7))
                self.inc(ins, "eT")
            lg_eT = self.counts["eT"]

            @block.vector
            def _(v, seg=seg, lg_eT=lg_eT):
                v.wait_ge(self.sems["eT"], lg_eT)
                ins = v.tensor_copy(out=t["lg_fm"][0:16, t0:t0 + TSEG],
                                    in_=pb[1][0:16, 0:TSEG])
                self.inc(ins, "eV")
            lgv_eV = self.counts["eV"]

            if upto == "proj":
                continue
            @block.gpsimd
            def _(g, seg=seg):
                g.wait_ge(self.sems["eV"], feat_eV)
                self.dma(g, t["parkK_dram"].rearrange("(tl p) t -> p tl t", p=128),
                         t["a_k"][:].rearrange("p (tl t) -> p tl t", t=TSEG), "sG")
            park_sG = self.counts["sG"]

            @block.sync
            def _(sync, seg=seg, park_sG=park_sG):
                sync.wait_ge(self.sems["sG"], park_sG)
                sync.wait_ge(self.sems["eT"], self.counts["eT"])
                for c in range(2):
                    self.trdma(sync, t["a_ktm"][:, c * DM:(c + 1) * DM],
                               t["parkK_dram"][:, c * 128:(c + 1) * 128], "sQ")
            ktm_sQ = self.counts["sQ"]

            cum_v = {}
            for tp in range(8):
                @block.tensor
                def _(tn, tp=tp, lgv_eV=lgv_eV):
                    bank = pb[tp % 2]
                    if tp < 2:
                        tn.wait_ge(self.sems["eV"], lgv_eV)
                    else:
                        tn.wait_ge(self.sems["eV"], cum_v[tp - 2])
                    for tt in range(tp + 1):
                        lhs = t["a_dd"] if tt < tp else t["a_ud"]
                        ins = tn.matmul(bank[:, 0:TSEG], lhs[:],
                                        t["a_k"][:, tt * TSEG:(tt + 1) * TSEG],
                                        start=(tt == 0), stop=(tt == tp))
                    self.inc(ins, "eT")
                mmc = self.counts["eT"]

                @block.vector
                def _(v, tp=tp, mmc=mmc):
                    bank = pb[tp % 2]
                    v.wait_ge(self.sems["eT"], mmc)
                    ins = v.tensor_copy(out=t["a_cum"][:, tp * TSEG:(tp + 1) * TSEG],
                                        in_=bank[:, 0:TSEG])
                    cum_v[tp] = self.inc(ins, "eV")

            if upto == "cum":
                continue
            zp_v = {}
            zmm = {}
            for tt in range(8):
                @block.vector
                def _(v, tt=tt):
                    buf = "a_z1" if tt % 2 == 0 else "a_z2"
                    v.wait_ge(self.sems["eV"], cum_v[tt])
                    if tt >= 2:
                        v.wait_ge(self.sems["eT"], zmm[tt - 2])
                    ins = v.tensor_tensor(out=t[buf][:],
                                          in0=t["a_q"][:, tt * TSEG:(tt + 1) * TSEG],
                                          in1=t["a_cum"][:, tt * TSEG:(tt + 1) * TSEG],
                                          op=ALU.mult)
                    zp_v[tt] = self.inc(ins, "eV")

                @block.tensor
                def _(tn, tt=tt):
                    buf = "a_z1" if tt % 2 == 0 else "a_z2"
                    tn.wait_ge(self.sems["eV"], zp_v[tt])
                    ins = tn.matmul(pb[3][0:16, 0:TSEG],
                                    t["c_bones16"][:, tt * 16:(tt + 1) * 16],
                                    t[buf][:], start=(tt == 0), stop=(tt == 7))
                    zmm[tt] = self.inc(ins, "eT")

            @block.vector
            def _(v, seg=seg):
                v.wait_ge(self.sems["eT"], zmm[7])
                v.tensor_copy(out=t["a_zsb"][0:16, :], in_=pb[3][0:16, 0:TSEG])
                v.tensor_scalar_add(t["a_zsb"][0:16, :], t["a_zsb"][0:16, :], 1e-6)
                ins = v.reciprocal(t["a_zsb"][0:16, :], t["a_zsb"][0:16, :])
                self.inc(ins, "eV")
            zsb_eV = self.counts["eV"]

            zr_v = {}
            for tt in range(8):
                @block.tensor
                def _(tn, tt=tt):
                    bank = pb[tt % 2]
                    tn.wait_ge(self.sems["eV"], zsb_eV if tt < 2 else zr_v[tt - 2])
                    ins = tn.matmul(bank[:, 0:TSEG],
                                    t["c_zsel"][:, tt * 128:(tt + 1) * 128],
                                    t["a_zsb"][0:16, :], start=True, stop=True)
                    self.inc(ins, "eT")
                mmc = self.counts["eT"]

                @block.vector
                def _(v, tt=tt, mmc=mmc):
                    bank = pb[tt % 2]
                    v.wait_ge(self.sems["eT"], mmc)
                    ins = v.tensor_copy(out=t["a_zrep"][:, tt * TSEG:(tt + 1) * TSEG],
                                        in_=bank[:, 0:TSEG])
                    zr_v[tt] = self.inc(ins, "eV")

            if upto == "z":
                continue
            for c in range(2):
                sc_v, y_v = {}, {}
                for tt in range(8):
                    qs = t["a_q"][:, tt * TSEG + c * 128: tt * TSEG + c * 128 + 128]
                    ks = t["a_k"][:, tt * TSEG + c * 128: tt * TSEG + c * 128 + 128]

                    @block.tensor
                    def _(tn, c=c, tt=tt, qs=qs, ks=ks, ktm_sQ=ktm_sQ):
                        pbP, pbQ = pb[(tt % 2) * 2], pb[(tt % 2) * 2 + 1]
                        if tt < 2:
                            tn.wait_ge(self.sems["sQ"], ktm_sQ)
                            tn.wait_ge(self.sems["eV"], self.counts["eV"])
                        else:
                            tn.wait_ge(self.sems["eV"], y_v[tt - 2])
                        tn.wait_ge(self.sems["eS"], sbf[tt])
                        for a in range(2):
                            tn.matmul(pbQ[64 * a:64 * a + 64, 0:128],
                                      t["a_Sbf"][:, tt * 128 + 64 * a:
                                                 tt * 128 + 64 * a + 64], qs,
                                      start=True, stop=True)
                        tn.matmul(pbP[0:128, 0:128], ks[0:64, :], qs[0:64, :],
                                  start=True, stop=True)
                        ins = tn.matmul(pbQ[0:128, 256:384], ks[64:128, :],
                                        qs[64:128, :], start=True, stop=True)
                        self.inc(ins, "eT")
                    mm1 = self.counts["eT"]

                    @block.vector
                    def _(v, c=c, tt=tt, mm1=mm1):
                        pbP, pbQ = pb[(tt % 2) * 2], pb[(tt % 2) * 2 + 1]
                        v.wait_ge(self.sems["eT"], mm1)
                        v.tensor_tensor(out=t["a_sc"][:, (tt * 2) * 128:(tt * 2) * 128 + 128],
                                        in0=pbP[:, 0:128], in1=t["c_triu"][:],
                                        op=ALU.mult)
                        ins = v.tensor_tensor(
                            out=t["a_sc"][:, (tt * 2 + 1) * 128:(tt * 2 + 1) * 128 + 128],
                            in0=pbQ[:, 256:384], in1=t["c_triu"][:], op=ALU.mult)
                        sc_v[tt] = self.inc(ins, "eV")

                    @block.tensor
                    def _(tn, c=c, tt=tt):
                        pbP, pbQ = pb[(tt % 2) * 2], pb[(tt % 2) * 2 + 1]
                        tn.wait_ge(self.sems["eV"], sc_v[tt])
                        for a in range(2):
                            tn.matmul(pbQ[64 * a:64 * a + 64, 128:256],
                                      t["a_vtm"][:, c * DM + tt * 128 + 64 * a:
                                                 c * DM + tt * 128 + 64 * a + 64],
                                      t["a_sc"][:, (tt * 2 + a) * 128:
                                                (tt * 2 + a) * 128 + 128],
                                      start=True, stop=True)
                        ins = tn.matmul(pbP[0:128, 256:384],
                                        t["a_ktm"][:, c * DM + tt * 128:
                                                   c * DM + tt * 128 + 128],
                                        t["a_vtm"][:, c * DM + tt * 128:
                                                   c * DM + tt * 128 + 128],
                                        start=True, stop=True)
                        self.inc(ins, "eT")
                    mm2 = self.counts["eT"]

                    @block.vector
                    def _(v, c=c, tt=tt, mm2=mm2):
                        pbP, pbQ = pb[(tt % 2) * 2], pb[(tt % 2) * 2 + 1]
                        v.wait_ge(self.sems["eT"], mm2)
                        zr = t["a_zrep"][:, tt * TSEG + c * 128:
                                         tt * TSEG + c * 128 + 128]
                        v.tensor_tensor(out=t["a_t1"][:, 0:128], in0=pbQ[:, 0:128],
                                        in1=zr, op=ALU.mult)
                        v.tensor_tensor(out=t["a_t2"][:, 0:128], in0=pbQ[:, 128:256],
                                        in1=zr, op=ALU.mult)
                        ins = v.tensor_tensor(out=t["a_y"][:, c * DM + tt * 128:
                                                           c * DM + tt * 128 + 128],
                                              in0=t["a_t1"][:, 0:128],
                                              in1=t["a_t2"][:, 0:128], op=ALU.add)
                        if "sadd" not in skip:
                            v.tensor_tensor(out=t["a_S"][0:64, tt * 64:(tt + 1) * 64],
                                            in0=t["a_S"][0:64, tt * 64:(tt + 1) * 64],
                                            in1=pbP[0:64, 256:320], op=ALU.add)
                            ins = v.tensor_tensor(out=t["a_S"][64:128, tt * 64:(tt + 1) * 64],
                                                  in0=t["a_S"][64:128, tt * 64:(tt + 1) * 64],
                                                  in1=pbP[64:128, 320:384], op=ALU.add)
                        y_v[tt] = self.inc(ins, "eV")

                    @block.scalar
                    def _(sc, c=c, tt=tt):
                        sc.wait_ge(self.sems["eV"], y_v[tt])
                        if "sbf" not in skip:
                            sc.activation(out=t["a_Sbf"][0:64, tt * 128: tt * 128 + 64],
                                          in_=t["a_S"][0:64, tt * 64:(tt + 1) * 64],
                                          func=ACT.Identity)
                            ins = sc.activation(out=t["a_Sbf"][64:128, tt * 128 + 64: tt * 128 + 128],
                                                in_=t["a_S"][64:128, tt * 64:(tt + 1) * 64],
                                                func=ACT.Identity)
                        else:
                            ins = sc.activation(out=t["a_t3"][0:128, 0:64],
                                                in_=t["a_S"][0:128, 0:64],
                                                func=ACT.Identity)
                        sbf[tt] = self.inc(ins, "eS")

                @block.tensor
                def _(tn, c=c):
                    tn.wait_ge(self.sems["eV"], y_v[7])
                    tn.wait_ge(self.sems["eV"], y_v[6])
                    for dh in range(2):
                        bank = pb[dh * 4]
                        for kt in range(8):
                            ins = tn.matmul(
                                bank[:, 0:512],
                                t["a_y"][:, c * DM + kt * 128: c * DM + kt * 128 + 128],
                                t["aw_o"][:, kt * DM + dh * 512: kt * DM + dh * 512 + 512],
                                start=(kt == 0), stop=(kt == 7))
                    self.inc(ins, "eT")
                op_eT = self.counts["eT"]

                @block.vector
                def _(v, c=c, op_eT=op_eT):
                    v.wait_ge(self.sems["eT"], op_eT)
                    v.wait_ge(self.sems["sG"], self.counts["sG"])
                    for dh in range(2):
                        ins = v.tensor_tensor(out=t["a_s"][:, dh * 512:(dh + 1) * 512],
                                              in0=pb[dh * 4][:, 0:512],
                                              in1=t["c_bo"][:, dh * 512:(dh + 1) * 512],
                                              op=ALU.add)
                    self.inc(ins, "eV")
                s_eV = self.counts["eV"]

                @block.gpsimd
                def _(g, c=c, seg=seg, s_eV=s_eV):
                    g.wait_ge(self.sems["eV"], s_eV)
                    self.dma(g, t["s_dram"][t0 + c * 128: t0 + c * 128 + 128, :],
                             t["a_s"][:], "sG")

    def ph_routing(self, layer):
        t, block = self.t, self.block
        E = E0 if layer == 0 else E1
        CG = CAP0G if layer == 0 else CAP1G
        LCAP = LCAP0 if layer == 0 else LCAP1
        SLOTS = SLOTS0 if layer == 0 else SLOTS1
        ecap = "c_ecap0" if layer == 0 else "c_ecap1"
        trashf = "c_trash0f" if layer == 0 else "c_trash1f"
        NE = NCHUNK * E
        HUGE = 1.0e6
        for n in ("r_lgtm", "r_m1tm", "r_m2tm", "r_s1tm", "r_s2tm",
                  "r_k1tm", "r_k2tm", "r_dif", "r_ttmp"):
            self.sbl(n, (128, NE), F32)
        for n in ("r_max", "r_sum", "r_g1", "r_max2", "r_g2", "r_den",
                  "r_f1f", "r_gk", "r_f2f"):
            self.sbl(n, (128, NCHUNK), F32)
        self.sbl("r_msk", (128, NCHUNK), U8)
        for n in ("r_m1fm", "r_m2fm", "r_loc1", "r_loc2", "r_k1fm",
                  "r_k2fm", "r_ftmp", "r_lgf"):
            self.sbl(n, (16, T_TOK), F32)
        self.sbl("r_ag", (self.ncores, 64), F32)
        self.sbl("r_pr", (1, 128), F32)
        self.sbl("r_sc", (16, 8), F32)
        self.sbl("r_xn", (128, DM), BF16)
        pba = self.psl("r_pba", (128, 512))
        pbb = self.psl("r_pbb", (128, 512))

        def e3(name):
            return t[name][:].rearrange("p (c e) -> p c e", e=E)

        def bc(name):
            return t[name][:].unsqueeze(2).broadcast_to((128, NCHUNK, E))

        @block.vector
        def _(v):
            ins = v.tensor_copy(out=t["r_lgf"][0:E, :], in_=t["lg_fm"][0:E, :])
            self.inc(ins, "eV")
        lgf_eV = self.counts["eV"]
        tr_v = []
        for c in range(NCHUNK):
            @block.tensor
            def _(tn, c=c):
                bank = pba if c % 2 == 0 else pbb
                tn.wait_ge(self.sems["eV"], tr_v[c - 2] if c >= 2 else lgf_eV)
                ins = tn.transpose(bank[0:128, 0:E],
                                   t["r_lgf"][0:E, c * 128:(c + 1) * 128],
                                   t["c_eye"][0:E, 0:E])
                self.inc(ins, "eT")
            cnt = self.counts["eT"]

            @block.vector
            def _(v, c=c, cnt=cnt):
                bank = pba if c % 2 == 0 else pbb
                v.wait_ge(self.sems["eT"], cnt)
                ins = v.tensor_copy(out=t["r_lgtm"][:, c * E:(c + 1) * E],
                                    in_=bank[0:128, 0:E])
                tr_v.append(self.inc(ins, "eV"))

        def vtree(v, src, scratch, op, out2):
            w = E // 2
            v.tensor_tensor(out=scratch[:, :, 0:w], in0=src[:, :, 0:w],
                            in1=src[:, :, w:2 * w], op=op)
            while w > 2:
                w2 = w // 2
                v.tensor_tensor(out=scratch[:, :, 0:w2], in0=scratch[:, :, 0:w2],
                                in1=scratch[:, :, w2:w], op=op)
                w = w2
            v.tensor_tensor(out=out2.unsqueeze(2), in0=scratch[:, :, 0:1],
                            in1=scratch[:, :, 1:2], op=op)

        @block.vector
        def _(v):
            v.wait_ge(self.sems["eV"], tr_v[-1])
            self.vgap(v)
            vtree(v, e3("r_lgtm"), e3("r_ttmp"), ALU.max, t["r_max"][:])
            self.vgap(v)
            for e in range(E):
                v.tensor_tensor(out=e3("r_m1tm")[:, :, e], in0=e3("r_lgtm")[:, :, e],
                                in1=t["r_max"][:], op=ALU.is_ge)
            self.vgap(v)
            v.scalar_tensor_tensor(out=t["r_ttmp"][:], in0=t["r_m1tm"][:],
                                   scalar=-1.0e30, in1=t["r_lgtm"][:],
                                   op0=ALU.mult, op1=ALU.add)
            vtree(v, e3("r_ttmp"), e3("r_dif"), ALU.max, t["r_max2"][:])
            self.vgap(v)
            for e in range(E):
                v.tensor_tensor(out=e3("r_m2tm")[:, :, e], in0=e3("r_ttmp")[:, :, e],
                                in1=t["r_max2"][:], op=ALU.is_ge)
            v.tensor_tensor(out=e3("r_dif"), in0=e3("r_lgtm"),
                            in1=bc("r_max"), op=ALU.subtract)
            ins = v.tensor_tensor(out=t["r_max2"][:], in0=t["r_max2"][:],
                                  in1=t["r_max"][:], op=ALU.subtract)
            self.inc(ins, "eV")

        @block.scalar
        def _(sc):
            sc.wait_ge(self.sems["eV"], self.counts["eV"])
            sc.activation(out=t["r_dif"][:], in_=t["r_dif"][:], func=ACT.Exp)
            ins = sc.activation(out=t["r_g2"][:], in_=t["r_max2"][:], func=ACT.Exp)
            self.inc(ins, "eS")
        exp_eS = self.counts["eS"]

        @block.vector
        def _(v):
            v.wait_ge(self.sems["eS"], exp_eS)
            vtree(v, e3("r_dif"), e3("r_ttmp"), ALU.add, t["r_sum"][:])
            self.vgap(v)
            v.reciprocal(t["r_g1"][:], t["r_sum"][:])
            self.vgap(v)
            v.tensor_tensor(out=t["r_g2"][:], in0=t["r_g2"][:], in1=t["r_g1"][:],
                            op=ALU.mult)
            if layer == 0:
                self.vgap(v)
                v.tensor_tensor(out=t["r_den"][:], in0=t["r_g1"][:],
                                in1=t["r_g2"][:], op=ALU.add)
                self.vgap(v)
                v.tensor_scalar_max(t["r_den"][:], t["r_den"][:], 1e-9)
                self.vgap(v)
                v.reciprocal(t["r_den"][:], t["r_den"][:])
                self.vgap(v)
                v.tensor_tensor(out=t["r_g1"][:], in0=t["r_g1"][:],
                                in1=t["r_den"][:], op=ALU.mult)
                ins = v.tensor_tensor(out=t["r_g2"][:], in0=t["r_g2"][:],
                                      in1=t["r_den"][:], op=ALU.mult)
            else:
                ins = v.tensor_copy(out=t["r_den"][:], in_=t["r_g1"][:])
            self.inc(ins, "eV")
        gates_eV = self.counts["eV"]

        plan3 = [("r_m1tm", "r_m1fm")] + ([("r_m2tm", "r_m2fm")] if layer == 0 else [])
        m2f_v = []
        for c in range(NCHUNK):
            for wi, (src, dst) in enumerate(plan3):
                idx = c * len(plan3) + wi

                @block.tensor
                def _(tn, c=c, src=src, idx=idx):
                    bank = pba if idx % 2 == 0 else pbb
                    if idx < 2:
                        tn.wait_ge(self.sems["eV"], gates_eV)
                    else:
                        tn.wait_ge(self.sems["eV"], m2f_v[idx - 2])
                    ins = tn.transpose(bank[0:E, 0:128],
                                       t[src][:, c * E:(c + 1) * E],
                                       t["c_eye"][:])
                    self.inc(ins, "eT")
                cnt = self.counts["eT"]

                @block.vector
                def _(v, c=c, dst=dst, idx=idx, cnt=cnt):
                    bank = pba if idx % 2 == 0 else pbb
                    v.wait_ge(self.sems["eT"], cnt)
                    ins = v.tensor_copy(out=t[dst][0:E, c * 128:(c + 1) * 128],
                                        in_=bank[0:E, 0:128])
                    m2f_v.append(self.inc(ins, "eV"))

        @block.vector
        def _(v):
            v.wait_ge(self.sems["eV"], m2f_v[-1])
            v.tensor_tensor_scan(out=t["r_loc1"][0:E, :], data0=t["r_m1fm"][0:E, :],
                                 data1=t["r_m1fm"][0:E, :], initial=0.0,
                                 op0=ALU.add, op1=ALU.bypass)
            if layer == 0:
                v.tensor_tensor_scan(out=t["r_loc2"][0:E, :],
                                     data0=t["r_m2fm"][0:E, :],
                                     data1=t["r_m2fm"][0:E, :], initial=0.0,
                                     op0=ALU.add, op1=ALU.bypass)
            v.tensor_copy(out=t["r_sc"][0:E, 6:7],
                          in_=t["r_loc1"][0:E, T_TOK - 1:T_TOK])
            self.vgap(v)
            ins = v.tensor_scalar_add(t["r_sc"][0:E, 7:8], t["r_sc"][0:E, 6:7], -1.0)
            self.inc(ins, "eV")
        scan_eV = self.counts["eV"]

        @block.gpsimd
        def _(g):
            g.wait_ge(self.sems["eV"], scan_eV)
            self.dma(g, t["cc_in"][0:1, 0:E], t["r_loc1"][0:E, T_TOK - 1:T_TOK], "sG")
            if layer == 0:
                self.dma(g, t["cc_in"][0:1, E:2 * E],
                         t["r_loc2"][0:E, T_TOK - 1:T_TOK], "sG")
            g.wait_ge(self.sems["sG"], self.counts["sG"])
            ins = g.collective_compute(
                "AllGather", ALU.bypass, replica_groups=[list(range(self.ncores))],
                ins=[t["cc_in"].opt()], outs=[t["cc_out"].opt()])
            self.inc(ins, "sCC")
            g.wait_ge(self.sems["sCC"], self.counts["sCC"])
            self.dma(g, t["r_ag"][:], t["cc_out"], "sG")
            g.wait_ge(self.sems["sG"], self.counts["sG"])
            ins = g.iota(t["c_scr"][0:1, 0:1], [[0, 1]], base=0, channel_multiplier=0)
            self.inc(ins, "eG")
        ag_eG = self.counts["eG"]

        @block.tensor
        def _(tn):
            tn.wait_ge(self.sems["eG"], ag_eG)
            tn.matmul(pba[0:1, 0:64], t["c_pref"][:], t["r_ag"][:],
                      start=True, stop=True)
            ins = tn.matmul(pba[0:1, 64:128], t["c_ones8"][:], t["r_ag"][:],
                            start=True, stop=True)
            self.inc(ins, "eT")
        pr_eT = self.counts["eT"]

        @block.vector
        def _(v):
            v.wait_ge(self.sems["eT"], pr_eT)
            ins = v.tensor_copy(out=t["r_pr"][0:1, 0:128], in_=pba[0:1, 0:128])
            self.inc(ins, "eV")
        pr_eV = self.counts["eV"]

        @block.gpsimd
        def _(g):
            g.wait_ge(self.sems["eV"], pr_eV)
            self.dma(g, t["row_dram"][0:2, 0:64], t["r_pr"][0:1, 0:128], "sG")
            g.wait_ge(self.sems["sG"], self.counts["sG"])
            self.dma(g, t["r_sc"][0:E, 0:1], t["row_dram"][0:1, 0:E], "sG")
            if layer == 0:
                self.dma(g, t["r_sc"][0:E, 1:2], t["row_dram"][0:1, E:2 * E], "sG")
            self.dma(g, t["r_sc"][0:E, 2:3], t["row_dram"][1:2, 0:E], "sG")
            g.wait_ge(self.sems["sG"], self.counts["sG"])
            ins = g.iota(t["c_scr"][0:1, 1:2], [[0, 1]], base=0, channel_multiplier=0)
            self.inc(ins, "eG")
        sc_eG = self.counts["eG"]

        @block.vector
        def _(v):
            v.wait_ge(self.sems["eG"], sc_eG)
            sl = t["r_sc"]

            def choice(locn, kfm, prefcols):
                v.tensor_scalar(out=t["r_ftmp"][0:E, :], in0=t[locn][0:E, :],
                                scalar1=prefcols[0], scalar2=None, op0=ALU.add)
                if len(prefcols) > 1:
                    v.tensor_scalar(out=t["r_ftmp"][0:E, :], in0=t["r_ftmp"][0:E, :],
                                    scalar1=prefcols[1], scalar2=None, op0=ALU.add)
                v.tensor_single_scalar(out=t[kfm][0:E, :], in_=t["r_ftmp"][0:E, :],
                                       scalar=CG + 0.5, op=ALU.is_lt)
                if locn == "r_loc2":
                    v.tensor_scalar(out=t[locn][0:E, :], in0=t[locn][0:E, :],
                                    scalar1=sl[0:E, 7:8], scalar2=None, op0=ALU.add)
                v.tensor_scalar_add(t[locn][0:E, :], t[locn][0:E, :], -1.0)
                v.tensor_single_scalar(out=t["r_ftmp"][0:E, :], in_=t[locn][0:E, :],
                                       scalar=LCAP - 0.5, op=ALU.is_lt)
                v.tensor_tensor(out=t[kfm][0:E, :], in0=t[kfm][0:E, :],
                                in1=t["r_ftmp"][0:E, :], op=ALU.mult)
                v.tensor_single_scalar(out=t["r_ftmp"][0:E, :], in_=t[locn][0:E, :],
                                       scalar=LCAP - 0.5, op=ALU.is_ge)
                v.tensor_scalar(out=t[locn][0:E, :], in0=t[locn][0:E, :],
                                scalar1=t[ecap][0:E, 0:1], scalar2=None, op0=ALU.add)
                v.scalar_tensor_tensor(out=t[locn][0:E, :], in0=t["r_ftmp"][0:E, :],
                                       scalar=HUGE, in1=t[locn][0:E, :],
                                       op0=ALU.mult, op1=ALU.add)
            choice("r_loc1", "r_k1fm", [sl[0:E, 0:1]])
            if layer == 0:
                choice("r_loc2", "r_k2fm", [sl[0:E, 1:2], sl[0:E, 2:3]])
            ins = v.tensor_copy(out=t["r_ftmp"][0:1, 0:1], in_=t["r_ftmp"][0:1, 0:1])
            self.inc(ins, "eV")
        fm_eV = self.counts["eV"]

        plan9 = [("r_loc1", "r_s1tm"), ("r_k1fm", "r_k1tm")]
        if layer == 0:
            plan9 += [("r_loc2", "r_s2tm"), ("r_k2fm", "r_k2tm")]
        s2t_v = []
        for c in range(NCHUNK):
            for wi, (src, dst) in enumerate(plan9):
                idx = c * len(plan9) + wi

                @block.tensor
                def _(tn, c=c, src=src, idx=idx):
                    bank = pba if idx % 2 == 0 else pbb
                    if idx < 2:
                        tn.wait_ge(self.sems["eV"], fm_eV)
                    else:
                        tn.wait_ge(self.sems["eV"], s2t_v[idx - 2])
                    ins = tn.transpose(bank[0:128, 0:E],
                                       t[src][0:E, c * 128:(c + 1) * 128],
                                       t["c_eye"][0:E, 0:E])
                    self.inc(ins, "eT")
                cnt = self.counts["eT"]

                @block.vector
                def _(v, c=c, dst=dst, idx=idx, cnt=cnt):
                    bank = pba if idx % 2 == 0 else pbb
                    v.wait_ge(self.sems["eT"], cnt)
                    ins = v.tensor_copy(out=t[dst][:, c * E:(c + 1) * E],
                                        in_=bank[0:128, 0:E])
                    s2t_v.append(self.inc(ins, "eV"))

        @block.vector
        def _(v):
            v.wait_ge(self.sems["eV"], s2t_v[-1])
            self.vgap(v)

            def tok(mtm, stm, ktm, fdst, gsrc, gdst, fcol):
                A, B = e3("r_ttmp"), e3("r_dif")
                v.tensor_tensor(out=t["r_ttmp"][:], in0=t[mtm][:], in1=t[stm][:],
                                op=ALU.mult)
                v.tensor_tensor(out=t["r_dif"][:], in0=t[mtm][:], in1=t[ktm][:],
                                op=ALU.mult)
                w = E // 2
                while w > 1:
                    v.tensor_tensor(out=A[:, :, 0:w], in0=A[:, :, 0:w],
                                    in1=A[:, :, w:2 * w], op=ALU.add)
                    v.tensor_tensor(out=B[:, :, 0:w], in0=B[:, :, 0:w],
                                    in1=B[:, :, w:2 * w], op=ALU.add)
                    w //= 2
                v.tensor_tensor(out=t[fdst][:].unsqueeze(2), in0=A[:, :, 0:1],
                                in1=A[:, :, 1:2], op=ALU.add)
                v.tensor_tensor(out=t["r_gk"][:].unsqueeze(2), in0=B[:, :, 0:1],
                                in1=B[:, :, 1:2], op=ALU.add)
                self.vgap(v)
                v.tensor_tensor(out=self.t[gdst][:], in0=t[gsrc][:],
                                in1=t["r_gk"][:], op=ALU.mult)
                v.tensor_single_scalar(out=t["r_max"][:], in_=t[fdst][:],
                                       scalar=float(SLOTS) - 0.5, op=ALU.is_ge)
                self.vgap(v)
                v.tensor_copy(out=t["r_msk"][:], in_=t["r_max"][:])
                self.vgap(v)
                v.select(out=t[fdst][:], mask=t["r_msk"][:],
                         on_true=t[trashf][:, 0:1].broadcast_to((128, NCHUNK)),
                         on_false=t[fdst][:])
                self.vgap(v)
                v.tensor_copy(out=self.t[fcol][:], in_=t[fdst][:])
            tok("r_m1tm", "r_s1tm", "r_k1tm", "r_f1f", "r_g1", "g1_col", "f1_col")
            if layer == 0:
                tok("r_m2tm", "r_s2tm", "r_k2tm", "r_f2f", "r_g2", "g2_col",
                    "f2_col")
            ins = v.tensor_copy(out=t["r_max"][0:1, 0:1], in_=t["r_max"][0:1, 0:1])
            self.inc(ins, "eV")
        rt_eV = self.counts["eV"]

        xn_ld = []
        for c in range(NCHUNK):
            @block.sync
            def _(sync, c=c):
                if c >= 1:
                    sync.wait_ge(self.sems["sG"], self.counts["sG"])
                xn_ld.append(self.dma(sync, t["r_xn"][:],
                                      t["xn_dram"][c * 128:(c + 1) * 128, :], "sQ"))

            @block.gpsimd
            def _(g, c=c):
                if c == 0:
                    g.wait_ge(self.sems["eV"], rt_eV)
                g.wait_ge(self.sems["sQ"], xn_ld[c])
                self.idma(g, t["disp_dram"], t["r_xn"][:], "sG",
                          out_offset=bass.IndirectOffsetOnAxis(
                              self.t["f1_col"][:, c:c + 1], 0))
                if layer == 0:
                    self.idma(g, t["disp_dram"], t["r_xn"][:], "sG",
                              out_offset=bass.IndirectOffsetOnAxis(
                                  self.t["f2_col"][:, c:c + 1], 0))

    def ph_moe(self, layer):
        t, block = self.t, self.block
        E = E0 if layer == 0 else E1
        LCAP = LCAP0 if layer == 0 else LCAP1
        wname1 = "w_gate0" if layer == 0 else "w1"
        wname2 = "w_down0" if layer == 0 else "w2"
        gfun = ACT.Gelu if layer == 0 else ACT.Gelu_apprx_tanh
        self.sbl("e_dfm", (128, 8 * 512), BF16)
        self.sbl("e_w1", (128, 8 * 512), BF16)
        self.sbl("e_w1b", (128, 8 * 512), BF16)
        self.sbl("e_h", (128, 32 * 512), BF16)
        self.sbl("e_w2", (128, 1024), BF16)
        self.sbl("e_w2b", (128, 1024), BF16)
        self.sbl("e_eo", (128, 4 * 1024), BF16)
        pb = [self.psl(f"e_pb{i}") for i in range(8)]
        groups = []
        n = LCAP
        off = 0
        while n > 0:
            g_ = min(512, n)
            groups.append((off, g_))
            off += g_
            n -= g_

        for e in range(E):
            for gi, (goff, gN) in enumerate(groups):
                s0 = e * LCAP + goff
                dld = []

                @block.sync
                def _(sync, s0=s0, gN=gN, e=e, gi=gi):
                    sync.wait_ge(self.sems["eT"], self.counts["eT"])
                    for dtl in range(8):
                        dld.append(self.trdma(
                            sync, t["e_dfm"][:, dtl * 512: dtl * 512 + gN],
                            t["disp_dram"][s0:s0 + gN, dtl * 128:(dtl + 1) * 128],
                            "sQ"))

                for fs in range(8):
                    wbuf = "e_w1" if fs % 2 == 0 else "e_w1b"

                    @block.gpsimd
                    def _(g, e=e, fs=fs, wbuf=wbuf):
                        g.wait_ge(self.sems["eT"], self.counts["eT"])
                        for dtl in range(8):
                            self.dma(g, t[wbuf][:, dtl * 512:(dtl + 1) * 512],
                                     t[wname1][e, dtl * 128:(dtl + 1) * 128,
                                               fs * 512:(fs + 1) * 512], "sG")
                    w_sG = self.counts["sG"]

                    for fc in range(4):
                        @block.tensor
                        def _(tn, fs=fs, fc=fc, gN=gN, wbuf=wbuf, w_sG=w_sG):
                            bank = pb[(fs * 4 + fc) % 4]
                            if fs == 0 and fc == 0:
                                tn.wait_ge(self.sems["sQ"], dld[-1])
                                tn.wait_ge(self.sems["eV"], self.counts["eV"])
                            tn.wait_ge(self.sems["sG"], w_sG)
                            tn.wait_ge(self.sems["eS"], self.counts["eS"])
                            for dtl in range(8):
                                ins = tn.matmul(
                                    bank[:, 0:gN],
                                    t[wbuf][:, dtl * 512 + fc * 128:
                                            dtl * 512 + fc * 128 + 128],
                                    t["e_dfm"][:, dtl * 512: dtl * 512 + gN],
                                    start=(dtl == 0), stop=(dtl == 7))
                            self.inc(ins, "eT")
                        mmc = self.counts["eT"]

                        @block.scalar
                        def _(sc, fs=fs, fc=fc, gN=gN, mmc=mmc):
                            bank = pb[(fs * 4 + fc) % 4]
                            sc.wait_ge(self.sems["eT"], mmc)
                            ins = sc.activation(
                                out=t["e_h"][:, (fs * 4 + fc) * 512:
                                             (fs * 4 + fc) * 512 + gN],
                                in_=bank[:, 0:gN], func=gfun)
                            self.inc(ins, "eS")
                h_eS = self.counts["eS"]

                nsub = (gN + 127) // 128
                for ft in range(32):
                    wbuf = "e_w2" if ft % 2 == 0 else "e_w2b"

                    @block.gpsimd
                    def _(g, e=e, ft=ft, wbuf=wbuf):
                        g.wait_ge(self.sems["eT"], self.counts["eT"])
                        self.dma(g, t[wbuf][:],
                                 t[wname2][e, ft * 128:(ft + 1) * 128, :], "sG")
                    w2_sG = self.counts["sG"]

                    @block.tensor
                    def _(tn, ft=ft, gN=gN, nsub=nsub, wbuf=wbuf,
                          w2_sG=w2_sG, h_eS=h_eS):
                        tn.wait_ge(self.sems["sG"], w2_sG)
                        if ft == 0:
                            tn.wait_ge(self.sems["eS"], h_eS)
                            tn.wait_ge(self.sems["eV"], self.counts["eV"])
                        for sub in range(nsub):
                            sw = min(128, gN - sub * 128)
                            for dh in range(2):
                                ins = tn.matmul(
                                    pb[sub * 2 + dh][0:sw, 0:512],
                                    t["e_h"][:, ft * 512 + sub * 128:
                                             ft * 512 + sub * 128 + sw],
                                    t[wbuf][:, dh * 512:(dh + 1) * 512],
                                    start=(ft == 0), stop=(ft == 31))
                        self.inc(ins, "eT")
                mm2 = self.counts["eT"]

                @block.vector
                def _(v, gN=gN, nsub=nsub, mm2=mm2):
                    v.wait_ge(self.sems["eT"], mm2)
                    for sub in range(nsub):
                        sw = min(128, gN - sub * 128)
                        for dh in range(2):
                            ins = v.tensor_copy(
                                out=t["e_eo"][0:sw, sub * 1024 + dh * 512:
                                              sub * 1024 + dh * 512 + 512],
                                in_=pb[sub * 2 + dh][0:sw, 0:512])
                    self.inc(ins, "eV")
                eo_eV = self.counts["eV"]

                @block.scalar
                def _(sc, s0=s0, gN=gN, nsub=nsub, eo_eV=eo_eV):
                    sc.wait_ge(self.sems["eV"], eo_eV)
                    for sub in range(nsub):
                        sw = min(128, gN - sub * 128)
                        self.dma(sc, t["eo_dram"][s0 + sub * 128:
                                                  s0 + sub * 128 + sw, :],
                                 t["e_eo"][0:sw, sub * 1024: sub * 1024 + 1024],
                                 "sA")

    def ph_combine(self, layer):
        t, block = self.t, self.block
        self.sbl("cb_eoA", (128, DM), BF16)
        self.sbl("cb_eoB", (128, DM), BF16)
        self.sbl("cb_s", (128, DM), BF16)
        self.sbl("cb_x", (128, DM), BF16)
        self.sbl("cb_g", (128, DM), F32)
        self.sbl("cb_gb", (128, DM), BF16)
        self.sbl("cb_o", (128, DM), BF16 if layer == 0 else F32)
        for c in range(NCHUNK):
            ld = {}

            @block.gpsimd
            def _(g, c=c):
                g.wait_ge(self.sems["eV"], self.counts["eV"])
                ld["A"] = self.idma(g, t["cb_eoA"][:], t["eo_dram"], "sG",
                                    in_offset=bass.IndirectOffsetOnAxis(
                                        self.t["f1_col"][:, c:c + 1], 0))
                if layer == 0:
                    ld["B"] = self.idma(g, t["cb_eoB"][:], t["eo_dram"], "sG",
                                        in_offset=bass.IndirectOffsetOnAxis(
                                            self.t["f2_col"][:, c:c + 1], 0))

            @block.sync
            def _(sync, c=c):
                sync.wait_ge(self.sems["eV"], self.counts["eV"])
                ld["s"] = self.dma(sync, t["cb_s"][:],
                                   t["s_dram"][c * 128:(c + 1) * 128, :], "sQ")
                ld["x"] = self.dma(sync, t["cb_x"][:],
                                   t["x_dram"][c * 128:(c + 1) * 128, :], "sQ")

            @block.vector
            def _(v, c=c):
                v.wait_ge(self.sems["sG"], ld["B"] if layer == 0 else ld["A"])
                v.wait_ge(self.sems["eS"], self.counts["eS"])
                ins = v.tensor_scalar(out=t["cb_g"][:], in0=t["cb_eoA"][:],
                                      scalar1=self.t["g1_col"][:, c:c + 1],
                                      scalar2=None, op0=ALU.mult)
                if layer == 0:
                    ins = v.scalar_tensor_tensor(
                        out=t["cb_g"][:], in0=t["cb_eoB"][:],
                        scalar=self.t["g2_col"][:, c:c + 1], in1=t["cb_g"][:],
                        op0=ALU.mult, op1=ALU.add)
                self.inc(ins, "eV")
            g_eV = self.counts["eV"]

            @block.scalar
            def _(sc, c=c, g_eV=g_eV):
                sc.wait_ge(self.sems["eV"], g_eV)
                ins = sc.activation(out=t["cb_gb"][:], in_=t["cb_g"][:],
                                    func=ACT.Silu)
                self.inc(ins, "eS")
            sg_eS = self.counts["eS"]

            @block.vector
            def _(v, c=c, sg_eS=sg_eS):
                v.wait_ge(self.sems["eS"], sg_eS)
                v.wait_ge(self.sems["sQ"], ld["x"])
                v.wait_ge(self.sems["sA"], self.counts["sA"])
                v.tensor_tensor(out=t["cb_g"][:], in0=t["cb_s"][:],
                                in1=t["cb_gb"][:], op=ALU.mult)
                ins = v.tensor_tensor(out=t["cb_o"][:], in0=t["cb_x"][:],
                                      in1=t["cb_g"][:], op=ALU.add)
                self.inc(ins, "eV")
            o_eV = self.counts["eV"]

            @block.scalar
            def _(sc, c=c, o_eV=o_eV):
                sc.wait_ge(self.sems["eV"], o_eV)
                if layer == 0:
                    self.dma(sc, t["x_dram"][c * 128:(c + 1) * 128, :],
                             t["cb_o"][:], "sA")
                else:
                    self.dma(sc, t["out"][c * 128:(c + 1) * 128, :],
                             t["cb_o"][:], "sA")


def kernel(**inputs):
    from concourse.bass_utils import run_bass_kernel_spmd
    x = np.asarray(inputs["L0_attn_x"], dtype=np.float32)
    mk = MK()
    nc = mk.build()
    in_maps = [host_inputs(b, x[b], inputs, ncores=8) for b in range(8)]
    res = run_bass_kernel_spmd(nc, in_maps, core_ids=list(range(8)))
    return np.stack([np.asarray(res.results[b]["out"], dtype=np.float32)
                     for b in range(8)], axis=0)


# version 04
# speedup vs baseline: 1.0079x; 1.0079x over previous
import contextlib
import numpy as np
import ml_dtypes
import concourse.bass as bass
import concourse.mybir as mybir

F32 = mybir.dt.float32
BF16 = mybir.dt.bfloat16
U8 = mybir.dt.uint8
U32 = mybir.dt.uint32
ALU = mybir.AluOpType
ACT = mybir.ActivationFunctionType
AX = mybir.AxisListType.X

T_TOK, DM, NCHUNK = 4096, 1024, 32
DI, NH, DS, HD, DPROJ = 2048, 32, 64, 64, 8224
E0, CAP0G, LCAP0 = 8, 8192, 1408
E1, CAP1G, LCAP1 = 16, 2560, 512
SLOTS0, SLOTS1, TRASH = E0 * LCAP0, E1 * LCAP1, 128
H1, DH1, FF = 16, 64, 4096
TSEG, NSEG, LC = 256, 16, 128
HALF = 256
NT = 16
LEVELS = (1, 2, 4, 8, 16, 32, 64)


def _bf(x):
    return np.ascontiguousarray(np.asarray(x, dtype=np.float32).astype(ml_dtypes.bfloat16))


def host_inputs(core_id, x_shard, W, ncores=8):
    perm = list(range(0, 4096))
    for h in range(NH):
        perm += [4096 + 129 * h + s for s in range(64)]
    for h in range(NH):
        perm += [4096 + 129 * h + 64 + s for s in range(64)]
    perm += [4096 + 129 * h + 128 for h in range(NH)]
    al = np.empty((128, 16), np.float32)
    dr = np.empty((128, 16), np.float32)
    cw = np.empty((128, 64), np.float32)
    cb = np.empty((128, 16), np.float32)
    for tt in range(16):
        for p in range(128):
            al[p, tt] = W["L0_attn_A_log"][2 * tt + p // 64, p % 64]
            dr[p, tt] = W["L0_attn_D"][2 * tt + p // 64]
            cb[p, tt] = W["L0_attn_conv_b"][128 * tt + p]
            for k in range(4):
                cw[p, 4 * tt + k] = W["L0_attn_conv_w"][128 * tt + p, k]
    hsel = np.zeros((32, 2048), np.float32)
    for tt in range(16):
        for a in range(2):
            hsel[2 * tt + a, 128 * tt + 64 * a:128 * tt + 64 * a + 64] = 1.0
    bones16 = np.zeros((128, 128), np.float32)
    zsel = np.zeros((16, 1024), np.float32)
    for dt_ in range(8):
        for p in range(128):
            h = 2 * dt_ + p // 64
            bones16[p, 16 * dt_ + h] = 1.0
            zsel[h, 128 * dt_ + p] = 1.0
    pref = np.zeros((ncores, 1), np.float32)
    pref[:core_id] = 1.0
    d = {
        "x": np.ascontiguousarray(x_shard.astype(np.float32)),
        "W_inT": np.ascontiguousarray(W["L0_attn_W_in"][perm, :].T),
        "W_outT": np.ascontiguousarray(W["L0_attn_W_out"].T),
        "conv_w_r": cw, "conv_b_r": cb, "A_log_r": al, "D_r": dr,
        "dt_bias_c": np.ascontiguousarray(W["L0_attn_dt_bias"].reshape(32, 1)),
        "wg0": np.ascontiguousarray(W["L0_ffn_wg"]),
        "w_gate0": np.ascontiguousarray(W["L0_ffn_w_gate"]),
        "w_down0": np.ascontiguousarray(W["L0_ffn_w_down"]),
        "Wq": np.ascontiguousarray(W["L1_attn_Wq"]),
        "Wk": np.ascontiguousarray(W["L1_attn_Wk"]),
        "Wv": np.ascontiguousarray(W["L1_attn_Wv"]),
        "Wo": np.ascontiguousarray(W["L1_attn_Wo"]),
        "bo": np.ascontiguousarray(
            np.broadcast_to(W["L1_attn_bo"].reshape(1, DM), (128, DM))),
        "gate_wT": np.ascontiguousarray(W["L1_ffn_gate_w"].T),
        "w1": np.ascontiguousarray(W["L1_ffn_w1"]),
        "w2": np.ascontiguousarray(W["L1_ffn_w2"]),
        "hsel": hsel, "bones16": bones16, "zsel": zsel,
        "eye": np.eye(128, dtype=np.float32),
        "zeros_bf": _bf(np.zeros((128, 128))),
        "triu_bf": _bf(np.triu(np.ones((128, 128)))),
        "ones1_bf": _bf(np.ones((1, 128))),
        "eye_bf": _bf(np.eye(128)),
        "dd_bf": _bf(np.tile(np.eye(64), (2, 2))),
        "ud_bf": _bf(np.block([[np.eye(64), np.eye(64)],
                               [np.zeros((64, 64)), np.eye(64)]])),
        "prefmask": pref, "onesmask": np.ones((ncores, 1), np.float32),
        "ecap0": np.ascontiguousarray((np.arange(E0) * LCAP0).astype(np.float32).reshape(E0, 1)),
        "ecap1": np.ascontiguousarray((np.arange(E1) * LCAP1).astype(np.float32).reshape(E1, 1)),
    }
    for w in LEVELS:
        m = np.zeros((128, 64), np.float32)
        for k in range(64 // w):
            m[2 * k * w: 2 * k * w + w, k * w:(k + 1) * w] = 1
        d[f"mbank{w}"] = _bf(np.tile(m, (1, 8)))
    d["mbankd"] = _bf(np.tile(np.eye(128, dtype=np.float32), (1, 4)))
    return d


class MK:
    def __init__(self, debug_taps=(), ncores=8):
        self.ncores = ncores
        self.nc = bass.Bass("TRN2", target_bir_lowering=False,
                            detect_race_conditions=False)
        self.debug_taps = set(debug_taps)
        self.tap_req = []
        self.es = contextlib.ExitStack()
        self.local_es = None
        self.counts, self.sems, self.t = {}, {}, {}
        self.pb = {}

    def sem(self, name):
        if name not in self.sems:
            self.sems[name] = self.es.enter_context(self.nc.semaphore(name))
            self.counts[name] = 0
        return self.sems[name]

    def inc(self, instr, name, by=1):
        instr.then_inc(self.sem(name), by)
        self.counts[name] += by
        return self.counts[name]

    def dma(self, eng, out, in_, qsem, **kw):
        ins = eng.dma_start(out=out, in_=in_, **kw)
        return self.inc(ins, qsem, 16)

    def idma(self, eng, out, in_, qsem, out_offset=None, in_offset=None):
        ins = eng.indirect_dma_start(out=out, out_offset=out_offset,
                                     in_=in_, in_offset=in_offset)
        return self.inc(ins, qsem, 16)

    def trdma(self, eng, out, in_, qsem):
        ins = eng.dma_start_transpose(out=out, in_=in_)
        return self.inc(ins, qsem, 16)

    def vgap(self, v):
        v.tensor_copy(out=self.t["c_gap"][:], in_=self.t["c_eye"][:, 0:64])

    def snap(self):
        return dict(self.counts)

    def wait(self, eng, snapshot, only=None):
        for name, v in snapshot.items():
            if v <= 0 or (only is not None and name not in only):
                continue
            eng.wait_ge(self.sems[name], v)

    def sb(self, name, shape, dtype):
        tt = self.es.enter_context(self.nc.sbuf_tensor(name, list(shape), dtype))
        self.t[name] = tt
        return tt

    def sbl(self, name, shape, dtype):
        self._uid = getattr(self, "_uid", 0) + 1
        tt = self.local_es.enter_context(
            self.nc.sbuf_tensor(f"{name}_{self._uid}", list(shape), dtype))
        self.t[name] = tt
        return tt

    def psl(self, name, shape=(128, 512)):
        self._uid = getattr(self, "_uid", 0) + 1
        tt = self.local_es.enter_context(
            self.nc.psum_tensor(f"{name}_{self._uid}", list(shape), F32))
        self.t[name] = tt
        return tt

    def dram(self, name, shape, dtype, kind=None):
        tt = (self.nc.dram_tensor(name, list(shape), dtype, kind=kind) if kind
              else self.nc.dram_tensor(name, list(shape), dtype))
        self.t[name] = tt.ap()
        return self.t[name]

    def tap(self, name, src, shape, dtype=F32):
        if name in self.debug_taps:
            out = self.dram("dbg_" + name, shape, dtype, kind="ExternalOutput")
            self.tap_req.append((name, out, src))

    def flush_taps(self):
        if not self.tap_req:
            return
        snapshot, block = self.snap(), self.block

        @block.gpsimd
        def _(g):
            self.wait(g, snapshot)
            for name, out, src in self.tap_req:
                self.dma(g, out, src, "sG")
            g.wait_ge(self.sems["sG"], self.counts["sG"])
        self.tap_req = []

    @contextlib.contextmanager
    def phase(self, name):
        snapshot, block = self.snap(), self.block
        for engn in ("sync", "vector", "scalar", "tensor", "gpsimd"):
            getattr(block, engn)(lambda eng, s=snapshot: self.wait(eng, s))
        self.local_es = contextlib.ExitStack()
        with self.local_es:
            yield
        self.local_es = None

    def build(self):
        with self.es:
            self.declare_io()
            self.block = self.es.enter_context(self.nc.Block())
            for s in ("sQ", "sA", "sG", "eV", "eS", "eT", "eG", "sCC"):
                self.sem(s)
            with self.phase("prep"):
                self.ph_prep()
            with self.phase("ln0"):
                self.ph_ln_park()
            with self.phase("mamba"):
                self.ph_mamba()
            with self.phase("rout0"):
                self.ph_routing(0)
            with self.phase("moe0"):
                self.ph_moe(0)
            with self.phase("comb0"):
                self.ph_combine(0)
            with self.phase("ln1"):
                self.ph_ln_park()
            with self.phase("attn"):
                self.ph_attn()
            with self.phase("rout1"):
                self.ph_routing(1)
            with self.phase("moe1"):
                self.ph_moe(1)
            with self.phase("comb1"):
                self.ph_combine(1)
            self.ph_final_wait()
            self.flush_taps()
        return self.nc

    def declare_io(self):
        dr = self.dram
        dr("x", (T_TOK, DM), F32, kind="ExternalInput")
        for n, sh in [
                ("W_inT", (DM, DPROJ)), ("W_outT", (DI, DM)), ("conv_w_r", (128, 64)),
                ("conv_b_r", (128, 16)), ("A_log_r", (128, 16)), ("D_r", (128, 16)),
                ("dt_bias_c", (32, 1)), ("wg0", (DM, E0)), ("w_gate0", (E0, DM, FF)),
                ("w_down0", (E0, FF, DM)), ("Wq", (DM, DM)), ("Wk", (DM, DM)),
                ("Wv", (DM, DM)), ("Wo", (DM, DM)), ("bo", (128, DM)),
                ("gate_wT", (DM, E1)), ("w1", (E1, DM, FF)), ("w2", (E1, FF, DM)),
                ("bones16", (128, 128)), ("eye", (128, 128)),
                ("prefmask", (self.ncores, 1)),
                ("onesmask", (self.ncores, 1)), ("ecap0", (E0, 1)),
                ("ecap1", (E1, 1))]:
            dr(n, sh, F32, kind="ExternalInput")
        for n, sh in [("hsel", (32, 2048)), ("zsel", (16, 1024))]:
            dr(n, sh, F32, kind="ExternalInput")
        for n, sh in [("zeros_bf", (128, 128)), ("triu_bf", (128, 128)),
                      ("ones1_bf", (1, 128)), ("eye_bf", (128, 128)),
                      ("dd_bf", (128, 128)), ("ud_bf", (128, 128))]:
            dr(n, sh, BF16, kind="ExternalInput")
        for w in LEVELS:
            dr(f"mbank{w}", (128, 512), BF16, kind="ExternalInput")
        dr("mbankd", (128, 512), BF16, kind="ExternalInput")
        dr("out", (T_TOK, DM), F32, kind="ExternalOutput")
        dr("x_dram", (T_TOK, DM), BF16)
        dr("xn_dram", (T_TOK, DM), BF16)
        dr("s_dram", (T_TOK, DM), BF16)
        dr("woutbf_dram", (DI, DM), BF16)
        dr("winbf_dram", (DM, DPROJ), BF16)
        dr("disp_dram", (SLOTS0 + TRASH, DM), BF16)
        dr("eo_dram", (SLOTS0 + TRASH, DM), BF16)
        dr("parkX_dram", (DI, HALF), BF16)
        dr("parkB_dram", (DI, HALF), BF16)
        dr("parkK_dram", (DM, TSEG), BF16)
        dr("cc_in", (1, 64), F32)
        dr("cc_out", (self.ncores, 64), F32)
        dr("row_dram", (4, 64), F32)
        self.sb("c_eye", (128, 128), F32)
        self.sb("c_eyebf", (128, 128), BF16)
        self.sb("c_zeros", (128, 128), BF16)
        self.sb("c_triu", (128, 128), BF16)
        self.sb("c_ones1", (1, 128), BF16)
        self.sb("c_hsel", (32, 2048), F32)
        self.sb("c_zsel", (16, 1024), F32)
        self.sb("c_bones16", (128, 128), F32)
        self.sb("c_convw", (128, 64), F32)
        self.sb("c_convb", (128, 16), F32)
        self.sb("c_negA", (128, 16), F32)
        self.sb("c_D", (128, 16), F32)
        self.sb("c_dtb", (32, 1), F32)
        self.sb("c_wg0", (128, 8 * E0), BF16)
        self.sb("c_gw1", (128, 8 * E1), BF16)
        self.sb("c_pref", (self.ncores, 1), F32)
        self.sb("c_ones8", (self.ncores, 1), F32)
        self.sb("c_bo", (128, DM), BF16)
        self.sb("c_ecap0", (E0, 1), F32)
        self.sb("c_ecap1", (E1, 1), F32)

        self.sb("c_iota", (128, 1), U32)
        self.sb("c_scr", (1, 2), U32)
        self.sb("c_gap", (128, 64), F32)
        self.sb("c_eps", (128, 1), F32)
        self.sb("c_eps6", (128, 1), F32)
        self.sb("c_trash0", (128, 1), U32)
        self.sb("c_trash1", (128, 1), U32)
        self.sb("c_trash0f", (128, 1), F32)
        self.sb("c_trash1f", (128, 1), F32)
        self.sb("lg_fm", (16, T_TOK), BF16)
        self.sb("f1_col", (128, NCHUNK), U32)
        self.sb("f2_col", (128, NCHUNK), U32)
        self.sb("g1_col", (128, NCHUNK), F32)
        self.sb("g2_col", (128, NCHUNK), F32)

    def ph_prep(self):
        t, block = self.t, self.block
        self.sbl("tmpA", (128, 16), F32)
        self.sbl("tmpZ", (128, DM), BF16)

        @block.sync
        def _(sync):
            for n, dst in [("eye", "c_eye"), ("eye_bf", "c_eyebf"),
                           ("conv_w_r", "c_convw"), ("conv_b_r", "c_convb"),
                           ("D_r", "c_D"),
                           ("A_log_r", "tmpA"), ("dt_bias_c", "c_dtb"),
                           ("zeros_bf", "c_zeros"),
                           ("triu_bf", "c_triu"), ("ones1_bf", "c_ones1"),
                           ("hsel", "c_hsel"), ("zsel", "c_zsel"),
                           ("bones16", "c_bones16"), ("prefmask", "c_pref"),
                           ("onesmask", "c_ones8"), ("ecap0", "c_ecap0"),
                           ("ecap1", "c_ecap1")]:
                self.dma(sync, t[dst][:], t[n], "sQ")


        @block.gpsimd
        def _(g):
            self.dma(g, t["x_dram"], t["x"], "sG")
            self.dma(g, t["woutbf_dram"], t["W_outT"], "sG")
            self.dma(g, t["winbf_dram"], t["W_inT"], "sG")
            self.dma(g, t["c_bo"][:], t["bo"], "sG")
            for dtl in range(8):
                self.dma(g, t["c_wg0"][:, dtl * E0:(dtl + 1) * E0],
                         t["wg0"][dtl * 128:(dtl + 1) * 128, :], "sG")
                self.dma(g, t["c_gw1"][:, dtl * E1:(dtl + 1) * E1],
                         t["gate_wT"][dtl * 128:(dtl + 1) * 128, :], "sG")
            ins = g.iota(t["c_iota"][:], [[0, 1]], base=0, channel_multiplier=1)
            self.inc(ins, "eG")

        snap = self.snap()

        @block.scalar
        def _(sc):
            self.wait(sc, snap, only=("sQ",))
            ins = sc.activation(out=t["c_negA"][:], in_=t["tmpA"][:], func=ACT.Exp)
            self.inc(ins, "eS")

        @block.vector
        def _(v):
            v.wait_ge(self.sems["eS"], self.counts["eS"])
            v.wait_ge(self.sems["eG"], self.counts["eG"])
            v.tensor_scalar_mul(t["c_negA"][:], t["c_negA"][:], -1.0)
            v.memset(t["c_eps"][:], 1e-5)
            v.memset(t["c_eps6"][:], 1e-6)
            v.tensor_scalar_add(t["c_trash0"][:], t["c_iota"][:], SLOTS0)
            v.tensor_scalar_add(t["c_trash1"][:], t["c_iota"][:], SLOTS1)
            v.tensor_copy(out=t["c_trash0f"][:], in_=t["c_trash0"][:])
            v.tensor_copy(out=t["c_trash1f"][:], in_=t["c_trash1"][:])
            ins = v.memset(t["tmpZ"][:], 0.0)
            self.inc(ins, "eV")
        z_eV = self.counts["eV"]

        @block.scalar
        def _(sc):
            sc.wait_ge(self.sems["eV"], z_eV)
            self.dma(sc, t["eo_dram"][SLOTS0:SLOTS0 + TRASH, :], t["tmpZ"][:], "sA")
            for c in range((SLOTS0 + TRASH) // 128):
                self.dma(sc, t["disp_dram"][c * 128:(c + 1) * 128, :],
                         t["tmpZ"][:], "sA")

        @block.gpsimd
        def _(g):
            g.wait_ge(self.sems["eV"], z_eV)
            self.dma(g, t["cc_in"], t["tmpZ"][0:1, 0:64], "sG")

    def ph_ln_park(self):
        t, block = self.t, self.block
        self.sbl("ln_x", (128, 2 * DM), BF16)
        self.sbl("ln_xn", (128, 2 * DM), BF16)
        self.sbl("ln_sq", (128, DM), F32)
        self.sbl("ln_cs", (128, DM), F32)
        self.sbl("ln_cq", (128, DM), F32)
        self.sbl("ln_st", (128, 8), F32)
        ld, xn_v = [], {}
        sG0 = self.counts["sG"]
        st = None
        for c in range(NCHUNK):
            b = c % 2

            @block.sync
            def _(sync, c=c, b=b):
                if c >= 2:
                    sync.wait_ge(self.sems["eV"], xn_v[c - 2])
                ld.append(self.dma(sync, t["ln_x"][:, b * DM:(b + 1) * DM],
                                   t["x_dram"][c * 128:(c + 1) * 128, :], "sQ"))

            @block.vector
            def _(v, c=c, b=b):
                xc = t["ln_x"][:, b * DM:(b + 1) * DM]
                v.wait_ge(self.sems["sQ"], ld[c])
                v.tensor_tensor(out=t["ln_sq"][:], in0=xc, in1=xc, op=ALU.mult)
                v.tensor_tensor_scan(out=t["ln_cs"][:], data0=xc, data1=xc,
                                     initial=0.0, op0=ALU.add, op1=ALU.bypass)
                ins = v.tensor_tensor_scan(out=t["ln_cq"][:], data0=t["ln_sq"][:],
                                           data1=t["ln_sq"][:], initial=0.0,
                                           op0=ALU.add, op1=ALU.bypass)
                self.inc(ins, "eV")
            scan_v = self.counts["eV"]

            @block.scalar
            def _(sc, c=c, scan_v=scan_v):
                st = t["ln_st"]
                sc.wait_ge(self.sems["eV"], scan_v)
                sc.activation(out=st[:, 0:1], in_=t["ln_cs"][:, DM - 1:DM],
                              func=ACT.Identity, scale=1.0 / DM)
                ins = sc.activation(out=st[:, 1:2], in_=t["ln_cq"][:, DM - 1:DM],
                                    func=ACT.Identity, scale=1.0 / DM)
                self.inc(ins, "eS")
            ext_s = self.counts["eS"]

            @block.vector
            def _(v, c=c, ext_s=ext_s):
                st = t["ln_st"]
                v.wait_ge(self.sems["eS"], ext_s)
                ins = v.tensor_scalar(out=st[:, 2:3], in0=st[:, 0:1],
                                      scalar1=st[:, 0:1], scalar2=st[:, 1:2],
                                      op0=ALU.mult, op1=ALU.subtract)
                self.inc(ins, "eV")
            var_v = self.counts["eV"]

            @block.scalar
            def _(sc, c=c, var_v=var_v):
                st = t["ln_st"]
                sc.wait_ge(self.sems["eV"], var_v)
                ins = sc.activation(out=st[:, 5:6], in_=st[:, 2:3], func=ACT.Sqrt,
                                    scale=-1.0, bias=t["c_eps"][:, 0:1])
                self.inc(ins, "eS")
            sig_s = self.counts["eS"]

            @block.vector
            def _(v, c=c, sig_s=sig_s):
                st = t["ln_st"]
                v.wait_ge(self.sems["eS"], sig_s)
                ins = v.reciprocal(st[:, 6:7], st[:, 5:6])
                self.inc(ins, "eV")
            rec_v = self.counts["eV"]

            @block.scalar
            def _(sc, c=c, rec_v=rec_v):
                st = t["ln_st"]
                sc.wait_ge(self.sems["eV"], rec_v)
                ins = sc.activation(out=st[:, 7:8], in_=st[:, 6:7],
                                    func=ACT.Identity)
                self.inc(ins, "eS")
            rcp_s = self.counts["eS"]

            @block.vector
            def _(v, c=c, b=b, rcp_s=rcp_s):
                st = t["ln_st"]
                xc = t["ln_x"][:, b * DM:(b + 1) * DM]
                xnc = t["ln_xn"][:, b * DM:(b + 1) * DM]
                v.wait_ge(self.sems["eS"], rcp_s)
                if c >= 2:
                    v.wait_ge(self.sems["sG"], sG0 + 16 * (c - 1))
                ins = v.tensor_scalar(out=xnc, in0=xc, scalar1=st[:, 0:1],
                                      scalar2=st[:, 7:8], op0=ALU.subtract,
                                      op1=ALU.mult)
                xn_v[c] = self.inc(ins, "eV")

            @block.gpsimd
            def _(g, c=c, b=b):
                g.wait_ge(self.sems["eV"], xn_v[c])
                self.dma(g, t["xn_dram"][c * 128:(c + 1) * 128, :],
                         t["ln_xn"][:, b * DM:(b + 1) * DM], "sG")

    def ph_final_wait(self):
        block = self.block
        snap = self.snap()

        @block.sync
        def _(sync):
            self.wait(sync, snap)

    def ph_mamba(self):
        t, block = self.t, self.block
        self.sbl("m_xnf", (128, 2048), BF16)
        self.sbl("m_win", (128, 4096), BF16)
        self.sbl("m_xi", (128, 16 * 260), BF16)
        self.sbl("m_zs", (128, 4096), BF16)
        self.sbl("m_B", (128, 4096), BF16)
        self.sbl("m_C", (128, 4096), BF16)
        self.sbl("m_dtp", (32, 256), F32)
        self.sbl("m_dtt", (32, 256), F32)
        self.sbl("m_dts", (32, 256), F32)
        self.sbl("m_convt", (128, 256), F32)
        self.sbl("m_Xc", (128, 4096), BF16)
        self.sbl("m_S", (128, 4096), F32)
        self.sbl("m_SB", (128, 4096), F32)
        self.sbl("m_Bt", (128, 4096), BF16)
        self.sbl("m_ccC", (128, 2048), BF16)
        self.sbl("m_c128", (128, 4096), BF16)
        self.sbl("m_b128", (128, 4096), BF16)
        self.sbl("m_dcol", (128, 32), F32)
        self.sbl("m_sc", (128, 8192), BF16)
        self.sbl("m_Xtm", (128, 2048), BF16)
        self.sbl("m_Btm", (128, 2048), BF16)
        self.sbl("m_yg", (128, 4096), BF16)
        self.sbl("m_h", (128, 1024), F32)
        self.sbl("m_hbf", (128, 1024), BF16)
        self.sbl("m_stm", (128, 2048), BF16)
        self.sbl("m_wo2", (128, 4096), BF16)
        self.sbl("m_tmp", (128, 512), BF16)
        for w in list(LEVELS) + ["d"]:
            self.sbl(f"m_mb{w}", (128, 512), BF16)
        pb = [self.psl(f"m_pb{i}") for i in range(8)]
        self.tail_v = {}

        @block.sync
        def _(sync):
            for w in list(LEVELS) + ["d"]:
                self.dma(sync, t[f"m_mb{w}"][:], t[f"mbank{w}"], "sQ")

        @block.vector
        def _(v):
            for tl in range(16):
                v.memset(t["m_xi"][:, tl * 260: tl * 260 + 4], 0.0)
            v.memset(t["m_h"][:], 0.0)
            ins = v.memset(t["m_hbf"][:], 0.0)
            self.inc(ins, "eV")

        for seg in range(NSEG):
            self._mamba_segment(seg, pb)

    def _mamba_segment(self, seg, pb):
        t, block = self.t, self.block
        t0 = seg * TSEG
        FG = [(fg * 512, 512) for fg in range(16)] + [(8192, 32)]
        NFG = len(FG)

        seg_snap = {k: self.counts[k] for k in ("eV", "eS", "eT")}
        ld_xnf = [None]
        ld_win = {}
        mm_done = {}
        evac_at = {}

        def emit_win_load(fg):
            flo, width = FG[fg]

            @block.sync
            def _(sync):
                if fg < 1:
                    self.wait(sync, seg_snap)
                if fg >= 1:
                    sync.wait_ge(self.sems["eT"], mm_done[("fg", fg - 1)])
                for dtl in range(8):
                    ld_win[(fg, dtl)] = self.dma(
                        sync, t["m_win"][:, dtl * 512: dtl * 512 + width],
                        t["winbf_dram"][dtl * 128:(dtl + 1) * 128,
                                        flo:flo + width], "sQ")

        @block.sync
        def _(sync):
            self.wait(sync, seg_snap)
            for dtl in range(8):
                ld_xnf.append(self.trdma(
                    sync, t["m_xnf"][:, dtl * 256:(dtl + 1) * 256],
                    t["xn_dram"][t0:t0 + TSEG, dtl * 128:(dtl + 1) * 128], "sQ"))

        emit_win_load(0)
        fcg = 0
        for fg in range(NFG):
            flo, width = FG[fg]
            par = fg % 2
            nsub = max(1, width // 128)

            @block.tensor
            def _(tn, fg=fg, width=width, nsub=nsub, fcg0=fcg):
                if fg == 0:
                    self.wait(tn, seg_snap)
                tn.wait_ge(self.sems["sQ"], ld_xnf[-1])
                tn.wait_ge(self.sems["sQ"], ld_win[(fg, 7)])
                for sub in range(nsub):
                    wq = min(width, 128)
                    bank = pb[(fcg0 + sub) % 4]
                    if fcg0 + sub >= 4:
                        sem, cnt = evac_at[fcg0 + sub - 4]
                        tn.wait_ge(self.sems[sem], cnt)
                    for dtl in range(8):
                        ins = tn.matmul(
                            bank[0:wq, 0:TSEG],
                            t["m_win"][:, dtl * 512 + sub * 128:
                                       dtl * 512 + sub * 128 + wq],
                            t["m_xnf"][:, dtl * 256:(dtl + 1) * 256],
                            start=(dtl == 0), stop=(dtl == 7))
                    mm_done[("fc", fcg0 + sub)] = self.inc(ins, "eT")
                mm_done[("fg", fg)] = self.counts["eT"]

            for sub in range(nsub):
                myfcg = fcg + sub
                bank = pb[myfcg % 4]
                if 16 <= myfcg < 32:
                    @block.scalar
                    def _(sc, myfcg=myfcg, bank=bank):
                        tl = myfcg - 16
                        sc.wait_ge(self.sems["eT"], mm_done[("fc", myfcg)])
                        ins = sc.activation(
                            out=t["m_zs"][:, tl * 256:(tl + 1) * 256],
                            in_=bank[:, 0:TSEG], func=ACT.Silu)
                        evac_at[myfcg] = ("eS", self.inc(ins, "eS"))
                else:
                    @block.vector
                    def _(v, myfcg=myfcg, bank=bank):
                        v.wait_ge(self.sems["eT"], mm_done[("fc", myfcg)])
                        if myfcg < 16:
                            dst = t["m_xi"][:, myfcg * 260 + 4: myfcg * 260 + 260]
                            ins = v.tensor_copy(out=dst, in_=bank[:, 0:TSEG])
                        elif myfcg < 48:
                            tl = myfcg - 32
                            ins = v.tensor_copy(
                                out=t["m_B"][:, tl * 256:(tl + 1) * 256],
                                in_=bank[:, 0:TSEG])
                        elif myfcg < 64:
                            tl = myfcg - 48
                            ins = v.tensor_copy(
                                out=t["m_C"][:, tl * 256:(tl + 1) * 256],
                                in_=bank[:, 0:TSEG])
                        else:
                            ins = v.tensor_copy(out=t["m_dtp"][:],
                                                in_=bank[0:32, 0:TSEG])
                        evac_at[myfcg] = ("eV", self.inc(ins, "eV"))
            fcg += nsub
            if fg + 1 < NFG:
                emit_win_load(fg + 1)

        @block.tensor
        def _(tn):
            for dtl in range(8):
                ins = tn.matmul(pb[4][0:E0, 0:TSEG],
                                t["c_wg0"][:, dtl * E0:(dtl + 1) * E0],
                                t["m_xnf"][:, dtl * 256:(dtl + 1) * 256],
                                start=(dtl == 0), stop=(dtl == 7))
            mm_done[("lg",)] = self.inc(ins, "eT")
            mm_done[("mseg", seg)] = self.counts["eT"]

        @block.vector
        def _(v):
            v.wait_ge(self.sems["eT"], mm_done[("lg",)])
            ins = v.tensor_copy(out=t["lg_fm"][0:E0, t0:t0 + TSEG],
                                in_=pb[4][0:E0, 0:TSEG])
            self.inc(ins, "eV")

        @block.scalar
        def _(sc):
            sem, cnt = evac_at[64]
            sc.wait_ge(self.sems[sem], cnt)
            sc.activation(out=t["m_dtt"][:], in_=t["m_dtp"][:], func=ACT.Abs,
                          bias=t["c_dtb"][:, 0:1])
            sc.activation(out=t["m_dtt"][:], in_=t["m_dtt"][:], func=ACT.Exp,
                          scale=-1.0)
            ins = sc.activation(out=t["m_dtt"][:], in_=t["m_dtt"][:], func=ACT.Ln,
                                bias=1.0)
            self.inc(ins, "eS")
        dt_eS = self.counts["eS"]

        @block.vector
        def _(v):
            v.wait_ge(self.sems["eS"], dt_eS)
            v.tensor_scalar(out=t["m_dts"][:], in0=t["m_dtp"][:],
                            scalar1=t["c_dtb"][:, 0:1], scalar2=0.0,
                            op0=ALU.add, op1=ALU.max)
            ins = v.tensor_add(t["m_dts"][:], t["m_dts"][:], t["m_dtt"][:])
            self.inc(ins, "eV")
        dt_eV = self.counts["eV"]

        conv_v = []
        for tl in range(16):
            @block.vector
            def _(v, tl=tl):
                if tl == 0:
                    sem, cnt = evac_at[15]
                    v.wait_ge(self.sems[sem], cnt)
                base = tl * 260

                def sl(k):
                    return t["m_xi"][:, base + 1 + k: base + 1 + k + TSEG]
                dst = t["m_convt"][:]
                v.tensor_scalar(out=dst, in0=sl(0),
                                scalar1=t["c_convw"][:, 4 * tl: 4 * tl + 1],
                                scalar2=None, op0=ALU.mult)
                for k in (1, 2, 3):
                    v.scalar_tensor_tensor(
                        out=dst, in0=sl(k),
                        scalar=t["c_convw"][:, 4 * tl + k: 4 * tl + k + 1],
                        in1=dst, op0=ALU.mult, op1=ALU.add)
                ins = v.tensor_copy(out=t["m_xi"][:, base + 1: base + 4],
                                    in_=t["m_xi"][:, base + 257: base + 260])
                conv_v.append(self.inc(ins, "eV"))

            @block.scalar
            def _(sc, tl=tl):
                sc.wait_ge(self.sems["eV"], conv_v[tl])
                ins = sc.activation(out=t["m_Xc"][:, tl * 256:(tl + 1) * 256],
                                    in_=t["m_convt"][:], func=ACT.Silu,
                                    bias=t["c_convb"][:, tl:tl + 1])
                self.inc(ins, "eS")
            cv_cnt = self.counts["eS"]

            @block.vector
            def _(v, cv_cnt=cv_cnt):
                v.wait_ge(self.sems["eS"], cv_cnt)
        xc_eS = self.counts["eS"]

        dtrep_v = []
        for tl in range(16):
            @block.tensor
            def _(tn, tl=tl):
                bank = pb[5 + (tl % 2)]
                if tl == 0:
                    tn.wait_ge(self.sems["eV"], dt_eV)
                if tl >= 2:
                    tn.wait_ge(self.sems["eV"], dtrep_v[tl - 2])
                ins = tn.matmul(bank[:, 0:TSEG],
                                t["c_hsel"][:, tl * 128:(tl + 1) * 128],
                                t["m_dts"][:], start=True, stop=True)
                self.inc(ins, "eT")
            mmc = self.counts["eT"]

            @block.vector
            def _(v, tl=tl, mmc=mmc):
                bank = pb[5 + (tl % 2)]
                v.wait_ge(self.sems["eT"], mmc)
                v.tensor_scalar(out=t["m_S"][:, tl * 256:(tl + 1) * 256],
                                in0=bank[:, 0:TSEG],
                                scalar1=t["c_negA"][:, tl:tl + 1],
                                scalar2=None, op0=ALU.mult)
                ins = v.tensor_tensor(out=t["m_B"][:, tl * 256:(tl + 1) * 256],
                                      in0=t["m_B"][:, tl * 256:(tl + 1) * 256],
                                      in1=bank[:, 0:TSEG], op=ALU.mult)
                dtrep_v.append(self.inc(ins, "eV"))

        @block.vector
        def _(v):
            v.wait_ge(self.sems["eT"], self.counts["eT"])
            v.memset(t["m_SB"][:], 0.0)
            v.memset(t["m_sc"][:], 0.0)
            ins = v.tensor_copy(out=t["m_Bt"][:], in_=t["m_B"][:])
            self.inc(ins, "eV")
        chain_eV = self.counts["eV"]

        p0_mm = {}
        for c in range(2):
            for tl in range(16):
                @block.tensor
                def _(tn, c=c, tl=tl):
                    if c == 0 and tl == 0:
                        tn.wait_ge(self.sems["eV"], chain_eV)
                        tn.wait_ge(self.sems["eS"], xc_eS)
                    if c == 1 and tl == 0:
                        tn.wait_ge(self.sems["eV"], self.counts["eV"])
                    for a in range(2):
                        q = tl * 2 + a
                        bank = pb[q % 8]
                        slot = q // 8
                        ins = tn.matmul(
                            bank[:, slot * 128:(slot + 1) * 128],
                            t["m_B"][64 * a:64 * a + 64,
                                     tl * 256 + c * 128: tl * 256 + c * 128 + 128],
                            t["m_C"][64 * a:64 * a + 64,
                                     tl * 256 + c * 128: tl * 256 + c * 128 + 128],
                            start=True, stop=True)
                        p0_mm[(c, q)] = self.inc(ins, "eT")
            for j in range(8):
                @block.vector
                def _(v, c=c, j=j):
                    v.wait_ge(self.sems["eT"], p0_mm[(c, 8 * 3 + j)])
                    bank = pb[j]
                    v.tensor_tensor(out=t["m_tmp"][:], in0=bank[:],
                                    in1=t["m_mbd"][:], op=ALU.mult)
                    dstv = t["m_sc"][:].rearrange("p (s j t) -> p s j t", j=8, t=256)
                    dstv = dstv[:, :, j, c * 128:(c + 1) * 128]
                    ins = v.tensor_tensor(out=dstv, in0=dstv, in1=t["m_tmp"][:],
                                          op=ALU.add)
                    self.inc(ins, "eV")
        p0_eV = self.counts["eV"]

        self._mamba_levels(seg, pb, p0_eV)

        self._mamba_tail(seg, pb)


    def _mamba_levels(self, seg, pb, p0_eV):
        t, block = self.t, self.block

        def parity_view(buf, w, par):
            u = 4096 // (2 * w)
            return buf[:].rearrange("p (u two w) -> p u two w", two=2, w=w)[:, :, par, :]

        def endcol_view(w, par):
            u = 4096 // (2 * w)
            v = t["m_S"][:].rearrange("p (u two w) -> p u two w", two=2, w=w)
            return v[:, :, par, w - 1:w].broadcast_to((128, u, w))

        prev_sel_eV = p0_eV
        for li, w in enumerate(LEVELS):
            u = 4096 // (2 * w)
            @block.scalar
            def _(sc, w=w, u=u):
                sc.wait_ge(self.sems["eV"], self.counts["eV"])
                sc.wait_ge(self.sems["eT"], self.counts["eT"])
                ins = sc.activation(
                    out=t["m_ccC"][:].rearrange("p (u w) -> p u w", w=w),
                    in_=parity_view(t["m_S"], w, 1), func=ACT.Exp)
                self.inc(ins, "eS")
            ccC_eS = self.counts["eS"]

            @block.vector
            def _(v, w=w, u=u, ccC_eS=ccC_eS):
                v.wait_ge(self.sems["eS"], ccC_eS)
                ins = v.tensor_tensor(
                    out=t["m_ccC"][:].rearrange("p (u w) -> p u w", w=w),
                    in0=parity_view(t["m_C"], w, 1),
                    in1=t["m_ccC"][:].rearrange("p (u w) -> p u w", w=w),
                    op=ALU.mult)
                self.inc(ins, "eV")

            if w > 1:
                wv = w // 2
                @block.scalar
                def _(sc, wv=wv):
                    sc.wait_ge(self.sems["eV"], self.counts["eV"])
                    sc.wait_ge(self.sems["eT"], self.counts["eT"])
                    ins = sc.activation(
                        out=parity_view(t["m_Bt"], wv, 0),
                        in_=parity_view(t["m_SB"], wv, 0), func=ACT.Exp)
                    self.inc(ins, "eS")
                ebt_eS = self.counts["eS"]

                @block.vector
                def _(v, wv=wv, ebt_eS=ebt_eS):
                    v.wait_ge(self.sems["eS"], ebt_eS)
                    ins = v.tensor_tensor(
                        out=parity_view(t["m_Bt"], wv, 0),
                        in0=parity_view(t["m_B"], wv, 0),
                        in1=parity_view(t["m_Bt"], wv, 0),
                        op=ALU.mult)
                    self.inc(ins, "eV")
            bt_eV = self.counts["eV"]

            @block.vector
            def _(v, w=w, bt_eV=bt_eV):
                v.wait_ge(self.sems["eV"], bt_eV)
                v.tensor_tensor(out=parity_view(t["m_SB"], w, 0),
                                in0=parity_view(t["m_SB"], w, 0),
                                in1=endcol_view(w, 1), op=ALU.add)
                ins = v.tensor_tensor(out=parity_view(t["m_S"], w, 1),
                                      in0=parity_view(t["m_S"], w, 1),
                                      in1=endcol_view(w, 0), op=ALU.add)
                self.inc(ins, "eV")

            for c in range(2):
                lvl_mm = {}
                for tl in range(16):
                    @block.tensor
                    def _(tn, c=c, tl=tl, w=w, li=li):
                        if tl == 0:
                            tn.wait_ge(self.sems["eV"], prev_sel_eV)
                            tn.wait_ge(self.sems["eV"], self.counts["eV"])
                        for a in range(2):
                            q = tl * 2 + a
                            bank = pb[4 * c + q % 4]
                            r = q // 4
                            ins = tn.matmul(
                                bank[:, r * 64:(r + 1) * 64],
                                t["m_Bt"][64 * a:64 * a + 64,
                                          tl * 256 + c * 128: tl * 256 + c * 128 + 128],
                                t["m_ccC"][64 * a:64 * a + 64,
                                           tl * 128 + c * 64: tl * 128 + c * 64 + 64],
                                start=True, stop=True)
                            lvl_mm[(c, q)] = self.inc(ins, "eT")
                for j in range(4):
                    @block.vector
                    def _(v, c=c, j=j, w=w):
                        v.wait_ge(self.sems["eT"], lvl_mm[(c, 28 + j)])
                        bank = pb[4 * c + j]
                        v.tensor_tensor(out=t["m_tmp"][:], in0=bank[:],
                                        in1=t[f"m_mb{w}"][:], op=ALU.mult)
                        dstv = t["m_sc"][:].rearrange("p (s j t) -> p s j t",
                                                      j=4, t=256)
                        dstv = dstv[:, :, j, c * 128:(c + 1) * 128]
                        dstv = dstv.rearrange("p s (nb two w) -> p s nb two w",
                                              two=2, w=w)[:, :, :, 1, :]
                        ins = v.tensor_tensor(out=dstv, in0=dstv,
                                              in1=t["m_tmp"][:], op=ALU.add)
                        self.inc(ins, "eV")
            prev_sel_eV = self.counts["eV"]

    def _mamba_tail(self, seg, pb):
        t, block = self.t, self.block
        t0 = seg * TSEG

        @block.scalar
        def _(sc):
            sc.wait_ge(self.sems["eV"], self.counts["eV"])
            sc.wait_ge(self.sems["eT"], self.counts["eT"])
            ins = sc.activation(out=t["m_c128"][:], in_=t["m_S"][:], func=ACT.Exp)
            self.inc(ins, "eS")
        c128_eS = self.counts["eS"]

        @block.vector
        def _(v):
            v.wait_ge(self.sems["eS"], c128_eS)
            for tl in range(16):
                for c in range(2):
                    v.tensor_copy(
                        out=t["m_dcol"][:, tl * 2 + c: tl * 2 + c + 1],
                        in_=t["m_c128"][:, tl * 256 + c * 128 + 127:
                                        tl * 256 + c * 128 + 128])
            ins = v.tensor_tensor(out=t["m_c128"][:], in0=t["m_C"][:],
                                  in1=t["m_c128"][:], op=ALU.mult)
            self.inc(ins, "eV")

        @block.scalar
        def _(sc):
            sc.wait_ge(self.sems["eV"], self.counts["eV"])
            sc.wait_ge(self.sems["eT"], self.counts["eT"])
            ins = sc.activation(out=t["m_b128"][:], in_=t["m_SB"][:], func=ACT.Exp)
            self.inc(ins, "eS")
        b128_eS = self.counts["eS"]

        @block.vector
        def _(v):
            v.wait_ge(self.sems["eS"], b128_eS)
            ins = v.tensor_tensor(out=t["m_b128"][:], in0=t["m_B"][:],
                                  in1=t["m_b128"][:], op=ALU.mult)
            self.inc(ins, "eV")
        b128_eV = self.counts["eV"]

        @block.gpsimd
        def _(g):
            g.wait_ge(self.sems["eV"], b128_eV)
            self.dma(g, t["parkX_dram"].rearrange("(tl p) t -> p tl t", p=128),
                     t["m_Xc"][:].rearrange("p (tl t) -> p tl t", t=256), "sG")
            self.dma(g, t["parkB_dram"].rearrange("(tl p) t -> p tl t", p=128),
                     t["m_b128"][:].rearrange("p (tl t) -> p tl t", t=256), "sG")
        park_sG = self.counts["sG"]

        for c in range(2):
            @block.sync
            def _(sync, c=c):
                sync.wait_ge(self.sems["sG"], park_sG)
                if c == 1:
                    sync.wait_ge(self.sems["eT"], self.counts["eT"])
                self.trdma(sync, t["m_Xtm"][:],
                           t["parkX_dram"][:, c * 128:(c + 1) * 128], "sQ")
                self.trdma(sync, t["m_Btm"][:],
                           t["parkB_dram"][:, c * 128:(c + 1) * 128], "sQ")
            tm_sQ = self.counts["sQ"]

            for tl in range(16):
                @block.tensor
                def _(tn, c=c, tl=tl, tm_sQ=tm_sQ):
                    pbF, pbY, pbS = pb[(tl % 2) * 3], pb[(tl % 2) * 3 + 1], pb[(tl % 2) * 3 + 2]
                    if tl == 0:
                        tn.wait_ge(self.sems["sQ"], tm_sQ)
                        tn.wait_ge(self.sems["eV"], self.counts["eV"])
                    if tl >= 2:
                        tn.wait_ge(self.sems["eV"], self.tail_v[(c, tl - 2)])
                    for a in range(2):
                        tn.matmul(pbF[64 * a:64 * a + 64, 0:128],
                                  t["m_hbf"][64 * a:64 * a + 64, tl * 64:(tl + 1) * 64],
                                  t["m_c128"][64 * a:64 * a + 64,
                                              tl * 256 + c * 128: tl * 256 + c * 128 + 128],
                                  start=True, stop=True)
                        tn.matmul(pbY[64 * a:64 * a + 64, 0:128],
                                  t["m_Xtm"][:, tl * 128 + 64 * a: tl * 128 + 64 * a + 64],
                                  t["m_sc"][:, (tl * 2 + a) * 256 + c * 128:
                                            (tl * 2 + a) * 256 + c * 128 + 128],
                                  start=True, stop=True)
                        ins = tn.matmul(pbS[64 * a:64 * a + 64, 0:64],
                                        t["m_Btm"][:, tl * 128 + 64 * a: tl * 128 + 64 * a + 64],
                                        t["m_Xtm"][:, tl * 128 + 64 * a: tl * 128 + 64 * a + 64],
                                        start=True, stop=True)
                    self.inc(ins, "eT")
                mm_cnt = self.counts["eT"]

                @block.vector
                def _(v, c=c, tl=tl, mm_cnt=mm_cnt):
                    pbF, pbY, pbS = pb[(tl % 2) * 3], pb[(tl % 2) * 3 + 1], pb[(tl % 2) * 3 + 2]
                    v.wait_ge(self.sems["eT"], mm_cnt)
                    ygs = t["m_yg"][:, tl * 256 + c * 128: tl * 256 + c * 128 + 128]
                    v.scalar_tensor_tensor(out=ygs,
                                           in0=t["m_Xc"][:, tl * 256 + c * 128:
                                                         tl * 256 + c * 128 + 128],
                                           scalar=t["c_D"][:, tl:tl + 1],
                                           in1=pbF[:, 0:128],
                                           op0=ALU.mult, op1=ALU.add)
                    v.tensor_tensor(out=ygs, in0=ygs, in1=pbY[:, 0:128], op=ALU.add)
                    v.tensor_tensor(out=ygs, in0=ygs,
                                    in1=t["m_zs"][:, tl * 256 + c * 128:
                                                  tl * 256 + c * 128 + 128],
                                    op=ALU.mult)
                    v.scalar_tensor_tensor(out=t["m_h"][:, tl * 64:(tl + 1) * 64],
                                           in0=t["m_h"][:, tl * 64:(tl + 1) * 64],
                                           scalar=t["m_dcol"][:, tl * 2 + c: tl * 2 + c + 1],
                                           in1=pbS[:, 0:64],
                                           op0=ALU.mult, op1=ALU.add)
                    ins = v.tensor_copy(out=t["m_hbf"][:, tl * 64:(tl + 1) * 64],
                                        in_=t["m_h"][:, tl * 64:(tl + 1) * 64])
                    self.tail_v[(c, tl)] = self.inc(ins, "eV")

        for dq in range(4):
            @block.gpsimd
            def _(g, dq=dq):
                g.wait_ge(self.sems["eT"], self.counts["eT"])
                for ft in range(16):
                    self.dma(g, t["m_wo2"][:, ft * 256:(ft + 1) * 256],
                             t["woutbf_dram"][ft * 128:(ft + 1) * 128,
                                              dq * 256:(dq + 1) * 256], "sG")
            wo_sG = self.counts["sG"]
            for c in range(2):
                @block.tensor
                def _(tn, dq=dq, c=c, wo_sG=wo_sG):
                    tn.wait_ge(self.sems["sG"], wo_sG)
                    if dq == 0 and c == 0:
                        tn.wait_ge(self.sems["eV"], self.tail_v[(1, 15)])
                    if (dq, c) != (0, 0):
                        tn.wait_ge(self.sems["eV"], self.counts["eV"])
                    bank = pb[6 + c]
                    for tl in range(16):
                        ins = tn.matmul(bank[:, 0:256],
                                        t["m_yg"][:, tl * 256 + c * 128:
                                                  tl * 256 + c * 128 + 128],
                                        t["m_wo2"][:, tl * 256:(tl + 1) * 256],
                                        start=(tl == 0), stop=(tl == 15))
                    self.inc(ins, "eT")
                op_eT = self.counts["eT"]

                @block.vector
                def _(v, dq=dq, c=c, op_eT=op_eT):
                    v.wait_ge(self.sems["eT"], op_eT)
                    ins = v.tensor_copy(
                        out=t["m_stm"][:, c * 1024 + dq * 256:
                                       c * 1024 + dq * 256 + 256],
                        in_=pb[6 + c][:, 0:256])
                    self.inc(ins, "eV")

        @block.scalar
        def _(sc):
            sc.wait_ge(self.sems["eV"], self.counts["eV"])
            for c in range(2):
                self.dma(sc, t["s_dram"][t0 + c * 128: t0 + (c + 1) * 128, :],
                         t["m_stm"][:, c * 1024:(c + 1) * 1024], "sA")

    def ph_attn(self, upto="full", nseg=NSEG, plainfeat=False, skip=()):
        t, block = self.t, self.block
        for n in ("aw_q", "aw_k", "aw_v", "aw_o"):
            self.sbl(n, (128, 8 * DM), BF16)
        self.sbl("a_dd", (128, 128), BF16)
        self.sbl("a_ud", (128, 128), BF16)
        self.sbl("a_xn", (128, 8 * TSEG), BF16)
        self.sbl("a_q", (128, 8 * TSEG), BF16)
        self.sbl("a_k", (128, 8 * TSEG), BF16)
        self.sbl("a_t1", (128, TSEG), F32)
        self.sbl("a_t2", (128, TSEG), F32)
        self.sbl("a_t3", (128, TSEG), F32)
        self.sbl("a_z1", (128, TSEG), F32)
        self.sbl("a_z2", (128, TSEG), F32)
        self.sbl("a_cum", (128, 8 * TSEG), F32)
        self.sbl("a_zsb", (16, TSEG), F32)
        self.sbl("a_zrep", (128, 8 * TSEG), F32)
        self.sbl("a_ktm", (128, 2 * DM), BF16)
        self.sbl("a_vtm", (128, 2 * DM), BF16)
        self.sbl("a_sc", (128, 16 * 128), BF16)
        self.sbl("a_y", (128, 2 * DM), BF16)
        self.sbl("a_S", (128, 8 * 64), F32)
        self.sbl("a_Sbf", (128, 8 * 128), BF16)
        self.sbl("a_s", (128, DM), BF16)
        pb = [self.psl(f"a_pb{i}") for i in range(8)]
        wmap = {"aw_q": "Wq", "aw_k": "Wk", "aw_v": "Wv", "aw_o": "Wo"}

        @block.gpsimd
        def _(g):
            for dst, src in wmap.items():
                for kt in range(8):
                    self.dma(g, t[dst][:, kt * DM:(kt + 1) * DM],
                             t[src][kt * 128:(kt + 1) * 128, :], "sG")

        @block.sync
        def _(sync):
            self.dma(sync, t["a_dd"][:], t["dd_bf"], "sQ")
            self.dma(sync, t["a_ud"][:], t["ud_bf"], "sQ")

        @block.vector
        def _(v):
            v.memset(t["a_S"][:], 0.0)
            ins = v.memset(t["a_Sbf"][:], 0.0)
            self.inc(ins, "eV")

        sbf = {tt: 0 for tt in range(8)}
        for seg in range(nseg):
            t0 = seg * TSEG
            @block.sync
            def _(sync, seg=seg):
                sync.wait_ge(self.sems["eT"], self.counts["eT"])
                for dtl in range(8):
                    self.trdma(
                        sync, t["a_xn"][:, dtl * TSEG:(dtl + 1) * TSEG],
                        t["xn_dram"][t0:t0 + TSEG, dtl * 128:(dtl + 1) * 128], "sQ")
            xn_sQ = self.counts["sQ"]

            fz = {}
            for st, (wsb, dst, b0) in enumerate(
                    [("aw_q", "a_q", 0), ("aw_k", "a_k", 2)]):
                for m in range(8):
                    @block.tensor
                    def _(tn, st=st, m=m, wsb=wsb, b0=b0, seg=seg, xn_sQ=xn_sQ):
                        bank = pb[b0 + m % 2]
                        if st == 0 and m == 0:
                            tn.wait_ge(self.sems["sQ"], xn_sQ)
                            tn.wait_ge(self.sems["eV"], self.counts["eV"])
                            if seg == 0:
                                tn.wait_ge(self.sems["sG"], self.counts["sG"])
                        if (st, m) >= (0, 2):
                            key = (st, m - 2) if m >= 2 else (st - 1, 6 + m)
                            tn.wait_ge(self.sems["eV"], fz[key][2])
                        for kt in range(8):
                            ins = tn.matmul(
                                bank[:, 0:TSEG],
                                t[wsb][:, kt * DM + m * 128: kt * DM + m * 128 + 128],
                                t["a_xn"][:, kt * TSEG:(kt + 1) * TSEG],
                                start=(kt == 0), stop=(kt == 7))
                        self.inc(ins, "eT")
                    mmc = self.counts["eT"]

                    @block.vector
                    def _(v, st=st, m=m, b0=b0, mmc=mmc):
                        bank = pb[b0 + m % 2]
                        v.wait_ge(self.sems["eT"], mmc)
                        if (st, m) > (0, 0):
                            pk = (st, m - 1) if m >= 1 else (st - 1, 7)
                            v.wait_ge(self.sems["eS"], fz[pk][1])
                        v.tensor_single_scalar(out=t["a_t1"][:], in_=bank[:, 0:TSEG],
                                               scalar=0.0, op=ALU.min)
                        ins = v.tensor_single_scalar(out=t["a_t3"][:], in_=bank[:, 0:TSEG],
                                                     scalar=0.0, op=ALU.max)
                        c1 = self.inc(ins, "eV")
                        fz.setdefault((st, m), [c1, 0, 0])[0] = c1

                    @block.scalar
                    def _(sc, st=st, m=m):
                        sc.wait_ge(self.sems["eV"], fz[(st, m)][0])
                        ins = sc.activation(out=t["a_t2"][:], in_=t["a_t1"][:],
                                            func=ACT.Exp)
                        fz[(st, m)][1] = self.inc(ins, "eS")

                    @block.vector
                    def _(v, st=st, m=m, b0=b0, dst=dst, plainfeat=plainfeat):
                        bank = pb[b0 + m % 2]
                        v.wait_ge(self.sems["eS"], fz[(st, m)][1])
                        if plainfeat:
                            ins = v.tensor_copy(
                                out=t[dst][:, m * TSEG:(m + 1) * TSEG],
                                in_=bank[:, 0:TSEG])
                        else:
                            ins = v.tensor_tensor(
                                out=t[dst][:, m * TSEG:(m + 1) * TSEG],
                                in0=t["a_t3"][:], in1=t["a_t2"][:], op=ALU.add)
                        fz[(st, m)][2] = self.inc(ins, "eV")
            feat_eV = self.counts["eV"]

            v_v = {}
            for c in range(2):
                for dh in range(2):
                    @block.tensor
                    def _(tn, c=c, dh=dh, seg=seg, xn_sQ=xn_sQ):
                        bank = pb[4 + c * 2 + dh]
                        tn.wait_ge(self.sems["sQ"], xn_sQ)
                        tn.wait_ge(self.sems["eV"], self.counts["eV"])
                        for kt in range(8):
                            ins = tn.matmul(
                                bank[:, 0:512],
                                t["a_xn"][:, kt * TSEG + c * 128: kt * TSEG + c * 128 + 128],
                                t["aw_v"][:, kt * DM + dh * 512: kt * DM + dh * 512 + 512],
                                start=(kt == 0), stop=(kt == 7))
                        self.inc(ins, "eT")
                    mmc = self.counts["eT"]

                    @block.vector
                    def _(v, c=c, dh=dh, mmc=mmc):
                        bank = pb[4 + c * 2 + dh]
                        v.wait_ge(self.sems["eT"], mmc)
                        ins = v.tensor_copy(
                            out=t["a_vtm"][:, c * DM + dh * 512: c * DM + dh * 512 + 512],
                            in_=bank[:, 0:512])
                        v_v[(c, dh)] = self.inc(ins, "eV")

            @block.tensor
            def _(tn, seg=seg):
                bank = pb[1]
                tn.wait_ge(self.sems["eV"], feat_eV)
                for kt in range(8):
                    ins = tn.matmul(bank[0:16, 0:TSEG],
                                    t["c_gw1"][:, kt * E1:(kt + 1) * E1],
                                    t["a_xn"][:, kt * TSEG:(kt + 1) * TSEG],
                                    start=(kt == 0), stop=(kt == 7))
                self.inc(ins, "eT")
            lg_eT = self.counts["eT"]

            @block.vector
            def _(v, seg=seg, lg_eT=lg_eT):
                v.wait_ge(self.sems["eT"], lg_eT)
                ins = v.tensor_copy(out=t["lg_fm"][0:16, t0:t0 + TSEG],
                                    in_=pb[1][0:16, 0:TSEG])
                self.inc(ins, "eV")
            lgv_eV = self.counts["eV"]

            if upto == "proj":
                continue
            @block.gpsimd
            def _(g, seg=seg):
                g.wait_ge(self.sems["eV"], feat_eV)
                self.dma(g, t["parkK_dram"].rearrange("(tl p) t -> p tl t", p=128),
                         t["a_k"][:].rearrange("p (tl t) -> p tl t", t=TSEG), "sG")
            park_sG = self.counts["sG"]

            @block.sync
            def _(sync, seg=seg, park_sG=park_sG):
                sync.wait_ge(self.sems["sG"], park_sG)
                sync.wait_ge(self.sems["eT"], self.counts["eT"])
                for c in range(2):
                    self.trdma(sync, t["a_ktm"][:, c * DM:(c + 1) * DM],
                               t["parkK_dram"][:, c * 128:(c + 1) * 128], "sQ")
            ktm_sQ = self.counts["sQ"]

            cum_v = {}
            for tp in range(8):
                @block.tensor
                def _(tn, tp=tp, lgv_eV=lgv_eV):
                    bank = pb[tp % 2]
                    if tp < 2:
                        tn.wait_ge(self.sems["eV"], lgv_eV)
                    else:
                        tn.wait_ge(self.sems["eV"], cum_v[tp - 2])
                    for tt in range(tp + 1):
                        lhs = t["a_dd"] if tt < tp else t["a_ud"]
                        ins = tn.matmul(bank[:, 0:TSEG], lhs[:],
                                        t["a_k"][:, tt * TSEG:(tt + 1) * TSEG],
                                        start=(tt == 0), stop=(tt == tp))
                    self.inc(ins, "eT")
                mmc = self.counts["eT"]

                @block.vector
                def _(v, tp=tp, mmc=mmc):
                    bank = pb[tp % 2]
                    v.wait_ge(self.sems["eT"], mmc)
                    ins = v.tensor_copy(out=t["a_cum"][:, tp * TSEG:(tp + 1) * TSEG],
                                        in_=bank[:, 0:TSEG])
                    cum_v[tp] = self.inc(ins, "eV")

            if upto == "cum":
                continue
            zp_v = {}
            zmm = {}
            for tt in range(8):
                @block.vector
                def _(v, tt=tt):
                    buf = "a_z1" if tt % 2 == 0 else "a_z2"
                    v.wait_ge(self.sems["eV"], cum_v[tt])
                    if tt >= 2:
                        v.wait_ge(self.sems["eT"], zmm[tt - 2])
                    ins = v.tensor_tensor(out=t[buf][:],
                                          in0=t["a_q"][:, tt * TSEG:(tt + 1) * TSEG],
                                          in1=t["a_cum"][:, tt * TSEG:(tt + 1) * TSEG],
                                          op=ALU.mult)
                    zp_v[tt] = self.inc(ins, "eV")

                @block.tensor
                def _(tn, tt=tt):
                    buf = "a_z1" if tt % 2 == 0 else "a_z2"
                    tn.wait_ge(self.sems["eV"], zp_v[tt])
                    ins = tn.matmul(pb[3][0:16, 0:TSEG],
                                    t["c_bones16"][:, tt * 16:(tt + 1) * 16],
                                    t[buf][:], start=(tt == 0), stop=(tt == 7))
                    zmm[tt] = self.inc(ins, "eT")

            @block.vector
            def _(v, seg=seg):
                v.wait_ge(self.sems["eT"], zmm[7])
                v.tensor_copy(out=t["a_zsb"][0:16, :], in_=pb[3][0:16, 0:TSEG])
                v.tensor_scalar_add(t["a_zsb"][0:16, :], t["a_zsb"][0:16, :], 1e-6)
                ins = v.reciprocal(t["a_zsb"][0:16, :], t["a_zsb"][0:16, :])
                self.inc(ins, "eV")
            zsb_eV = self.counts["eV"]

            zr_v = {}
            for tt in range(8):
                @block.tensor
                def _(tn, tt=tt):
                    bank = pb[tt % 2]
                    tn.wait_ge(self.sems["eV"], zsb_eV if tt < 2 else zr_v[tt - 2])
                    ins = tn.matmul(bank[:, 0:TSEG],
                                    t["c_zsel"][:, tt * 128:(tt + 1) * 128],
                                    t["a_zsb"][0:16, :], start=True, stop=True)
                    self.inc(ins, "eT")
                mmc = self.counts["eT"]

                @block.vector
                def _(v, tt=tt, mmc=mmc):
                    bank = pb[tt % 2]
                    v.wait_ge(self.sems["eT"], mmc)
                    ins = v.tensor_copy(out=t["a_zrep"][:, tt * TSEG:(tt + 1) * TSEG],
                                        in_=bank[:, 0:TSEG])
                    zr_v[tt] = self.inc(ins, "eV")

            if upto == "z":
                continue
            for c in range(2):
                sc_v, y_v = {}, {}
                for tt in range(8):
                    qs = t["a_q"][:, tt * TSEG + c * 128: tt * TSEG + c * 128 + 128]
                    ks = t["a_k"][:, tt * TSEG + c * 128: tt * TSEG + c * 128 + 128]

                    @block.tensor
                    def _(tn, c=c, tt=tt, qs=qs, ks=ks, ktm_sQ=ktm_sQ):
                        pbP, pbQ = pb[(tt % 2) * 2], pb[(tt % 2) * 2 + 1]
                        if tt < 2:
                            tn.wait_ge(self.sems["sQ"], ktm_sQ)
                            tn.wait_ge(self.sems["eV"], self.counts["eV"])
                        else:
                            tn.wait_ge(self.sems["eV"], y_v[tt - 2])
                        tn.wait_ge(self.sems["eS"], sbf[tt])
                        for a in range(2):
                            tn.matmul(pbQ[64 * a:64 * a + 64, 0:128],
                                      t["a_Sbf"][:, tt * 128 + 64 * a:
                                                 tt * 128 + 64 * a + 64], qs,
                                      start=True, stop=True)
                        tn.matmul(pbP[0:128, 0:128], ks[0:64, :], qs[0:64, :],
                                  start=True, stop=True)
                        ins = tn.matmul(pbQ[0:128, 256:384], ks[64:128, :],
                                        qs[64:128, :], start=True, stop=True)
                        self.inc(ins, "eT")
                    mm1 = self.counts["eT"]

                    @block.vector
                    def _(v, c=c, tt=tt, mm1=mm1):
                        pbP, pbQ = pb[(tt % 2) * 2], pb[(tt % 2) * 2 + 1]
                        v.wait_ge(self.sems["eT"], mm1)
                        v.tensor_tensor(out=t["a_sc"][:, (tt * 2) * 128:(tt * 2) * 128 + 128],
                                        in0=pbP[:, 0:128], in1=t["c_triu"][:],
                                        op=ALU.mult)
                        ins = v.tensor_tensor(
                            out=t["a_sc"][:, (tt * 2 + 1) * 128:(tt * 2 + 1) * 128 + 128],
                            in0=pbQ[:, 256:384], in1=t["c_triu"][:], op=ALU.mult)
                        sc_v[tt] = self.inc(ins, "eV")

                    @block.tensor
                    def _(tn, c=c, tt=tt):
                        pbP, pbQ = pb[(tt % 2) * 2], pb[(tt % 2) * 2 + 1]
                        tn.wait_ge(self.sems["eV"], sc_v[tt])
                        for a in range(2):
                            tn.matmul(pbQ[64 * a:64 * a + 64, 128:256],
                                      t["a_vtm"][:, c * DM + tt * 128 + 64 * a:
                                                 c * DM + tt * 128 + 64 * a + 64],
                                      t["a_sc"][:, (tt * 2 + a) * 128:
                                                (tt * 2 + a) * 128 + 128],
                                      start=True, stop=True)
                        ins = tn.matmul(pbP[0:128, 256:384],
                                        t["a_ktm"][:, c * DM + tt * 128:
                                                   c * DM + tt * 128 + 128],
                                        t["a_vtm"][:, c * DM + tt * 128:
                                                   c * DM + tt * 128 + 128],
                                        start=True, stop=True)
                        self.inc(ins, "eT")
                    mm2 = self.counts["eT"]

                    @block.vector
                    def _(v, c=c, tt=tt, mm2=mm2):
                        pbP, pbQ = pb[(tt % 2) * 2], pb[(tt % 2) * 2 + 1]
                        v.wait_ge(self.sems["eT"], mm2)
                        zr = t["a_zrep"][:, tt * TSEG + c * 128:
                                         tt * TSEG + c * 128 + 128]
                        v.tensor_tensor(out=t["a_t1"][:, 0:128], in0=pbQ[:, 0:128],
                                        in1=zr, op=ALU.mult)
                        v.tensor_tensor(out=t["a_t2"][:, 0:128], in0=pbQ[:, 128:256],
                                        in1=zr, op=ALU.mult)
                        ins = v.tensor_tensor(out=t["a_y"][:, c * DM + tt * 128:
                                                           c * DM + tt * 128 + 128],
                                              in0=t["a_t1"][:, 0:128],
                                              in1=t["a_t2"][:, 0:128], op=ALU.add)
                        if "sadd" not in skip:
                            v.tensor_tensor(out=t["a_S"][0:64, tt * 64:(tt + 1) * 64],
                                            in0=t["a_S"][0:64, tt * 64:(tt + 1) * 64],
                                            in1=pbP[0:64, 256:320], op=ALU.add)
                            ins = v.tensor_tensor(out=t["a_S"][64:128, tt * 64:(tt + 1) * 64],
                                                  in0=t["a_S"][64:128, tt * 64:(tt + 1) * 64],
                                                  in1=pbP[64:128, 320:384], op=ALU.add)
                        y_v[tt] = self.inc(ins, "eV")

                    @block.scalar
                    def _(sc, c=c, tt=tt):
                        sc.wait_ge(self.sems["eV"], y_v[tt])
                        if "sbf" not in skip:
                            sc.activation(out=t["a_Sbf"][0:64, tt * 128: tt * 128 + 64],
                                          in_=t["a_S"][0:64, tt * 64:(tt + 1) * 64],
                                          func=ACT.Identity)
                            ins = sc.activation(out=t["a_Sbf"][64:128, tt * 128 + 64: tt * 128 + 128],
                                                in_=t["a_S"][64:128, tt * 64:(tt + 1) * 64],
                                                func=ACT.Identity)
                        else:
                            ins = sc.activation(out=t["a_t3"][0:128, 0:64],
                                                in_=t["a_S"][0:128, 0:64],
                                                func=ACT.Identity)
                        sbf[tt] = self.inc(ins, "eS")

                @block.tensor
                def _(tn, c=c):
                    tn.wait_ge(self.sems["eV"], y_v[7])
                    tn.wait_ge(self.sems["eV"], y_v[6])
                    for dh in range(2):
                        bank = pb[dh * 4]
                        for kt in range(8):
                            ins = tn.matmul(
                                bank[:, 0:512],
                                t["a_y"][:, c * DM + kt * 128: c * DM + kt * 128 + 128],
                                t["aw_o"][:, kt * DM + dh * 512: kt * DM + dh * 512 + 512],
                                start=(kt == 0), stop=(kt == 7))
                    self.inc(ins, "eT")
                op_eT = self.counts["eT"]

                @block.vector
                def _(v, c=c, op_eT=op_eT):
                    v.wait_ge(self.sems["eT"], op_eT)
                    v.wait_ge(self.sems["sG"], self.counts["sG"])
                    for dh in range(2):
                        ins = v.tensor_tensor(out=t["a_s"][:, dh * 512:(dh + 1) * 512],
                                              in0=pb[dh * 4][:, 0:512],
                                              in1=t["c_bo"][:, dh * 512:(dh + 1) * 512],
                                              op=ALU.add)
                    self.inc(ins, "eV")
                s_eV = self.counts["eV"]

                @block.gpsimd
                def _(g, c=c, seg=seg, s_eV=s_eV):
                    g.wait_ge(self.sems["eV"], s_eV)
                    self.dma(g, t["s_dram"][t0 + c * 128: t0 + c * 128 + 128, :],
                             t["a_s"][:], "sG")

    def ph_routing(self, layer):
        t, block = self.t, self.block
        E = E0 if layer == 0 else E1
        CG = CAP0G if layer == 0 else CAP1G
        LCAP = LCAP0 if layer == 0 else LCAP1
        SLOTS = SLOTS0 if layer == 0 else SLOTS1
        ecap = "c_ecap0" if layer == 0 else "c_ecap1"
        trashf = "c_trash0f" if layer == 0 else "c_trash1f"
        NE = NCHUNK * E
        HUGE = 1.0e6
        for n in ("r_lgtm", "r_m1tm", "r_m2tm", "r_s1tm", "r_s2tm",
                  "r_k1tm", "r_k2tm", "r_dif", "r_ttmp"):
            self.sbl(n, (128, NE), F32)
        for n in ("r_max", "r_sum", "r_g1", "r_max2", "r_g2", "r_den",
                  "r_f1f", "r_gk", "r_f2f"):
            self.sbl(n, (128, NCHUNK), F32)
        self.sbl("r_msk", (128, NCHUNK), U8)
        for n in ("r_m1fm", "r_m2fm", "r_loc1", "r_loc2", "r_k1fm",
                  "r_k2fm", "r_ftmp", "r_lgf"):
            self.sbl(n, (16, T_TOK), F32)
        self.sbl("r_ag", (self.ncores, 64), F32)
        self.sbl("r_pr", (1, 128), F32)
        self.sbl("r_sc", (16, 8), F32)
        self.sbl("r_xn", (128, DM), BF16)
        pba = self.psl("r_pba", (128, 512))
        pbb = self.psl("r_pbb", (128, 512))

        def e3(name):
            return t[name][:].rearrange("p (c e) -> p c e", e=E)

        def bc(name):
            return t[name][:].unsqueeze(2).broadcast_to((128, NCHUNK, E))

        @block.vector
        def _(v):
            ins = v.tensor_copy(out=t["r_lgf"][0:E, :], in_=t["lg_fm"][0:E, :])
            self.inc(ins, "eV")
        lgf_eV = self.counts["eV"]
        tr_v = []
        for c in range(NCHUNK):
            @block.tensor
            def _(tn, c=c):
                bank = pba if c % 2 == 0 else pbb
                tn.wait_ge(self.sems["eV"], tr_v[c - 2] if c >= 2 else lgf_eV)
                ins = tn.transpose(bank[0:128, 0:E],
                                   t["r_lgf"][0:E, c * 128:(c + 1) * 128],
                                   t["c_eye"][0:E, 0:E])
                self.inc(ins, "eT")
            cnt = self.counts["eT"]

            @block.vector
            def _(v, c=c, cnt=cnt):
                bank = pba if c % 2 == 0 else pbb
                v.wait_ge(self.sems["eT"], cnt)
                ins = v.tensor_copy(out=t["r_lgtm"][:, c * E:(c + 1) * E],
                                    in_=bank[0:128, 0:E])
                tr_v.append(self.inc(ins, "eV"))

        def vtree(v, src, scratch, op, out2):
            w = E // 2
            v.tensor_tensor(out=scratch[:, :, 0:w], in0=src[:, :, 0:w],
                            in1=src[:, :, w:2 * w], op=op)
            while w > 2:
                w2 = w // 2
                v.tensor_tensor(out=scratch[:, :, 0:w2], in0=scratch[:, :, 0:w2],
                                in1=scratch[:, :, w2:w], op=op)
                w = w2
            v.tensor_tensor(out=out2.unsqueeze(2), in0=scratch[:, :, 0:1],
                            in1=scratch[:, :, 1:2], op=op)

        @block.vector
        def _(v):
            v.wait_ge(self.sems["eV"], tr_v[-1])
            self.vgap(v)
            vtree(v, e3("r_lgtm"), e3("r_ttmp"), ALU.max, t["r_max"][:])
            self.vgap(v)
            for e in range(E):
                v.tensor_tensor(out=e3("r_m1tm")[:, :, e], in0=e3("r_lgtm")[:, :, e],
                                in1=t["r_max"][:], op=ALU.is_ge)
            self.vgap(v)
            v.scalar_tensor_tensor(out=t["r_ttmp"][:], in0=t["r_m1tm"][:],
                                   scalar=-1.0e30, in1=t["r_lgtm"][:],
                                   op0=ALU.mult, op1=ALU.add)
            vtree(v, e3("r_ttmp"), e3("r_dif"), ALU.max, t["r_max2"][:])
            self.vgap(v)
            for e in range(E):
                v.tensor_tensor(out=e3("r_m2tm")[:, :, e], in0=e3("r_ttmp")[:, :, e],
                                in1=t["r_max2"][:], op=ALU.is_ge)
            v.tensor_tensor(out=e3("r_dif"), in0=e3("r_lgtm"),
                            in1=bc("r_max"), op=ALU.subtract)
            ins = v.tensor_tensor(out=t["r_max2"][:], in0=t["r_max2"][:],
                                  in1=t["r_max"][:], op=ALU.subtract)
            self.inc(ins, "eV")

        @block.scalar
        def _(sc):
            sc.wait_ge(self.sems["eV"], self.counts["eV"])
            sc.activation(out=t["r_dif"][:], in_=t["r_dif"][:], func=ACT.Exp)
            ins = sc.activation(out=t["r_g2"][:], in_=t["r_max2"][:], func=ACT.Exp)
            self.inc(ins, "eS")
        exp_eS = self.counts["eS"]

        @block.vector
        def _(v):
            v.wait_ge(self.sems["eS"], exp_eS)
            vtree(v, e3("r_dif"), e3("r_ttmp"), ALU.add, t["r_sum"][:])
            self.vgap(v)
            v.reciprocal(t["r_g1"][:], t["r_sum"][:])
            self.vgap(v)
            v.tensor_tensor(out=t["r_g2"][:], in0=t["r_g2"][:], in1=t["r_g1"][:],
                            op=ALU.mult)
            if layer == 0:
                self.vgap(v)
                v.tensor_tensor(out=t["r_den"][:], in0=t["r_g1"][:],
                                in1=t["r_g2"][:], op=ALU.add)
                self.vgap(v)
                v.tensor_scalar_max(t["r_den"][:], t["r_den"][:], 1e-9)
                self.vgap(v)
                v.reciprocal(t["r_den"][:], t["r_den"][:])
                self.vgap(v)
                v.tensor_tensor(out=t["r_g1"][:], in0=t["r_g1"][:],
                                in1=t["r_den"][:], op=ALU.mult)
                ins = v.tensor_tensor(out=t["r_g2"][:], in0=t["r_g2"][:],
                                      in1=t["r_den"][:], op=ALU.mult)
            else:
                ins = v.tensor_copy(out=t["r_den"][:], in_=t["r_g1"][:])
            self.inc(ins, "eV")
        gates_eV = self.counts["eV"]

        plan3 = [("r_m1tm", "r_m1fm")] + ([("r_m2tm", "r_m2fm")] if layer == 0 else [])
        m2f_v = []
        for c in range(NCHUNK):
            for wi, (src, dst) in enumerate(plan3):
                idx = c * len(plan3) + wi

                @block.tensor
                def _(tn, c=c, src=src, idx=idx):
                    bank = pba if idx % 2 == 0 else pbb
                    if idx < 2:
                        tn.wait_ge(self.sems["eV"], gates_eV)
                    else:
                        tn.wait_ge(self.sems["eV"], m2f_v[idx - 2])
                    ins = tn.transpose(bank[0:E, 0:128],
                                       t[src][:, c * E:(c + 1) * E],
                                       t["c_eye"][:])
                    self.inc(ins, "eT")
                cnt = self.counts["eT"]

                @block.vector
                def _(v, c=c, dst=dst, idx=idx, cnt=cnt):
                    bank = pba if idx % 2 == 0 else pbb
                    v.wait_ge(self.sems["eT"], cnt)
                    ins = v.tensor_copy(out=t[dst][0:E, c * 128:(c + 1) * 128],
                                        in_=bank[0:E, 0:128])
                    m2f_v.append(self.inc(ins, "eV"))

        @block.vector
        def _(v):
            v.wait_ge(self.sems["eV"], m2f_v[-1])
            v.tensor_tensor_scan(out=t["r_loc1"][0:E, :], data0=t["r_m1fm"][0:E, :],
                                 data1=t["r_m1fm"][0:E, :], initial=0.0,
                                 op0=ALU.add, op1=ALU.bypass)
            if layer == 0:
                v.tensor_tensor_scan(out=t["r_loc2"][0:E, :],
                                     data0=t["r_m2fm"][0:E, :],
                                     data1=t["r_m2fm"][0:E, :], initial=0.0,
                                     op0=ALU.add, op1=ALU.bypass)
            v.tensor_copy(out=t["r_sc"][0:E, 6:7],
                          in_=t["r_loc1"][0:E, T_TOK - 1:T_TOK])
            self.vgap(v)
            ins = v.tensor_scalar_add(t["r_sc"][0:E, 7:8], t["r_sc"][0:E, 6:7], -1.0)
            self.inc(ins, "eV")
        scan_eV = self.counts["eV"]

        @block.gpsimd
        def _(g):
            g.wait_ge(self.sems["eV"], scan_eV)
            self.dma(g, t["cc_in"][0:1, 0:E], t["r_loc1"][0:E, T_TOK - 1:T_TOK], "sG")
            if layer == 0:
                self.dma(g, t["cc_in"][0:1, E:2 * E],
                         t["r_loc2"][0:E, T_TOK - 1:T_TOK], "sG")
            g.wait_ge(self.sems["sG"], self.counts["sG"])
            ins = g.collective_compute(
                "AllGather", ALU.bypass, replica_groups=[list(range(self.ncores))],
                ins=[t["cc_in"].opt()], outs=[t["cc_out"].opt()])
            self.inc(ins, "sCC")
            g.wait_ge(self.sems["sCC"], self.counts["sCC"])
            self.dma(g, t["r_ag"][:], t["cc_out"], "sG")
            g.wait_ge(self.sems["sG"], self.counts["sG"])
            ins = g.iota(t["c_scr"][0:1, 0:1], [[0, 1]], base=0, channel_multiplier=0)
            self.inc(ins, "eG")
        ag_eG = self.counts["eG"]

        @block.tensor
        def _(tn):
            tn.wait_ge(self.sems["eG"], ag_eG)
            tn.matmul(pba[0:1, 0:64], t["c_pref"][:], t["r_ag"][:],
                      start=True, stop=True)
            ins = tn.matmul(pba[0:1, 64:128], t["c_ones8"][:], t["r_ag"][:],
                            start=True, stop=True)
            self.inc(ins, "eT")
        pr_eT = self.counts["eT"]

        @block.vector
        def _(v):
            v.wait_ge(self.sems["eT"], pr_eT)
            ins = v.tensor_copy(out=t["r_pr"][0:1, 0:128], in_=pba[0:1, 0:128])
            self.inc(ins, "eV")
        pr_eV = self.counts["eV"]

        @block.gpsimd
        def _(g):
            g.wait_ge(self.sems["eV"], pr_eV)
            self.dma(g, t["row_dram"][0:2, 0:64], t["r_pr"][0:1, 0:128], "sG")
            g.wait_ge(self.sems["sG"], self.counts["sG"])
            self.dma(g, t["r_sc"][0:E, 0:1], t["row_dram"][0:1, 0:E], "sG")
            if layer == 0:
                self.dma(g, t["r_sc"][0:E, 1:2], t["row_dram"][0:1, E:2 * E], "sG")
            self.dma(g, t["r_sc"][0:E, 2:3], t["row_dram"][1:2, 0:E], "sG")
            g.wait_ge(self.sems["sG"], self.counts["sG"])
            ins = g.iota(t["c_scr"][0:1, 1:2], [[0, 1]], base=0, channel_multiplier=0)
            self.inc(ins, "eG")
        sc_eG = self.counts["eG"]

        @block.vector
        def _(v):
            v.wait_ge(self.sems["eG"], sc_eG)
            sl = t["r_sc"]

            def choice(locn, kfm, prefcols):
                v.tensor_scalar(out=t["r_ftmp"][0:E, :], in0=t[locn][0:E, :],
                                scalar1=prefcols[0], scalar2=None, op0=ALU.add)
                if len(prefcols) > 1:
                    v.tensor_scalar(out=t["r_ftmp"][0:E, :], in0=t["r_ftmp"][0:E, :],
                                    scalar1=prefcols[1], scalar2=None, op0=ALU.add)
                v.tensor_single_scalar(out=t[kfm][0:E, :], in_=t["r_ftmp"][0:E, :],
                                       scalar=CG + 0.5, op=ALU.is_lt)
                if locn == "r_loc2":
                    v.tensor_scalar(out=t[locn][0:E, :], in0=t[locn][0:E, :],
                                    scalar1=sl[0:E, 7:8], scalar2=None, op0=ALU.add)
                v.tensor_scalar_add(t[locn][0:E, :], t[locn][0:E, :], -1.0)
                v.tensor_single_scalar(out=t["r_ftmp"][0:E, :], in_=t[locn][0:E, :],
                                       scalar=LCAP - 0.5, op=ALU.is_lt)
                v.tensor_tensor(out=t[kfm][0:E, :], in0=t[kfm][0:E, :],
                                in1=t["r_ftmp"][0:E, :], op=ALU.mult)
                v.tensor_single_scalar(out=t["r_ftmp"][0:E, :], in_=t[locn][0:E, :],
                                       scalar=LCAP - 0.5, op=ALU.is_ge)
                v.tensor_scalar(out=t[locn][0:E, :], in0=t[locn][0:E, :],
                                scalar1=t[ecap][0:E, 0:1], scalar2=None, op0=ALU.add)
                v.scalar_tensor_tensor(out=t[locn][0:E, :], in0=t["r_ftmp"][0:E, :],
                                       scalar=HUGE, in1=t[locn][0:E, :],
                                       op0=ALU.mult, op1=ALU.add)
            choice("r_loc1", "r_k1fm", [sl[0:E, 0:1]])
            if layer == 0:
                choice("r_loc2", "r_k2fm", [sl[0:E, 1:2], sl[0:E, 2:3]])
            ins = v.tensor_copy(out=t["r_ftmp"][0:1, 0:1], in_=t["r_ftmp"][0:1, 0:1])
            self.inc(ins, "eV")
        fm_eV = self.counts["eV"]

        plan9 = [("r_loc1", "r_s1tm"), ("r_k1fm", "r_k1tm")]
        if layer == 0:
            plan9 += [("r_loc2", "r_s2tm"), ("r_k2fm", "r_k2tm")]
        s2t_v = []
        for c in range(NCHUNK):
            for wi, (src, dst) in enumerate(plan9):
                idx = c * len(plan9) + wi

                @block.tensor
                def _(tn, c=c, src=src, idx=idx):
                    bank = pba if idx % 2 == 0 else pbb
                    if idx < 2:
                        tn.wait_ge(self.sems["eV"], fm_eV)
                    else:
                        tn.wait_ge(self.sems["eV"], s2t_v[idx - 2])
                    ins = tn.transpose(bank[0:128, 0:E],
                                       t[src][0:E, c * 128:(c + 1) * 128],
                                       t["c_eye"][0:E, 0:E])
                    self.inc(ins, "eT")
                cnt = self.counts["eT"]

                @block.vector
                def _(v, c=c, dst=dst, idx=idx, cnt=cnt):
                    bank = pba if idx % 2 == 0 else pbb
                    v.wait_ge(self.sems["eT"], cnt)
                    ins = v.tensor_copy(out=t[dst][:, c * E:(c + 1) * E],
                                        in_=bank[0:128, 0:E])
                    s2t_v.append(self.inc(ins, "eV"))

        @block.vector
        def _(v):
            v.wait_ge(self.sems["eV"], s2t_v[-1])
            self.vgap(v)

            def tok(mtm, stm, ktm, fdst, gsrc, gdst, fcol):
                A, B = e3("r_ttmp"), e3("r_dif")
                v.tensor_tensor(out=t["r_ttmp"][:], in0=t[mtm][:], in1=t[stm][:],
                                op=ALU.mult)
                v.tensor_tensor(out=t["r_dif"][:], in0=t[mtm][:], in1=t[ktm][:],
                                op=ALU.mult)
                w = E // 2
                while w > 1:
                    v.tensor_tensor(out=A[:, :, 0:w], in0=A[:, :, 0:w],
                                    in1=A[:, :, w:2 * w], op=ALU.add)
                    v.tensor_tensor(out=B[:, :, 0:w], in0=B[:, :, 0:w],
                                    in1=B[:, :, w:2 * w], op=ALU.add)
                    w //= 2
                v.tensor_tensor(out=t[fdst][:].unsqueeze(2), in0=A[:, :, 0:1],
                                in1=A[:, :, 1:2], op=ALU.add)
                v.tensor_tensor(out=t["r_gk"][:].unsqueeze(2), in0=B[:, :, 0:1],
                                in1=B[:, :, 1:2], op=ALU.add)
                self.vgap(v)
                v.tensor_tensor(out=self.t[gdst][:], in0=t[gsrc][:],
                                in1=t["r_gk"][:], op=ALU.mult)
                v.tensor_single_scalar(out=t["r_max"][:], in_=t[fdst][:],
                                       scalar=float(SLOTS) - 0.5, op=ALU.is_ge)
                self.vgap(v)
                v.tensor_copy(out=t["r_msk"][:], in_=t["r_max"][:])
                self.vgap(v)
                v.select(out=t[fdst][:], mask=t["r_msk"][:],
                         on_true=t[trashf][:, 0:1].broadcast_to((128, NCHUNK)),
                         on_false=t[fdst][:])
                self.vgap(v)
                v.tensor_copy(out=self.t[fcol][:], in_=t[fdst][:])
            tok("r_m1tm", "r_s1tm", "r_k1tm", "r_f1f", "r_g1", "g1_col", "f1_col")
            if layer == 0:
                tok("r_m2tm", "r_s2tm", "r_k2tm", "r_f2f", "r_g2", "g2_col",
                    "f2_col")
            ins = v.tensor_copy(out=t["r_max"][0:1, 0:1], in_=t["r_max"][0:1, 0:1])
            self.inc(ins, "eV")
        rt_eV = self.counts["eV"]

        xn_ld = []
        for c in range(NCHUNK):
            @block.sync
            def _(sync, c=c):
                if c >= 1:
                    sync.wait_ge(self.sems["sG"], self.counts["sG"])
                xn_ld.append(self.dma(sync, t["r_xn"][:],
                                      t["xn_dram"][c * 128:(c + 1) * 128, :], "sQ"))

            @block.gpsimd
            def _(g, c=c):
                if c == 0:
                    g.wait_ge(self.sems["eV"], rt_eV)
                g.wait_ge(self.sems["sQ"], xn_ld[c])
                self.idma(g, t["disp_dram"], t["r_xn"][:], "sG",
                          out_offset=bass.IndirectOffsetOnAxis(
                              self.t["f1_col"][:, c:c + 1], 0))
                if layer == 0:
                    self.idma(g, t["disp_dram"], t["r_xn"][:], "sG",
                              out_offset=bass.IndirectOffsetOnAxis(
                                  self.t["f2_col"][:, c:c + 1], 0))

    def ph_moe(self, layer):
        t, block = self.t, self.block
        E = E0 if layer == 0 else E1
        LCAP = LCAP0 if layer == 0 else LCAP1
        wname1 = "w_gate0" if layer == 0 else "w1"
        wname2 = "w_down0" if layer == 0 else "w2"
        gfun = ACT.Gelu if layer == 0 else ACT.Gelu_apprx_tanh
        self.sbl("e_dfm", (128, 8 * 512), BF16)
        self.sbl("e_w1", (128, 8 * 512), BF16)
        self.sbl("e_w1b", (128, 8 * 512), BF16)
        self.sbl("e_h", (128, 32 * 512), BF16)
        self.sbl("e_w2", (128, 1024), BF16)
        self.sbl("e_w2b", (128, 1024), BF16)
        self.sbl("e_eo", (128, 4 * 1024), BF16)
        pb = [self.psl(f"e_pb{i}") for i in range(8)]
        groups = []
        n = LCAP
        off = 0
        while n > 0:
            g_ = min(512, n)
            groups.append((off, g_))
            off += g_
            n -= g_

        for e in range(E):
            for gi, (goff, gN) in enumerate(groups):
                s0 = e * LCAP + goff
                dld = []

                @block.sync
                def _(sync, s0=s0, gN=gN, e=e, gi=gi):
                    sync.wait_ge(self.sems["eT"], self.counts["eT"])
                    for dtl in range(8):
                        dld.append(self.trdma(
                            sync, t["e_dfm"][:, dtl * 512: dtl * 512 + gN],
                            t["disp_dram"][s0:s0 + gN, dtl * 128:(dtl + 1) * 128],
                            "sQ"))

                for fs in range(8):
                    wbuf = "e_w1" if fs % 2 == 0 else "e_w1b"

                    @block.gpsimd
                    def _(g, e=e, fs=fs, wbuf=wbuf):
                        g.wait_ge(self.sems["eT"], self.counts["eT"])
                        for dtl in range(8):
                            self.dma(g, t[wbuf][:, dtl * 512:(dtl + 1) * 512],
                                     t[wname1][e, dtl * 128:(dtl + 1) * 128,
                                               fs * 512:(fs + 1) * 512], "sG")
                    w_sG = self.counts["sG"]

                    for fc in range(4):
                        @block.tensor
                        def _(tn, fs=fs, fc=fc, gN=gN, wbuf=wbuf, w_sG=w_sG):
                            bank = pb[(fs * 4 + fc) % 4]
                            if fs == 0 and fc == 0:
                                tn.wait_ge(self.sems["sQ"], dld[-1])
                                tn.wait_ge(self.sems["eV"], self.counts["eV"])
                            tn.wait_ge(self.sems["sG"], w_sG)
                            tn.wait_ge(self.sems["eS"], self.counts["eS"])
                            for dtl in range(8):
                                ins = tn.matmul(
                                    bank[:, 0:gN],
                                    t[wbuf][:, dtl * 512 + fc * 128:
                                            dtl * 512 + fc * 128 + 128],
                                    t["e_dfm"][:, dtl * 512: dtl * 512 + gN],
                                    start=(dtl == 0), stop=(dtl == 7))
                            self.inc(ins, "eT")
                        mmc = self.counts["eT"]

                        @block.scalar
                        def _(sc, fs=fs, fc=fc, gN=gN, mmc=mmc):
                            bank = pb[(fs * 4 + fc) % 4]
                            sc.wait_ge(self.sems["eT"], mmc)
                            ins = sc.activation(
                                out=t["e_h"][:, (fs * 4 + fc) * 512:
                                             (fs * 4 + fc) * 512 + gN],
                                in_=bank[:, 0:gN], func=gfun)
                            self.inc(ins, "eS")
                h_eS = self.counts["eS"]

                nsub = (gN + 127) // 128
                for ft in range(32):
                    wbuf = "e_w2" if ft % 2 == 0 else "e_w2b"

                    @block.gpsimd
                    def _(g, e=e, ft=ft, wbuf=wbuf):
                        g.wait_ge(self.sems["eT"], self.counts["eT"])
                        self.dma(g, t[wbuf][:],
                                 t[wname2][e, ft * 128:(ft + 1) * 128, :], "sG")
                    w2_sG = self.counts["sG"]

                    @block.tensor
                    def _(tn, ft=ft, gN=gN, nsub=nsub, wbuf=wbuf,
                          w2_sG=w2_sG, h_eS=h_eS):
                        tn.wait_ge(self.sems["sG"], w2_sG)
                        if ft == 0:
                            tn.wait_ge(self.sems["eS"], h_eS)
                            tn.wait_ge(self.sems["eV"], self.counts["eV"])
                        for sub in range(nsub):
                            sw = min(128, gN - sub * 128)
                            for dh in range(2):
                                ins = tn.matmul(
                                    pb[sub * 2 + dh][0:sw, 0:512],
                                    t["e_h"][:, ft * 512 + sub * 128:
                                             ft * 512 + sub * 128 + sw],
                                    t[wbuf][:, dh * 512:(dh + 1) * 512],
                                    start=(ft == 0), stop=(ft == 31))
                        self.inc(ins, "eT")
                mm2 = self.counts["eT"]

                @block.vector
                def _(v, gN=gN, nsub=nsub, mm2=mm2):
                    v.wait_ge(self.sems["eT"], mm2)
                    for sub in range(nsub):
                        sw = min(128, gN - sub * 128)
                        for dh in range(2):
                            ins = v.tensor_copy(
                                out=t["e_eo"][0:sw, sub * 1024 + dh * 512:
                                              sub * 1024 + dh * 512 + 512],
                                in_=pb[sub * 2 + dh][0:sw, 0:512])
                    self.inc(ins, "eV")
                eo_eV = self.counts["eV"]

                @block.scalar
                def _(sc, s0=s0, gN=gN, nsub=nsub, eo_eV=eo_eV):
                    sc.wait_ge(self.sems["eV"], eo_eV)
                    for sub in range(nsub):
                        sw = min(128, gN - sub * 128)
                        self.dma(sc, t["eo_dram"][s0 + sub * 128:
                                                  s0 + sub * 128 + sw, :],
                                 t["e_eo"][0:sw, sub * 1024: sub * 1024 + 1024],
                                 "sA")

    def ph_combine(self, layer):
        t, block = self.t, self.block
        self.sbl("cb_eoA", (128, DM), BF16)
        self.sbl("cb_eoB", (128, DM), BF16)
        self.sbl("cb_s", (128, DM), BF16)
        self.sbl("cb_x", (128, DM), BF16)
        self.sbl("cb_g", (128, DM), F32)
        self.sbl("cb_gb", (128, DM), BF16)
        self.sbl("cb_o", (128, DM), BF16 if layer == 0 else F32)
        for c in range(NCHUNK):
            ld = {}

            @block.gpsimd
            def _(g, c=c):
                g.wait_ge(self.sems["eV"], self.counts["eV"])
                ld["A"] = self.idma(g, t["cb_eoA"][:], t["eo_dram"], "sG",
                                    in_offset=bass.IndirectOffsetOnAxis(
                                        self.t["f1_col"][:, c:c + 1], 0))
                if layer == 0:
                    ld["B"] = self.idma(g, t["cb_eoB"][:], t["eo_dram"], "sG",
                                        in_offset=bass.IndirectOffsetOnAxis(
                                            self.t["f2_col"][:, c:c + 1], 0))

            @block.sync
            def _(sync, c=c):
                sync.wait_ge(self.sems["eV"], self.counts["eV"])
                ld["s"] = self.dma(sync, t["cb_s"][:],
                                   t["s_dram"][c * 128:(c + 1) * 128, :], "sQ")
                ld["x"] = self.dma(sync, t["cb_x"][:],
                                   t["x_dram"][c * 128:(c + 1) * 128, :], "sQ")

            @block.vector
            def _(v, c=c):
                v.wait_ge(self.sems["sG"], ld["B"] if layer == 0 else ld["A"])
                v.wait_ge(self.sems["eS"], self.counts["eS"])
                ins = v.tensor_scalar(out=t["cb_g"][:], in0=t["cb_eoA"][:],
                                      scalar1=self.t["g1_col"][:, c:c + 1],
                                      scalar2=None, op0=ALU.mult)
                if layer == 0:
                    ins = v.scalar_tensor_tensor(
                        out=t["cb_g"][:], in0=t["cb_eoB"][:],
                        scalar=self.t["g2_col"][:, c:c + 1], in1=t["cb_g"][:],
                        op0=ALU.mult, op1=ALU.add)
                self.inc(ins, "eV")
            g_eV = self.counts["eV"]

            @block.scalar
            def _(sc, c=c, g_eV=g_eV):
                sc.wait_ge(self.sems["eV"], g_eV)
                ins = sc.activation(out=t["cb_gb"][:], in_=t["cb_g"][:],
                                    func=ACT.Silu)
                self.inc(ins, "eS")
            sg_eS = self.counts["eS"]

            @block.vector
            def _(v, c=c, sg_eS=sg_eS):
                v.wait_ge(self.sems["eS"], sg_eS)
                v.wait_ge(self.sems["sQ"], ld["x"])
                v.wait_ge(self.sems["sA"], self.counts["sA"])
                v.tensor_tensor(out=t["cb_g"][:], in0=t["cb_s"][:],
                                in1=t["cb_gb"][:], op=ALU.mult)
                ins = v.tensor_tensor(out=t["cb_o"][:], in0=t["cb_x"][:],
                                      in1=t["cb_g"][:], op=ALU.add)
                self.inc(ins, "eV")
            o_eV = self.counts["eV"]

            @block.scalar
            def _(sc, c=c, o_eV=o_eV):
                sc.wait_ge(self.sems["eV"], o_eV)
                if layer == 0:
                    self.dma(sc, t["x_dram"][c * 128:(c + 1) * 128, :],
                             t["cb_o"][:], "sA")
                else:
                    self.dma(sc, t["out"][c * 128:(c + 1) * 128, :],
                             t["cb_o"][:], "sA")


def kernel(**inputs):
    from concourse.bass_utils import run_bass_kernel_spmd
    x = np.asarray(inputs["L0_attn_x"], dtype=np.float32)
    mk = MK()
    nc = mk.build()
    in_maps = [host_inputs(b, x[b], inputs, ncores=8) for b in range(8)]
    res = run_bass_kernel_spmd(nc, in_maps, core_ids=list(range(8)))
    return np.stack([np.asarray(res.results[b]["out"], dtype=np.float32)
                     for b in range(8)], axis=0)


# version 05 (final)
# speedup vs baseline: 1.3340x; 1.3234x over previous
import contextlib
import numpy as np
import ml_dtypes
import concourse.bass as bass
import concourse.mybir as mybir

F32 = mybir.dt.float32
BF16 = mybir.dt.bfloat16
U8 = mybir.dt.uint8
U32 = mybir.dt.uint32
ALU = mybir.AluOpType
ACT = mybir.ActivationFunctionType
AX = mybir.AxisListType.X

T_TOK, DM, NCHUNK = 4096, 1024, 32
DI, NH, DS, HD, DPROJ = 2048, 32, 64, 64, 8224
E0, CAP0G, LCAP0 = 8, 8192, 1408
E1, CAP1G, LCAP1 = 16, 2560, 512
SLOTS0, SLOTS1, TRASH = E0 * LCAP0, E1 * LCAP1, 128
H1, DH1, FF = 16, 64, 4096
TSEG, NSEG, LC = 256, 16, 128
HALF = 256
NT = 16
LEVELS = (1, 2, 4, 8, 16, 32, 64)


def _bf(x):
    return np.ascontiguousarray(np.asarray(x, dtype=np.float32).astype(ml_dtypes.bfloat16))


def host_inputs(core_id, x_shard, W, ncores=8):
    perm = list(range(0, 4096))
    for h in range(NH):
        perm += [4096 + 129 * h + s for s in range(64)]
    for h in range(NH):
        perm += [4096 + 129 * h + 64 + s for s in range(64)]
    perm += [4096 + 129 * h + 128 for h in range(NH)]
    al = np.empty((128, 16), np.float32)
    dr = np.empty((128, 16), np.float32)
    cw = np.empty((128, 64), np.float32)
    cb = np.empty((128, 16), np.float32)
    for tt in range(16):
        for p in range(128):
            al[p, tt] = W["L0_attn_A_log"][2 * tt + p // 64, p % 64]
            dr[p, tt] = W["L0_attn_D"][2 * tt + p // 64]
            cb[p, tt] = W["L0_attn_conv_b"][128 * tt + p]
            for k in range(4):
                cw[p, 4 * tt + k] = W["L0_attn_conv_w"][128 * tt + p, k]
    hsel = np.zeros((32, 2048), np.float32)
    for tt in range(16):
        for a in range(2):
            hsel[2 * tt + a, 128 * tt + 64 * a:128 * tt + 64 * a + 64] = 1.0
    bones16 = np.zeros((128, 128), np.float32)
    zsel = np.zeros((16, 1024), np.float32)
    for dt_ in range(8):
        for p in range(128):
            h = 2 * dt_ + p // 64
            bones16[p, 16 * dt_ + h] = 1.0
            zsel[h, 128 * dt_ + p] = 1.0
    pref = np.zeros((ncores, 1), np.float32)
    pref[:core_id] = 1.0
    d = {
        "x": np.ascontiguousarray(x_shard.astype(np.float32)),
        "W_inT": np.ascontiguousarray(W["L0_attn_W_in"][perm, :].T),
        "W_outT": np.ascontiguousarray(W["L0_attn_W_out"].T),
        "conv_w_r": cw, "conv_b_r": cb, "A_log_r": al, "D_r": dr,
        "dt_bias_c": np.ascontiguousarray(W["L0_attn_dt_bias"].reshape(32, 1)),
        "wg0": np.ascontiguousarray(W["L0_ffn_wg"]),
        "w_gate0": np.ascontiguousarray(W["L0_ffn_w_gate"]),
        "w_down0": np.ascontiguousarray(W["L0_ffn_w_down"]),
        "Wq": np.ascontiguousarray(W["L1_attn_Wq"]),
        "Wk": np.ascontiguousarray(W["L1_attn_Wk"]),
        "Wv": np.ascontiguousarray(W["L1_attn_Wv"]),
        "Wo": np.ascontiguousarray(W["L1_attn_Wo"]),
        "bo": np.ascontiguousarray(
            np.broadcast_to(W["L1_attn_bo"].reshape(1, DM), (128, DM))),
        "gate_wT": np.ascontiguousarray(W["L1_ffn_gate_w"].T),
        "w1": np.ascontiguousarray(W["L1_ffn_w1"]),
        "w2": np.ascontiguousarray(W["L1_ffn_w2"]),
        "hsel": hsel, "bones16": bones16, "zsel": zsel,
        "eye": np.eye(128, dtype=np.float32),
        "zeros_bf": _bf(np.zeros((128, 128))),
        "triu_bf": _bf(np.triu(np.ones((128, 128)))),
        "ones1_bf": _bf(np.ones((1, 128))),
        "eye_bf": _bf(np.eye(128)),
        "dd_bf": _bf(np.tile(np.eye(64), (2, 2))),
        "ud_bf": _bf(np.block([[np.eye(64), np.eye(64)],
                               [np.zeros((64, 64)), np.eye(64)]])),
        "prefmask": pref, "onesmask": np.ones((ncores, 1), np.float32),
        "ecap0": np.ascontiguousarray((np.arange(E0) * LCAP0).astype(np.float32).reshape(E0, 1)),
        "ecap1": np.ascontiguousarray((np.arange(E1) * LCAP1).astype(np.float32).reshape(E1, 1)),
    }
    for w in LEVELS:
        m = np.zeros((128, 64), np.float32)
        for k in range(64 // w):
            m[2 * k * w: 2 * k * w + w, k * w:(k + 1) * w] = 1
        d[f"mbank{w}"] = _bf(np.tile(m, (1, 8)))
    d["mbankd"] = _bf(np.tile(np.eye(128, dtype=np.float32), (1, 4)))
    return d


class MK:
    def __init__(self, debug_taps=(), ncores=8):
        self.ncores = ncores
        self.nc = bass.Bass("TRN2", target_bir_lowering=False,
                            detect_race_conditions=False)
        self.debug_taps = set(debug_taps)
        self.tap_req = []
        self.es = contextlib.ExitStack()
        self.local_es = None
        self.counts, self.sems, self.t = {}, {}, {}
        self.pb = {}

    def sem(self, name):
        if name not in self.sems:
            self.sems[name] = self.es.enter_context(self.nc.semaphore(name))
            self.counts[name] = 0
        return self.sems[name]

    def inc(self, instr, name, by=1):
        instr.then_inc(self.sem(name), by)
        self.counts[name] += by
        return self.counts[name]

    def dma(self, eng, out, in_, qsem, **kw):
        ins = eng.dma_start(out=out, in_=in_, **kw)
        return self.inc(ins, qsem, 16)

    def idma(self, eng, out, in_, qsem, out_offset=None, in_offset=None):
        ins = eng.indirect_dma_start(out=out, out_offset=out_offset,
                                     in_=in_, in_offset=in_offset)
        return self.inc(ins, qsem, 16)

    def trdma(self, eng, out, in_, qsem):
        ins = eng.dma_start_transpose(out=out, in_=in_)
        return self.inc(ins, qsem, 16)

    def vgap(self, v):
        v.tensor_copy(out=self.t["c_gap"][:], in_=self.t["c_eye"][:, 0:64])

    def snap(self):
        return dict(self.counts)

    def wait(self, eng, snapshot, only=None):
        for name, v in snapshot.items():
            if v <= 0 or (only is not None and name not in only):
                continue
            eng.wait_ge(self.sems[name], v)

    def sb(self, name, shape, dtype):
        tt = self.es.enter_context(self.nc.sbuf_tensor(name, list(shape), dtype))
        self.t[name] = tt
        return tt

    def sbl(self, name, shape, dtype):
        self._uid = getattr(self, "_uid", 0) + 1
        tt = self.local_es.enter_context(
            self.nc.sbuf_tensor(f"{name}_{self._uid}", list(shape), dtype))
        self.t[name] = tt
        return tt

    def psl(self, name, shape=(128, 512)):
        self._uid = getattr(self, "_uid", 0) + 1
        tt = self.local_es.enter_context(
            self.nc.psum_tensor(f"{name}_{self._uid}", list(shape), F32))
        self.t[name] = tt
        return tt

    def dram(self, name, shape, dtype, kind=None):
        tt = (self.nc.dram_tensor(name, list(shape), dtype, kind=kind) if kind
              else self.nc.dram_tensor(name, list(shape), dtype))
        self.t[name] = tt.ap()
        return self.t[name]

    def tap(self, name, src, shape, dtype=F32):
        if name in self.debug_taps:
            out = self.dram("dbg_" + name, shape, dtype, kind="ExternalOutput")
            self.tap_req.append((name, out, src))

    def flush_taps(self):
        if not self.tap_req:
            return
        snapshot, block = self.snap(), self.block

        @block.gpsimd
        def _(g):
            self.wait(g, snapshot)
            for name, out, src in self.tap_req:
                self.dma(g, out, src, "sG")
            g.wait_ge(self.sems["sG"], self.counts["sG"])
        self.tap_req = []

    @contextlib.contextmanager
    def phase(self, name):
        snapshot, block = self.snap(), self.block
        for engn in ("sync", "vector", "scalar", "tensor", "gpsimd"):
            getattr(block, engn)(lambda eng, s=snapshot: self.wait(eng, s))
        self.local_es = contextlib.ExitStack()
        with self.local_es:
            yield
        self.local_es = None

    def build(self):
        with self.es:
            self.declare_io()
            self.block = self.es.enter_context(self.nc.Block())
            for s in ("sQ", "sA", "sG", "eV", "eS", "eT", "eG", "sCC"):
                self.sem(s)
            with self.phase("prep"):
                self.ph_prep()
            with self.phase("ln0"):
                self.ph_ln_park()
            with self.phase("mamba"):
                self.ph_mamba()
            with self.phase("rout0"):
                self.ph_routing(0)
            with self.phase("moe0"):
                self.ph_moe(0)
            with self.phase("comb0"):
                self.ph_combine(0)
            with self.phase("ln1"):
                self.ph_ln_park()
            with self.phase("attn"):
                self.ph_attn()
            with self.phase("rout1"):
                self.ph_routing(1)
            with self.phase("moe1"):
                self.ph_moe(1)
            with self.phase("comb1"):
                self.ph_combine(1)
            self.ph_final_wait()
            self.flush_taps()
        return self.nc

    def declare_io(self):
        dr = self.dram
        dr("x", (T_TOK, DM), F32, kind="ExternalInput")
        for n, sh in [
                ("W_inT", (DM, DPROJ)), ("W_outT", (DI, DM)), ("conv_w_r", (128, 64)),
                ("conv_b_r", (128, 16)), ("A_log_r", (128, 16)), ("D_r", (128, 16)),
                ("dt_bias_c", (32, 1)), ("wg0", (DM, E0)), ("w_gate0", (E0, DM, FF)),
                ("w_down0", (E0, FF, DM)), ("Wq", (DM, DM)), ("Wk", (DM, DM)),
                ("Wv", (DM, DM)), ("Wo", (DM, DM)), ("bo", (128, DM)),
                ("gate_wT", (DM, E1)), ("w1", (E1, DM, FF)), ("w2", (E1, FF, DM)),
                ("bones16", (128, 128)), ("eye", (128, 128)),
                ("prefmask", (self.ncores, 1)),
                ("onesmask", (self.ncores, 1)), ("ecap0", (E0, 1)),
                ("ecap1", (E1, 1))]:
            dr(n, sh, F32, kind="ExternalInput")
        for n, sh in [("hsel", (32, 2048)), ("zsel", (16, 1024))]:
            dr(n, sh, F32, kind="ExternalInput")
        for n, sh in [("zeros_bf", (128, 128)), ("triu_bf", (128, 128)),
                      ("ones1_bf", (1, 128)), ("eye_bf", (128, 128)),
                      ("dd_bf", (128, 128)), ("ud_bf", (128, 128))]:
            dr(n, sh, BF16, kind="ExternalInput")
        for w in LEVELS:
            dr(f"mbank{w}", (128, 512), BF16, kind="ExternalInput")
        dr("mbankd", (128, 512), BF16, kind="ExternalInput")
        dr("out", (T_TOK, DM), F32, kind="ExternalOutput")
        dr("x_dram", (T_TOK, DM), BF16)
        dr("xn_dram", (T_TOK, DM), BF16)
        dr("s_dram", (T_TOK, DM), BF16)
        dr("woutbf_dram", (DI, DM), BF16)
        dr("winbf_dram", (DM, DPROJ), BF16)
        dr("disp_dram", (SLOTS0 + TRASH, DM), BF16)
        dr("eo_dram", (SLOTS0 + TRASH, DM), BF16)
        dr("parkX_dram", (DI, HALF), BF16)
        dr("parkB_dram", (DI, HALF), BF16)
        dr("parkK_dram", (DM, TSEG), BF16)
        dr("cc_in", (1, 64), F32)
        dr("cc_out", (self.ncores, 64), F32)
        dr("row_dram", (4, 64), F32)
        self.sb("c_eye", (128, 128), F32)
        self.sb("c_eyebf", (128, 128), BF16)
        self.sb("c_zeros", (128, 128), BF16)
        self.sb("c_triu", (128, 128), BF16)
        self.sb("c_ones1", (1, 128), BF16)
        self.sb("c_hsel", (32, 2048), F32)
        self.sb("c_zsel", (16, 1024), F32)
        self.sb("c_bones16", (128, 128), F32)
        self.sb("c_convw", (128, 64), F32)
        self.sb("c_convb", (128, 16), F32)
        self.sb("c_negA", (128, 16), F32)
        self.sb("c_D", (128, 16), F32)
        self.sb("c_dtb", (32, 1), F32)
        self.sb("c_wg0", (128, 8 * E0), BF16)
        self.sb("c_gw1", (128, 8 * E1), BF16)
        self.sb("c_pref", (self.ncores, 1), F32)
        self.sb("c_ones8", (self.ncores, 1), F32)
        self.sb("c_bo", (128, DM), BF16)
        self.sb("c_ecap0", (E0, 1), F32)
        self.sb("c_ecap1", (E1, 1), F32)

        self.sb("c_iota", (128, 1), U32)
        self.sb("c_scr", (1, 2), U32)
        self.sb("c_gap", (128, 64), F32)
        self.sb("c_eps", (128, 1), F32)
        self.sb("c_eps6", (128, 1), F32)
        self.sb("c_trash0", (128, 1), U32)
        self.sb("c_trash1", (128, 1), U32)
        self.sb("c_trash0f", (128, 1), F32)
        self.sb("c_trash1f", (128, 1), F32)
        self.sb("lg_fm", (16, T_TOK), BF16)
        self.sb("f1_col", (128, NCHUNK), U32)
        self.sb("f2_col", (128, NCHUNK), U32)
        self.sb("g1_col", (128, NCHUNK), F32)
        self.sb("g2_col", (128, NCHUNK), F32)

    def ph_prep(self):
        t, block = self.t, self.block
        self.sbl("tmpA", (128, 16), F32)
        self.sbl("tmpZ", (128, DM), BF16)

        @block.sync
        def _(sync):
            for n, dst in [("eye", "c_eye"), ("eye_bf", "c_eyebf"),
                           ("conv_w_r", "c_convw"), ("conv_b_r", "c_convb"),
                           ("D_r", "c_D"),
                           ("A_log_r", "tmpA"), ("dt_bias_c", "c_dtb"),
                           ("zeros_bf", "c_zeros"),
                           ("triu_bf", "c_triu"), ("ones1_bf", "c_ones1"),
                           ("hsel", "c_hsel"), ("zsel", "c_zsel"),
                           ("bones16", "c_bones16"), ("prefmask", "c_pref"),
                           ("onesmask", "c_ones8"), ("ecap0", "c_ecap0"),
                           ("ecap1", "c_ecap1")]:
                self.dma(sync, t[dst][:], t[n], "sQ")


        @block.gpsimd
        def _(g):
            self.dma(g, t["x_dram"], t["x"], "sG")
            self.dma(g, t["woutbf_dram"], t["W_outT"], "sG")
            self.dma(g, t["winbf_dram"], t["W_inT"], "sG")
            self.dma(g, t["c_bo"][:], t["bo"], "sG")
            for dtl in range(8):
                self.dma(g, t["c_wg0"][:, dtl * E0:(dtl + 1) * E0],
                         t["wg0"][dtl * 128:(dtl + 1) * 128, :], "sG")
                self.dma(g, t["c_gw1"][:, dtl * E1:(dtl + 1) * E1],
                         t["gate_wT"][dtl * 128:(dtl + 1) * 128, :], "sG")
            ins = g.iota(t["c_iota"][:], [[0, 1]], base=0, channel_multiplier=1)
            self.inc(ins, "eG")

        snap = self.snap()

        @block.scalar
        def _(sc):
            self.wait(sc, snap, only=("sQ",))
            ins = sc.activation(out=t["c_negA"][:], in_=t["tmpA"][:], func=ACT.Exp)
            self.inc(ins, "eS")

        @block.vector
        def _(v):
            v.wait_ge(self.sems["eS"], self.counts["eS"])
            v.wait_ge(self.sems["eG"], self.counts["eG"])
            v.tensor_scalar_mul(t["c_negA"][:], t["c_negA"][:], -1.0)
            v.memset(t["c_eps"][:], 1e-5)
            v.memset(t["c_eps6"][:], 1e-6)
            v.tensor_scalar_add(t["c_trash0"][:], t["c_iota"][:], SLOTS0)
            v.tensor_scalar_add(t["c_trash1"][:], t["c_iota"][:], SLOTS1)
            v.tensor_copy(out=t["c_trash0f"][:], in_=t["c_trash0"][:])
            v.tensor_copy(out=t["c_trash1f"][:], in_=t["c_trash1"][:])
            ins = v.memset(t["tmpZ"][:], 0.0)
            self.inc(ins, "eV")
        z_eV = self.counts["eV"]

        @block.scalar
        def _(sc):
            sc.wait_ge(self.sems["eV"], z_eV)
            self.dma(sc, t["eo_dram"][SLOTS0:SLOTS0 + TRASH, :], t["tmpZ"][:], "sA")
            for c in range((SLOTS0 + TRASH) // 128):
                self.dma(sc, t["disp_dram"][c * 128:(c + 1) * 128, :],
                         t["tmpZ"][:], "sA")

        @block.gpsimd
        def _(g):
            g.wait_ge(self.sems["eV"], z_eV)
            self.dma(g, t["cc_in"], t["tmpZ"][0:1, 0:64], "sG")

    def ph_ln_park(self):
        t, block = self.t, self.block
        self.sbl("ln_x", (128, 2 * DM), BF16)
        self.sbl("ln_xn", (128, 2 * DM), BF16)
        self.sbl("ln_sq", (128, DM), F32)
        self.sbl("ln_cs", (128, DM), F32)
        self.sbl("ln_cq", (128, DM), F32)
        self.sbl("ln_st", (128, 8), F32)
        ld, xn_v = [], {}
        sG0 = self.counts["sG"]
        st = None
        for c in range(NCHUNK):
            b = c % 2

            @block.sync
            def _(sync, c=c, b=b):
                if c >= 2:
                    sync.wait_ge(self.sems["eV"], xn_v[c - 2])
                ld.append(self.dma(sync, t["ln_x"][:, b * DM:(b + 1) * DM],
                                   t["x_dram"][c * 128:(c + 1) * 128, :], "sQ"))

            @block.vector
            def _(v, c=c, b=b):
                xc = t["ln_x"][:, b * DM:(b + 1) * DM]
                v.wait_ge(self.sems["sQ"], ld[c])
                v.tensor_tensor(out=t["ln_sq"][:], in0=xc, in1=xc, op=ALU.mult)
                v.tensor_tensor_scan(out=t["ln_cs"][:], data0=xc, data1=xc,
                                     initial=0.0, op0=ALU.add, op1=ALU.bypass)
                ins = v.tensor_tensor_scan(out=t["ln_cq"][:], data0=t["ln_sq"][:],
                                           data1=t["ln_sq"][:], initial=0.0,
                                           op0=ALU.add, op1=ALU.bypass)
                self.inc(ins, "eV")
            scan_v = self.counts["eV"]

            @block.scalar
            def _(sc, c=c, scan_v=scan_v):
                st = t["ln_st"]
                sc.wait_ge(self.sems["eV"], scan_v)
                sc.activation(out=st[:, 0:1], in_=t["ln_cs"][:, DM - 1:DM],
                              func=ACT.Identity, scale=1.0 / DM)
                ins = sc.activation(out=st[:, 1:2], in_=t["ln_cq"][:, DM - 1:DM],
                                    func=ACT.Identity, scale=1.0 / DM)
                self.inc(ins, "eS")
            ext_s = self.counts["eS"]

            @block.vector
            def _(v, c=c, ext_s=ext_s):
                st = t["ln_st"]
                v.wait_ge(self.sems["eS"], ext_s)
                ins = v.tensor_scalar(out=st[:, 2:3], in0=st[:, 0:1],
                                      scalar1=st[:, 0:1], scalar2=st[:, 1:2],
                                      op0=ALU.mult, op1=ALU.subtract)
                self.inc(ins, "eV")
            var_v = self.counts["eV"]

            @block.scalar
            def _(sc, c=c, var_v=var_v):
                st = t["ln_st"]
                sc.wait_ge(self.sems["eV"], var_v)
                ins = sc.activation(out=st[:, 5:6], in_=st[:, 2:3], func=ACT.Sqrt,
                                    scale=-1.0, bias=t["c_eps"][:, 0:1])
                self.inc(ins, "eS")
            sig_s = self.counts["eS"]

            @block.vector
            def _(v, c=c, sig_s=sig_s):
                st = t["ln_st"]
                v.wait_ge(self.sems["eS"], sig_s)
                ins = v.reciprocal(st[:, 6:7], st[:, 5:6])
                self.inc(ins, "eV")
            rec_v = self.counts["eV"]

            @block.scalar
            def _(sc, c=c, rec_v=rec_v):
                st = t["ln_st"]
                sc.wait_ge(self.sems["eV"], rec_v)
                ins = sc.activation(out=st[:, 7:8], in_=st[:, 6:7],
                                    func=ACT.Identity)
                self.inc(ins, "eS")
            rcp_s = self.counts["eS"]

            @block.vector
            def _(v, c=c, b=b, rcp_s=rcp_s):
                st = t["ln_st"]
                xc = t["ln_x"][:, b * DM:(b + 1) * DM]
                xnc = t["ln_xn"][:, b * DM:(b + 1) * DM]
                v.wait_ge(self.sems["eS"], rcp_s)
                if c >= 2:
                    v.wait_ge(self.sems["sG"], sG0 + 16 * (c - 1))
                ins = v.tensor_scalar(out=xnc, in0=xc, scalar1=st[:, 0:1],
                                      scalar2=st[:, 7:8], op0=ALU.subtract,
                                      op1=ALU.mult)
                xn_v[c] = self.inc(ins, "eV")

            @block.gpsimd
            def _(g, c=c, b=b):
                g.wait_ge(self.sems["eV"], xn_v[c])
                self.dma(g, t["xn_dram"][c * 128:(c + 1) * 128, :],
                         t["ln_xn"][:, b * DM:(b + 1) * DM], "sG")

    def ph_final_wait(self):
        block = self.block
        snap = self.snap()

        @block.sync
        def _(sync):
            self.wait(sync, snap)

    def ph_mamba(self):
        t, block = self.t, self.block
        self.sbl("m_xnf", (128, 2048), BF16)
        self.sbl("m_win", (128, 4096), BF16)
        self.sbl("m_xi", (128, 16 * 260), BF16)
        self.sbl("m_zs", (128, 4096), BF16)
        self.sbl("m_B", (128, 4096), BF16)
        self.sbl("m_C", (128, 4096), BF16)
        self.sbl("m_dtp", (32, 256), F32)
        self.sbl("m_dtt", (32, 256), F32)
        self.sbl("m_dts", (32, 256), F32)
        self.sbl("m_convt", (128, 256), F32)
        self.sbl("m_Xc", (128, 4096), BF16)
        self.sbl("m_S", (128, 4096), F32)
        self.sbl("m_SB", (128, 4096), F32)
        self.sbl("m_Bt", (128, 4096), BF16)
        self.sbl("m_ccC", (128, 2048), BF16)
        self.sbl("m_c128", (128, 4096), BF16)
        self.sbl("m_b128", (128, 4096), BF16)
        self.sbl("m_dcol", (128, 32), F32)
        self.sbl("m_sc", (128, 8192), BF16)
        self.sbl("m_Xtm", (128, 2048), BF16)
        self.sbl("m_Btm", (128, 2048), BF16)
        self.sbl("m_yg", (128, 4096), BF16)
        self.sbl("m_h", (128, 1024), F32)
        self.sbl("m_hbf", (128, 1024), BF16)
        self.sbl("m_stm", (128, 2048), BF16)
        self.sbl("m_wo2", (128, 4096), BF16)
        self.sbl("m_tmp", (128, 512), BF16)
        for w in list(LEVELS) + ["d"]:
            self.sbl(f"m_mb{w}", (128, 512), BF16)
        pb = [self.psl(f"m_pb{i}") for i in range(8)]
        self.tail_v = {}

        @block.sync
        def _(sync):
            for w in list(LEVELS) + ["d"]:
                self.dma(sync, t[f"m_mb{w}"][:], t[f"mbank{w}"], "sQ")

        @block.vector
        def _(v):
            for tl in range(16):
                v.memset(t["m_xi"][:, tl * 260: tl * 260 + 4], 0.0)
            v.memset(t["m_h"][:], 0.0)
            ins = v.memset(t["m_hbf"][:], 0.0)
            self.inc(ins, "eV")

        for seg in range(NSEG):
            self._mamba_segment(seg, pb)

    def _mamba_segment(self, seg, pb):
        t, block = self.t, self.block
        t0 = seg * TSEG
        FG = [(fg * 512, 512) for fg in range(16)] + [(8192, 32)]
        NFG = len(FG)

        seg_snap = {k: self.counts[k] for k in ("eV", "eS", "eT")}
        ld_xnf = [None]
        ld_win = {}
        mm_done = {}
        evac_at = {}

        def emit_win_load(fg):
            flo, width = FG[fg]

            @block.sync
            def _(sync):
                if fg < 1:
                    self.wait(sync, seg_snap)
                if fg >= 1:
                    sync.wait_ge(self.sems["eT"], mm_done[("fg", fg - 1)])
                for dtl in range(8):
                    ld_win[(fg, dtl)] = self.dma(
                        sync, t["m_win"][:, dtl * 512: dtl * 512 + width],
                        t["winbf_dram"][dtl * 128:(dtl + 1) * 128,
                                        flo:flo + width], "sQ")

        @block.sync
        def _(sync):
            self.wait(sync, seg_snap)
            for dtl in range(8):
                ld_xnf.append(self.trdma(
                    sync, t["m_xnf"][:, dtl * 256:(dtl + 1) * 256],
                    t["xn_dram"][t0:t0 + TSEG, dtl * 128:(dtl + 1) * 128], "sQ"))

        emit_win_load(0)
        fcg = 0
        for fg in range(NFG):
            flo, width = FG[fg]
            par = fg % 2
            nsub = max(1, width // 128)

            @block.tensor
            def _(tn, fg=fg, width=width, nsub=nsub, fcg0=fcg):
                if fg == 0:
                    self.wait(tn, seg_snap)
                tn.wait_ge(self.sems["sQ"], ld_xnf[-1])
                tn.wait_ge(self.sems["sQ"], ld_win[(fg, 7)])
                for sub in range(nsub):
                    wq = min(width, 128)
                    bank = pb[(fcg0 + sub) % 4]
                    if fcg0 + sub >= 4:
                        sem, cnt = evac_at[fcg0 + sub - 4]
                        tn.wait_ge(self.sems[sem], cnt)
                    for dtl in range(8):
                        ins = tn.matmul(
                            bank[0:wq, 0:TSEG],
                            t["m_win"][:, dtl * 512 + sub * 128:
                                       dtl * 512 + sub * 128 + wq],
                            t["m_xnf"][:, dtl * 256:(dtl + 1) * 256],
                            start=(dtl == 0), stop=(dtl == 7))
                    mm_done[("fc", fcg0 + sub)] = self.inc(ins, "eT")
                mm_done[("fg", fg)] = self.counts["eT"]

            for sub in range(nsub):
                myfcg = fcg + sub
                bank = pb[myfcg % 4]
                if 16 <= myfcg < 32:
                    @block.scalar
                    def _(sc, myfcg=myfcg, bank=bank):
                        tl = myfcg - 16
                        sc.wait_ge(self.sems["eT"], mm_done[("fc", myfcg)])
                        ins = sc.activation(
                            out=t["m_zs"][:, tl * 256:(tl + 1) * 256],
                            in_=bank[:, 0:TSEG], func=ACT.Silu)
                        evac_at[myfcg] = ("eS", self.inc(ins, "eS"))
                else:
                    @block.vector
                    def _(v, myfcg=myfcg, bank=bank):
                        v.wait_ge(self.sems["eT"], mm_done[("fc", myfcg)])
                        if myfcg < 16:
                            dst = t["m_xi"][:, myfcg * 260 + 4: myfcg * 260 + 260]
                            ins = v.tensor_copy(out=dst, in_=bank[:, 0:TSEG])
                        elif myfcg < 48:
                            tl = myfcg - 32
                            ins = v.tensor_copy(
                                out=t["m_B"][:, tl * 256:(tl + 1) * 256],
                                in_=bank[:, 0:TSEG])
                        elif myfcg < 64:
                            tl = myfcg - 48
                            ins = v.tensor_copy(
                                out=t["m_C"][:, tl * 256:(tl + 1) * 256],
                                in_=bank[:, 0:TSEG])
                        else:
                            ins = v.tensor_copy(out=t["m_dtp"][:],
                                                in_=bank[0:32, 0:TSEG])
                        evac_at[myfcg] = ("eV", self.inc(ins, "eV"))
            fcg += nsub
            if fg + 1 < NFG:
                emit_win_load(fg + 1)

        @block.tensor
        def _(tn):
            for dtl in range(8):
                ins = tn.matmul(pb[4][0:E0, 0:TSEG],
                                t["c_wg0"][:, dtl * E0:(dtl + 1) * E0],
                                t["m_xnf"][:, dtl * 256:(dtl + 1) * 256],
                                start=(dtl == 0), stop=(dtl == 7))
            mm_done[("lg",)] = self.inc(ins, "eT")
            mm_done[("mseg", seg)] = self.counts["eT"]

        @block.vector
        def _(v):
            v.wait_ge(self.sems["eT"], mm_done[("lg",)])
            ins = v.tensor_copy(out=t["lg_fm"][0:E0, t0:t0 + TSEG],
                                in_=pb[4][0:E0, 0:TSEG])
            self.inc(ins, "eV")

        @block.scalar
        def _(sc):
            sem, cnt = evac_at[64]
            sc.wait_ge(self.sems[sem], cnt)
            sc.activation(out=t["m_dtt"][:], in_=t["m_dtp"][:], func=ACT.Abs,
                          bias=t["c_dtb"][:, 0:1])
            sc.activation(out=t["m_dtt"][:], in_=t["m_dtt"][:], func=ACT.Exp,
                          scale=-1.0)
            ins = sc.activation(out=t["m_dtt"][:], in_=t["m_dtt"][:], func=ACT.Ln,
                                bias=1.0)
            self.inc(ins, "eS")
        dt_eS = self.counts["eS"]

        @block.vector
        def _(v):
            v.wait_ge(self.sems["eS"], dt_eS)
            v.tensor_scalar(out=t["m_dts"][:], in0=t["m_dtp"][:],
                            scalar1=t["c_dtb"][:, 0:1], scalar2=0.0,
                            op0=ALU.add, op1=ALU.max)
            ins = v.tensor_add(t["m_dts"][:], t["m_dts"][:], t["m_dtt"][:])
            self.inc(ins, "eV")
        dt_eV = self.counts["eV"]

        conv_v = []
        for tl in range(16):
            @block.vector
            def _(v, tl=tl):
                if tl == 0:
                    sem, cnt = evac_at[15]
                    v.wait_ge(self.sems[sem], cnt)
                base = tl * 260

                def sl(k):
                    return t["m_xi"][:, base + 1 + k: base + 1 + k + TSEG]
                dst = t["m_convt"][:]
                v.tensor_scalar(out=dst, in0=sl(0),
                                scalar1=t["c_convw"][:, 4 * tl: 4 * tl + 1],
                                scalar2=None, op0=ALU.mult)
                for k in (1, 2, 3):
                    v.scalar_tensor_tensor(
                        out=dst, in0=sl(k),
                        scalar=t["c_convw"][:, 4 * tl + k: 4 * tl + k + 1],
                        in1=dst, op0=ALU.mult, op1=ALU.add)
                ins = v.tensor_copy(out=t["m_xi"][:, base + 1: base + 4],
                                    in_=t["m_xi"][:, base + 257: base + 260])
                conv_v.append(self.inc(ins, "eV"))

            @block.scalar
            def _(sc, tl=tl):
                sc.wait_ge(self.sems["eV"], conv_v[tl])
                ins = sc.activation(out=t["m_Xc"][:, tl * 256:(tl + 1) * 256],
                                    in_=t["m_convt"][:], func=ACT.Silu,
                                    bias=t["c_convb"][:, tl:tl + 1])
                self.inc(ins, "eS")
            cv_cnt = self.counts["eS"]

            @block.vector
            def _(v, cv_cnt=cv_cnt):
                v.wait_ge(self.sems["eS"], cv_cnt)
        xc_eS = self.counts["eS"]

        dtrep_v = []
        for tl in range(16):
            @block.tensor
            def _(tn, tl=tl):
                bank = pb[5 + (tl % 2)]
                if tl == 0:
                    tn.wait_ge(self.sems["eV"], dt_eV)
                if tl >= 2:
                    tn.wait_ge(self.sems["eV"], dtrep_v[tl - 2])
                ins = tn.matmul(bank[:, 0:TSEG],
                                t["c_hsel"][:, tl * 128:(tl + 1) * 128],
                                t["m_dts"][:], start=True, stop=True)
                self.inc(ins, "eT")
            mmc = self.counts["eT"]

            @block.vector
            def _(v, tl=tl, mmc=mmc):
                bank = pb[5 + (tl % 2)]
                v.wait_ge(self.sems["eT"], mmc)
                v.tensor_scalar(out=t["m_S"][:, tl * 256:(tl + 1) * 256],
                                in0=bank[:, 0:TSEG],
                                scalar1=t["c_negA"][:, tl:tl + 1],
                                scalar2=None, op0=ALU.mult)
                ins = v.tensor_tensor(out=t["m_B"][:, tl * 256:(tl + 1) * 256],
                                      in0=t["m_B"][:, tl * 256:(tl + 1) * 256],
                                      in1=bank[:, 0:TSEG], op=ALU.mult)
                dtrep_v.append(self.inc(ins, "eV"))

        @block.vector
        def _(v):
            v.wait_ge(self.sems["eT"], self.counts["eT"])
            v.memset(t["m_SB"][:], 0.0)
            v.memset(t["m_sc"][:], 0.0)
            ins = v.tensor_copy(out=t["m_Bt"][:], in_=t["m_B"][:])
            self.inc(ins, "eV")
        chain_eV = self.counts["eV"]

        p0_mm = {}
        for c in range(2):
            for tl in range(16):
                @block.tensor
                def _(tn, c=c, tl=tl):
                    if c == 0 and tl == 0:
                        tn.wait_ge(self.sems["eV"], chain_eV)
                        tn.wait_ge(self.sems["eS"], xc_eS)
                    if c == 1 and tl == 0:
                        tn.wait_ge(self.sems["eV"], self.counts["eV"])
                    for a in range(2):
                        q = tl * 2 + a
                        bank = pb[q % 8]
                        slot = q // 8
                        ins = tn.matmul(
                            bank[:, slot * 128:(slot + 1) * 128],
                            t["m_B"][64 * a:64 * a + 64,
                                     tl * 256 + c * 128: tl * 256 + c * 128 + 128],
                            t["m_C"][64 * a:64 * a + 64,
                                     tl * 256 + c * 128: tl * 256 + c * 128 + 128],
                            start=True, stop=True)
                        p0_mm[(c, q)] = self.inc(ins, "eT")
            for j in range(8):
                @block.vector
                def _(v, c=c, j=j):
                    v.wait_ge(self.sems["eT"], p0_mm[(c, 8 * 3 + j)])
                    bank = pb[j]
                    v.tensor_tensor(out=t["m_tmp"][:], in0=bank[:],
                                    in1=t["m_mbd"][:], op=ALU.mult)
                    dstv = t["m_sc"][:].rearrange("p (s j t) -> p s j t", j=8, t=256)
                    dstv = dstv[:, :, j, c * 128:(c + 1) * 128]
                    ins = v.tensor_tensor(out=dstv, in0=dstv, in1=t["m_tmp"][:],
                                          op=ALU.add)
                    self.inc(ins, "eV")
        p0_eV = self.counts["eV"]

        self._mamba_levels(seg, pb, p0_eV)

        self._mamba_tail(seg, pb)


    def _mamba_levels(self, seg, pb, p0_eV):
        t, block = self.t, self.block

        def parity_view(buf, w, par):
            u = 4096 // (2 * w)
            return buf[:].rearrange("p (u two w) -> p u two w", two=2, w=w)[:, :, par, :]

        def endcol_view(w, par):
            u = 4096 // (2 * w)
            v = t["m_S"][:].rearrange("p (u two w) -> p u two w", two=2, w=w)
            return v[:, :, par, w - 1:w].broadcast_to((128, u, w))

        prev_sel_eV = p0_eV
        for li, w in enumerate(LEVELS):
            u = 4096 // (2 * w)
            @block.scalar
            def _(sc, w=w, u=u):
                sc.wait_ge(self.sems["eV"], self.counts["eV"])
                sc.wait_ge(self.sems["eT"], self.counts["eT"])
                ins = sc.activation(
                    out=t["m_ccC"][:].rearrange("p (u w) -> p u w", w=w),
                    in_=parity_view(t["m_S"], w, 1), func=ACT.Exp)
                self.inc(ins, "eS")
            ccC_eS = self.counts["eS"]

            @block.vector
            def _(v, w=w, u=u, ccC_eS=ccC_eS):
                v.wait_ge(self.sems["eS"], ccC_eS)
                ins = v.tensor_tensor(
                    out=t["m_ccC"][:].rearrange("p (u w) -> p u w", w=w),
                    in0=parity_view(t["m_C"], w, 1),
                    in1=t["m_ccC"][:].rearrange("p (u w) -> p u w", w=w),
                    op=ALU.mult)
                self.inc(ins, "eV")

            if w > 1:
                wv = w // 2
                @block.scalar
                def _(sc, wv=wv):
                    sc.wait_ge(self.sems["eV"], self.counts["eV"])
                    sc.wait_ge(self.sems["eT"], self.counts["eT"])
                    ins = sc.activation(
                        out=parity_view(t["m_Bt"], wv, 0),
                        in_=parity_view(t["m_SB"], wv, 0), func=ACT.Exp)
                    self.inc(ins, "eS")
                ebt_eS = self.counts["eS"]

                @block.vector
                def _(v, wv=wv, ebt_eS=ebt_eS):
                    v.wait_ge(self.sems["eS"], ebt_eS)
                    ins = v.tensor_tensor(
                        out=parity_view(t["m_Bt"], wv, 0),
                        in0=parity_view(t["m_B"], wv, 0),
                        in1=parity_view(t["m_Bt"], wv, 0),
                        op=ALU.mult)
                    self.inc(ins, "eV")
            bt_eV = self.counts["eV"]

            @block.vector
            def _(v, w=w, bt_eV=bt_eV):
                v.wait_ge(self.sems["eV"], bt_eV)
                v.tensor_tensor(out=parity_view(t["m_SB"], w, 0),
                                in0=parity_view(t["m_SB"], w, 0),
                                in1=endcol_view(w, 1), op=ALU.add)
                ins = v.tensor_tensor(out=parity_view(t["m_S"], w, 1),
                                      in0=parity_view(t["m_S"], w, 1),
                                      in1=endcol_view(w, 0), op=ALU.add)
                self.inc(ins, "eV")

            for c in range(2):
                lvl_mm = {}
                for tl in range(16):
                    @block.tensor
                    def _(tn, c=c, tl=tl, w=w, li=li):
                        if tl == 0:
                            tn.wait_ge(self.sems["eV"], prev_sel_eV)
                            tn.wait_ge(self.sems["eV"], self.counts["eV"])
                        for a in range(2):
                            q = tl * 2 + a
                            bank = pb[4 * c + q % 4]
                            r = q // 4
                            ins = tn.matmul(
                                bank[:, r * 64:(r + 1) * 64],
                                t["m_Bt"][64 * a:64 * a + 64,
                                          tl * 256 + c * 128: tl * 256 + c * 128 + 128],
                                t["m_ccC"][64 * a:64 * a + 64,
                                           tl * 128 + c * 64: tl * 128 + c * 64 + 64],
                                start=True, stop=True)
                            lvl_mm[(c, q)] = self.inc(ins, "eT")
                for j in range(4):
                    @block.vector
                    def _(v, c=c, j=j, w=w):
                        v.wait_ge(self.sems["eT"], lvl_mm[(c, 28 + j)])
                        bank = pb[4 * c + j]
                        v.tensor_tensor(out=t["m_tmp"][:], in0=bank[:],
                                        in1=t[f"m_mb{w}"][:], op=ALU.mult)
                        dstv = t["m_sc"][:].rearrange("p (s j t) -> p s j t",
                                                      j=4, t=256)
                        dstv = dstv[:, :, j, c * 128:(c + 1) * 128]
                        dstv = dstv.rearrange("p s (nb two w) -> p s nb two w",
                                              two=2, w=w)[:, :, :, 1, :]
                        ins = v.tensor_tensor(out=dstv, in0=dstv,
                                              in1=t["m_tmp"][:], op=ALU.add)
                        self.inc(ins, "eV")
            prev_sel_eV = self.counts["eV"]

    def _mamba_tail(self, seg, pb):
        t, block = self.t, self.block
        t0 = seg * TSEG

        @block.scalar
        def _(sc):
            sc.wait_ge(self.sems["eV"], self.counts["eV"])
            sc.wait_ge(self.sems["eT"], self.counts["eT"])
            ins = sc.activation(out=t["m_c128"][:], in_=t["m_S"][:], func=ACT.Exp)
            self.inc(ins, "eS")
        c128_eS = self.counts["eS"]

        @block.vector
        def _(v):
            v.wait_ge(self.sems["eS"], c128_eS)
            for tl in range(16):
                for c in range(2):
                    v.tensor_copy(
                        out=t["m_dcol"][:, tl * 2 + c: tl * 2 + c + 1],
                        in_=t["m_c128"][:, tl * 256 + c * 128 + 127:
                                        tl * 256 + c * 128 + 128])
            ins = v.tensor_tensor(out=t["m_c128"][:], in0=t["m_C"][:],
                                  in1=t["m_c128"][:], op=ALU.mult)
            self.inc(ins, "eV")

        @block.scalar
        def _(sc):
            sc.wait_ge(self.sems["eV"], self.counts["eV"])
            sc.wait_ge(self.sems["eT"], self.counts["eT"])
            ins = sc.activation(out=t["m_b128"][:], in_=t["m_SB"][:], func=ACT.Exp)
            self.inc(ins, "eS")
        b128_eS = self.counts["eS"]

        @block.vector
        def _(v):
            v.wait_ge(self.sems["eS"], b128_eS)
            ins = v.tensor_tensor(out=t["m_b128"][:], in0=t["m_B"][:],
                                  in1=t["m_b128"][:], op=ALU.mult)
            self.inc(ins, "eV")
        b128_eV = self.counts["eV"]

        @block.gpsimd
        def _(g):
            g.wait_ge(self.sems["eV"], b128_eV)
            self.dma(g, t["parkX_dram"].rearrange("(tl p) t -> p tl t", p=128),
                     t["m_Xc"][:].rearrange("p (tl t) -> p tl t", t=256), "sG")
            self.dma(g, t["parkB_dram"].rearrange("(tl p) t -> p tl t", p=128),
                     t["m_b128"][:].rearrange("p (tl t) -> p tl t", t=256), "sG")
        park_sG = self.counts["sG"]

        for c in range(2):
            @block.sync
            def _(sync, c=c):
                sync.wait_ge(self.sems["sG"], park_sG)
                if c == 1:
                    sync.wait_ge(self.sems["eT"], self.counts["eT"])
                self.trdma(sync, t["m_Xtm"][:],
                           t["parkX_dram"][:, c * 128:(c + 1) * 128], "sQ")
                self.trdma(sync, t["m_Btm"][:],
                           t["parkB_dram"][:, c * 128:(c + 1) * 128], "sQ")
            tm_sQ = self.counts["sQ"]

            for tl in range(16):
                @block.tensor
                def _(tn, c=c, tl=tl, tm_sQ=tm_sQ):
                    pbF, pbY, pbS = pb[(tl % 2) * 3], pb[(tl % 2) * 3 + 1], pb[(tl % 2) * 3 + 2]
                    if tl == 0:
                        tn.wait_ge(self.sems["sQ"], tm_sQ)
                        tn.wait_ge(self.sems["eV"], self.counts["eV"])
                    if tl >= 2:
                        tn.wait_ge(self.sems["eV"], self.tail_v[(c, tl - 2)])
                    for a in range(2):
                        tn.matmul(pbF[64 * a:64 * a + 64, 0:128],
                                  t["m_hbf"][64 * a:64 * a + 64, tl * 64:(tl + 1) * 64],
                                  t["m_c128"][64 * a:64 * a + 64,
                                              tl * 256 + c * 128: tl * 256 + c * 128 + 128],
                                  start=True, stop=True)
                        tn.matmul(pbY[64 * a:64 * a + 64, 0:128],
                                  t["m_Xtm"][:, tl * 128 + 64 * a: tl * 128 + 64 * a + 64],
                                  t["m_sc"][:, (tl * 2 + a) * 256 + c * 128:
                                            (tl * 2 + a) * 256 + c * 128 + 128],
                                  start=True, stop=True)
                        ins = tn.matmul(pbS[64 * a:64 * a + 64, 0:64],
                                        t["m_Btm"][:, tl * 128 + 64 * a: tl * 128 + 64 * a + 64],
                                        t["m_Xtm"][:, tl * 128 + 64 * a: tl * 128 + 64 * a + 64],
                                        start=True, stop=True)
                    self.inc(ins, "eT")
                mm_cnt = self.counts["eT"]

                @block.vector
                def _(v, c=c, tl=tl, mm_cnt=mm_cnt):
                    pbF, pbY, pbS = pb[(tl % 2) * 3], pb[(tl % 2) * 3 + 1], pb[(tl % 2) * 3 + 2]
                    v.wait_ge(self.sems["eT"], mm_cnt)
                    ygs = t["m_yg"][:, tl * 256 + c * 128: tl * 256 + c * 128 + 128]
                    v.scalar_tensor_tensor(out=ygs,
                                           in0=t["m_Xc"][:, tl * 256 + c * 128:
                                                         tl * 256 + c * 128 + 128],
                                           scalar=t["c_D"][:, tl:tl + 1],
                                           in1=pbF[:, 0:128],
                                           op0=ALU.mult, op1=ALU.add)
                    v.tensor_tensor(out=ygs, in0=ygs, in1=pbY[:, 0:128], op=ALU.add)
                    v.tensor_tensor(out=ygs, in0=ygs,
                                    in1=t["m_zs"][:, tl * 256 + c * 128:
                                                  tl * 256 + c * 128 + 128],
                                    op=ALU.mult)
                    v.scalar_tensor_tensor(out=t["m_h"][:, tl * 64:(tl + 1) * 64],
                                           in0=t["m_h"][:, tl * 64:(tl + 1) * 64],
                                           scalar=t["m_dcol"][:, tl * 2 + c: tl * 2 + c + 1],
                                           in1=pbS[:, 0:64],
                                           op0=ALU.mult, op1=ALU.add)
                    ins = v.tensor_copy(out=t["m_hbf"][:, tl * 64:(tl + 1) * 64],
                                        in_=t["m_h"][:, tl * 64:(tl + 1) * 64])
                    self.tail_v[(c, tl)] = self.inc(ins, "eV")

        for dq in range(4):
            @block.gpsimd
            def _(g, dq=dq):
                g.wait_ge(self.sems["eT"], self.counts["eT"])
                for ft in range(16):
                    self.dma(g, t["m_wo2"][:, ft * 256:(ft + 1) * 256],
                             t["woutbf_dram"][ft * 128:(ft + 1) * 128,
                                              dq * 256:(dq + 1) * 256], "sG")
            wo_sG = self.counts["sG"]
            for c in range(2):
                @block.tensor
                def _(tn, dq=dq, c=c, wo_sG=wo_sG):
                    tn.wait_ge(self.sems["sG"], wo_sG)
                    if dq == 0 and c == 0:
                        tn.wait_ge(self.sems["eV"], self.tail_v[(1, 15)])
                    if (dq, c) != (0, 0):
                        tn.wait_ge(self.sems["eV"], self.counts["eV"])
                    bank = pb[6 + c]
                    for tl in range(16):
                        ins = tn.matmul(bank[:, 0:256],
                                        t["m_yg"][:, tl * 256 + c * 128:
                                                  tl * 256 + c * 128 + 128],
                                        t["m_wo2"][:, tl * 256:(tl + 1) * 256],
                                        start=(tl == 0), stop=(tl == 15))
                    self.inc(ins, "eT")
                op_eT = self.counts["eT"]

                @block.vector
                def _(v, dq=dq, c=c, op_eT=op_eT):
                    v.wait_ge(self.sems["eT"], op_eT)
                    ins = v.tensor_copy(
                        out=t["m_stm"][:, c * 1024 + dq * 256:
                                       c * 1024 + dq * 256 + 256],
                        in_=pb[6 + c][:, 0:256])
                    self.inc(ins, "eV")

        @block.scalar
        def _(sc):
            sc.wait_ge(self.sems["eV"], self.counts["eV"])
            for c in range(2):
                self.dma(sc, t["s_dram"][t0 + c * 128: t0 + (c + 1) * 128, :],
                         t["m_stm"][:, c * 1024:(c + 1) * 1024], "sA")

    def ph_attn(self, upto="full", nseg=NSEG, plainfeat=False, skip=()):
        t, block = self.t, self.block
        for n in ("aw_q", "aw_k", "aw_v", "aw_o"):
            self.sbl(n, (128, 8 * DM), BF16)
        self.sbl("a_dd", (128, 128), BF16)
        self.sbl("a_ud", (128, 128), BF16)
        self.sbl("a_xn", (128, 8 * TSEG), BF16)
        self.sbl("a_q", (128, 8 * TSEG), BF16)
        self.sbl("a_k", (128, 8 * TSEG), BF16)
        self.sbl("a_t1", (128, TSEG), F32)
        self.sbl("a_t2", (128, TSEG), F32)
        self.sbl("a_t3", (128, TSEG), F32)
        self.sbl("a_z1", (128, TSEG), F32)
        self.sbl("a_z2", (128, TSEG), F32)
        self.sbl("a_cum", (128, 8 * TSEG), F32)
        self.sbl("a_zsb", (16, TSEG), F32)
        self.sbl("a_zrep", (128, 8 * TSEG), F32)
        self.sbl("a_ktm", (128, 2 * DM), BF16)
        self.sbl("a_vtm", (128, 2 * DM), BF16)
        self.sbl("a_sc", (128, 16 * 128), BF16)
        self.sbl("a_y", (128, 2 * DM), BF16)
        self.sbl("a_S", (128, 8 * 64), F32)
        self.sbl("a_Sbf", (128, 8 * 128), BF16)
        self.sbl("a_s", (128, DM), BF16)
        pb = [self.psl(f"a_pb{i}") for i in range(8)]
        wmap = {"aw_q": "Wq", "aw_k": "Wk", "aw_v": "Wv", "aw_o": "Wo"}

        @block.gpsimd
        def _(g):
            for dst, src in wmap.items():
                for kt in range(8):
                    self.dma(g, t[dst][:, kt * DM:(kt + 1) * DM],
                             t[src][kt * 128:(kt + 1) * 128, :], "sG")

        @block.sync
        def _(sync):
            self.dma(sync, t["a_dd"][:], t["dd_bf"], "sQ")
            self.dma(sync, t["a_ud"][:], t["ud_bf"], "sQ")

        @block.vector
        def _(v):
            v.memset(t["a_S"][:], 0.0)
            ins = v.memset(t["a_Sbf"][:], 0.0)
            self.inc(ins, "eV")

        sbf = {tt: 0 for tt in range(8)}
        for seg in range(nseg):
            t0 = seg * TSEG
            @block.sync
            def _(sync, seg=seg):
                sync.wait_ge(self.sems["eT"], self.counts["eT"])
                for dtl in range(8):
                    self.trdma(
                        sync, t["a_xn"][:, dtl * TSEG:(dtl + 1) * TSEG],
                        t["xn_dram"][t0:t0 + TSEG, dtl * 128:(dtl + 1) * 128], "sQ")
            xn_sQ = self.counts["sQ"]

            fz = {}
            for st, (wsb, dst, b0) in enumerate(
                    [("aw_q", "a_q", 0), ("aw_k", "a_k", 2)]):
                for m in range(8):
                    @block.tensor
                    def _(tn, st=st, m=m, wsb=wsb, b0=b0, seg=seg, xn_sQ=xn_sQ):
                        bank = pb[b0 + m % 2]
                        if st == 0 and m == 0:
                            tn.wait_ge(self.sems["sQ"], xn_sQ)
                            tn.wait_ge(self.sems["eV"], self.counts["eV"])
                            if seg == 0:
                                tn.wait_ge(self.sems["sG"], self.counts["sG"])
                        if (st, m) >= (0, 2):
                            key = (st, m - 2) if m >= 2 else (st - 1, 6 + m)
                            tn.wait_ge(self.sems["eV"], fz[key][2])
                        for kt in range(8):
                            ins = tn.matmul(
                                bank[:, 0:TSEG],
                                t[wsb][:, kt * DM + m * 128: kt * DM + m * 128 + 128],
                                t["a_xn"][:, kt * TSEG:(kt + 1) * TSEG],
                                start=(kt == 0), stop=(kt == 7))
                        self.inc(ins, "eT")
                    mmc = self.counts["eT"]

                    @block.vector
                    def _(v, st=st, m=m, b0=b0, mmc=mmc):
                        bank = pb[b0 + m % 2]
                        v.wait_ge(self.sems["eT"], mmc)
                        if (st, m) > (0, 0):
                            pk = (st, m - 1) if m >= 1 else (st - 1, 7)
                            v.wait_ge(self.sems["eS"], fz[pk][1])
                        v.tensor_single_scalar(out=t["a_t1"][:], in_=bank[:, 0:TSEG],
                                               scalar=0.0, op=ALU.min)
                        ins = v.tensor_single_scalar(out=t["a_t3"][:], in_=bank[:, 0:TSEG],
                                                     scalar=0.0, op=ALU.max)
                        c1 = self.inc(ins, "eV")
                        fz.setdefault((st, m), [c1, 0, 0])[0] = c1

                    @block.scalar
                    def _(sc, st=st, m=m):
                        sc.wait_ge(self.sems["eV"], fz[(st, m)][0])
                        ins = sc.activation(out=t["a_t2"][:], in_=t["a_t1"][:],
                                            func=ACT.Exp)
                        fz[(st, m)][1] = self.inc(ins, "eS")

                    @block.vector
                    def _(v, st=st, m=m, b0=b0, dst=dst, plainfeat=plainfeat):
                        bank = pb[b0 + m % 2]
                        v.wait_ge(self.sems["eS"], fz[(st, m)][1])
                        if plainfeat:
                            ins = v.tensor_copy(
                                out=t[dst][:, m * TSEG:(m + 1) * TSEG],
                                in_=bank[:, 0:TSEG])
                        else:
                            ins = v.tensor_tensor(
                                out=t[dst][:, m * TSEG:(m + 1) * TSEG],
                                in0=t["a_t3"][:], in1=t["a_t2"][:], op=ALU.add)
                        fz[(st, m)][2] = self.inc(ins, "eV")
            feat_eV = self.counts["eV"]

            v_v = {}
            for c in range(2):
                for dh in range(2):
                    @block.tensor
                    def _(tn, c=c, dh=dh, seg=seg, xn_sQ=xn_sQ):
                        bank = pb[4 + c * 2 + dh]
                        tn.wait_ge(self.sems["sQ"], xn_sQ)
                        tn.wait_ge(self.sems["eV"], self.counts["eV"])
                        for kt in range(8):
                            ins = tn.matmul(
                                bank[:, 0:512],
                                t["a_xn"][:, kt * TSEG + c * 128: kt * TSEG + c * 128 + 128],
                                t["aw_v"][:, kt * DM + dh * 512: kt * DM + dh * 512 + 512],
                                start=(kt == 0), stop=(kt == 7))
                        self.inc(ins, "eT")
                    mmc = self.counts["eT"]

                    @block.vector
                    def _(v, c=c, dh=dh, mmc=mmc):
                        bank = pb[4 + c * 2 + dh]
                        v.wait_ge(self.sems["eT"], mmc)
                        ins = v.tensor_copy(
                            out=t["a_vtm"][:, c * DM + dh * 512: c * DM + dh * 512 + 512],
                            in_=bank[:, 0:512])
                        v_v[(c, dh)] = self.inc(ins, "eV")

            @block.tensor
            def _(tn, seg=seg):
                bank = pb[1]
                tn.wait_ge(self.sems["eV"], feat_eV)
                for kt in range(8):
                    ins = tn.matmul(bank[0:16, 0:TSEG],
                                    t["c_gw1"][:, kt * E1:(kt + 1) * E1],
                                    t["a_xn"][:, kt * TSEG:(kt + 1) * TSEG],
                                    start=(kt == 0), stop=(kt == 7))
                self.inc(ins, "eT")
            lg_eT = self.counts["eT"]

            @block.vector
            def _(v, seg=seg, lg_eT=lg_eT):
                v.wait_ge(self.sems["eT"], lg_eT)
                ins = v.tensor_copy(out=t["lg_fm"][0:16, t0:t0 + TSEG],
                                    in_=pb[1][0:16, 0:TSEG])
                self.inc(ins, "eV")
            lgv_eV = self.counts["eV"]

            if upto == "proj":
                continue
            @block.gpsimd
            def _(g, seg=seg):
                g.wait_ge(self.sems["eV"], feat_eV)
                self.dma(g, t["parkK_dram"].rearrange("(tl p) t -> p tl t", p=128),
                         t["a_k"][:].rearrange("p (tl t) -> p tl t", t=TSEG), "sG")
            park_sG = self.counts["sG"]

            @block.sync
            def _(sync, seg=seg, park_sG=park_sG):
                sync.wait_ge(self.sems["sG"], park_sG)
                sync.wait_ge(self.sems["eT"], self.counts["eT"])
                for c in range(2):
                    self.trdma(sync, t["a_ktm"][:, c * DM:(c + 1) * DM],
                               t["parkK_dram"][:, c * 128:(c + 1) * 128], "sQ")
            ktm_sQ = self.counts["sQ"]

            cum_v = {}
            for tp in range(8):
                @block.tensor
                def _(tn, tp=tp, lgv_eV=lgv_eV):
                    bank = pb[tp % 2]
                    if tp < 2:
                        tn.wait_ge(self.sems["eV"], lgv_eV)
                    else:
                        tn.wait_ge(self.sems["eV"], cum_v[tp - 2])
                    for tt in range(tp + 1):
                        lhs = t["a_dd"] if tt < tp else t["a_ud"]
                        ins = tn.matmul(bank[:, 0:TSEG], lhs[:],
                                        t["a_k"][:, tt * TSEG:(tt + 1) * TSEG],
                                        start=(tt == 0), stop=(tt == tp))
                    self.inc(ins, "eT")
                mmc = self.counts["eT"]

                @block.vector
                def _(v, tp=tp, mmc=mmc):
                    bank = pb[tp % 2]
                    v.wait_ge(self.sems["eT"], mmc)
                    ins = v.tensor_copy(out=t["a_cum"][:, tp * TSEG:(tp + 1) * TSEG],
                                        in_=bank[:, 0:TSEG])
                    cum_v[tp] = self.inc(ins, "eV")

            if upto == "cum":
                continue
            zp_v = {}
            zmm = {}
            for tt in range(8):
                @block.vector
                def _(v, tt=tt):
                    buf = "a_z1" if tt % 2 == 0 else "a_z2"
                    v.wait_ge(self.sems["eV"], cum_v[tt])
                    if tt >= 2:
                        v.wait_ge(self.sems["eT"], zmm[tt - 2])
                    ins = v.tensor_tensor(out=t[buf][:],
                                          in0=t["a_q"][:, tt * TSEG:(tt + 1) * TSEG],
                                          in1=t["a_cum"][:, tt * TSEG:(tt + 1) * TSEG],
                                          op=ALU.mult)
                    zp_v[tt] = self.inc(ins, "eV")

                @block.tensor
                def _(tn, tt=tt):
                    buf = "a_z1" if tt % 2 == 0 else "a_z2"
                    tn.wait_ge(self.sems["eV"], zp_v[tt])
                    ins = tn.matmul(pb[3][0:16, 0:TSEG],
                                    t["c_bones16"][:, tt * 16:(tt + 1) * 16],
                                    t[buf][:], start=(tt == 0), stop=(tt == 7))
                    zmm[tt] = self.inc(ins, "eT")

            @block.vector
            def _(v, seg=seg):
                v.wait_ge(self.sems["eT"], zmm[7])
                v.tensor_copy(out=t["a_zsb"][0:16, :], in_=pb[3][0:16, 0:TSEG])
                v.tensor_scalar_add(t["a_zsb"][0:16, :], t["a_zsb"][0:16, :], 1e-6)
                ins = v.reciprocal(t["a_zsb"][0:16, :], t["a_zsb"][0:16, :])
                self.inc(ins, "eV")
            zsb_eV = self.counts["eV"]

            zr_v = {}
            for tt in range(8):
                @block.tensor
                def _(tn, tt=tt):
                    bank = pb[tt % 2]
                    tn.wait_ge(self.sems["eV"], zsb_eV if tt < 2 else zr_v[tt - 2])
                    ins = tn.matmul(bank[:, 0:TSEG],
                                    t["c_zsel"][:, tt * 128:(tt + 1) * 128],
                                    t["a_zsb"][0:16, :], start=True, stop=True)
                    self.inc(ins, "eT")
                mmc = self.counts["eT"]

                @block.vector
                def _(v, tt=tt, mmc=mmc):
                    bank = pb[tt % 2]
                    v.wait_ge(self.sems["eT"], mmc)
                    ins = v.tensor_copy(out=t["a_zrep"][:, tt * TSEG:(tt + 1) * TSEG],
                                        in_=bank[:, 0:TSEG])
                    zr_v[tt] = self.inc(ins, "eV")

            if upto == "z":
                continue
            for c in range(2):
                sc_v, y_v = {}, {}
                for tt in range(8):
                    qs = t["a_q"][:, tt * TSEG + c * 128: tt * TSEG + c * 128 + 128]
                    ks = t["a_k"][:, tt * TSEG + c * 128: tt * TSEG + c * 128 + 128]

                    @block.tensor
                    def _(tn, c=c, tt=tt, qs=qs, ks=ks, ktm_sQ=ktm_sQ):
                        pbP, pbQ = pb[(tt % 2) * 2], pb[(tt % 2) * 2 + 1]
                        if tt < 2:
                            tn.wait_ge(self.sems["sQ"], ktm_sQ)
                            tn.wait_ge(self.sems["eV"], self.counts["eV"])
                        else:
                            tn.wait_ge(self.sems["eV"], y_v[tt - 2])
                        tn.wait_ge(self.sems["eS"], sbf[tt])
                        for a in range(2):
                            tn.matmul(pbQ[64 * a:64 * a + 64, 0:128],
                                      t["a_Sbf"][:, tt * 128 + 64 * a:
                                                 tt * 128 + 64 * a + 64], qs,
                                      start=True, stop=True)
                        tn.matmul(pbP[0:128, 0:128], ks[0:64, :], qs[0:64, :],
                                  start=True, stop=True)
                        ins = tn.matmul(pbQ[0:128, 256:384], ks[64:128, :],
                                        qs[64:128, :], start=True, stop=True)
                        self.inc(ins, "eT")
                    mm1 = self.counts["eT"]

                    @block.vector
                    def _(v, c=c, tt=tt, mm1=mm1):
                        pbP, pbQ = pb[(tt % 2) * 2], pb[(tt % 2) * 2 + 1]
                        v.wait_ge(self.sems["eT"], mm1)
                        v.tensor_tensor(out=t["a_sc"][:, (tt * 2) * 128:(tt * 2) * 128 + 128],
                                        in0=pbP[:, 0:128], in1=t["c_triu"][:],
                                        op=ALU.mult)
                        ins = v.tensor_tensor(
                            out=t["a_sc"][:, (tt * 2 + 1) * 128:(tt * 2 + 1) * 128 + 128],
                            in0=pbQ[:, 256:384], in1=t["c_triu"][:], op=ALU.mult)
                        sc_v[tt] = self.inc(ins, "eV")

                    @block.tensor
                    def _(tn, c=c, tt=tt):
                        pbP, pbQ = pb[(tt % 2) * 2], pb[(tt % 2) * 2 + 1]
                        tn.wait_ge(self.sems["eV"], sc_v[tt])
                        for a in range(2):
                            tn.matmul(pbQ[64 * a:64 * a + 64, 128:256],
                                      t["a_vtm"][:, c * DM + tt * 128 + 64 * a:
                                                 c * DM + tt * 128 + 64 * a + 64],
                                      t["a_sc"][:, (tt * 2 + a) * 128:
                                                (tt * 2 + a) * 128 + 128],
                                      start=True, stop=True)
                        ins = tn.matmul(pbP[0:128, 256:384],
                                        t["a_ktm"][:, c * DM + tt * 128:
                                                   c * DM + tt * 128 + 128],
                                        t["a_vtm"][:, c * DM + tt * 128:
                                                   c * DM + tt * 128 + 128],
                                        start=True, stop=True)
                        self.inc(ins, "eT")
                    mm2 = self.counts["eT"]

                    @block.vector
                    def _(v, c=c, tt=tt, mm2=mm2):
                        pbP, pbQ = pb[(tt % 2) * 2], pb[(tt % 2) * 2 + 1]
                        v.wait_ge(self.sems["eT"], mm2)
                        zr = t["a_zrep"][:, tt * TSEG + c * 128:
                                         tt * TSEG + c * 128 + 128]
                        v.tensor_tensor(out=t["a_t1"][:, 0:128], in0=pbQ[:, 0:128],
                                        in1=zr, op=ALU.mult)
                        v.tensor_tensor(out=t["a_t2"][:, 0:128], in0=pbQ[:, 128:256],
                                        in1=zr, op=ALU.mult)
                        ins = v.tensor_tensor(out=t["a_y"][:, c * DM + tt * 128:
                                                           c * DM + tt * 128 + 128],
                                              in0=t["a_t1"][:, 0:128],
                                              in1=t["a_t2"][:, 0:128], op=ALU.add)
                        if "sadd" not in skip:
                            v.tensor_tensor(out=t["a_S"][0:64, tt * 64:(tt + 1) * 64],
                                            in0=t["a_S"][0:64, tt * 64:(tt + 1) * 64],
                                            in1=pbP[0:64, 256:320], op=ALU.add)
                            ins = v.tensor_tensor(out=t["a_S"][64:128, tt * 64:(tt + 1) * 64],
                                                  in0=t["a_S"][64:128, tt * 64:(tt + 1) * 64],
                                                  in1=pbP[64:128, 320:384], op=ALU.add)
                        y_v[tt] = self.inc(ins, "eV")

                    @block.scalar
                    def _(sc, c=c, tt=tt):
                        sc.wait_ge(self.sems["eV"], y_v[tt])
                        if "sbf" not in skip:
                            sc.activation(out=t["a_Sbf"][0:64, tt * 128: tt * 128 + 64],
                                          in_=t["a_S"][0:64, tt * 64:(tt + 1) * 64],
                                          func=ACT.Identity)
                            ins = sc.activation(out=t["a_Sbf"][64:128, tt * 128 + 64: tt * 128 + 128],
                                                in_=t["a_S"][64:128, tt * 64:(tt + 1) * 64],
                                                func=ACT.Identity)
                        else:
                            ins = sc.activation(out=t["a_t3"][0:128, 0:64],
                                                in_=t["a_S"][0:128, 0:64],
                                                func=ACT.Identity)
                        sbf[tt] = self.inc(ins, "eS")

                @block.tensor
                def _(tn, c=c):
                    tn.wait_ge(self.sems["eV"], y_v[7])
                    tn.wait_ge(self.sems["eV"], y_v[6])
                    for dh in range(2):
                        bank = pb[dh * 4]
                        for kt in range(8):
                            ins = tn.matmul(
                                bank[:, 0:512],
                                t["a_y"][:, c * DM + kt * 128: c * DM + kt * 128 + 128],
                                t["aw_o"][:, kt * DM + dh * 512: kt * DM + dh * 512 + 512],
                                start=(kt == 0), stop=(kt == 7))
                    self.inc(ins, "eT")
                op_eT = self.counts["eT"]

                @block.vector
                def _(v, c=c, op_eT=op_eT):
                    v.wait_ge(self.sems["eT"], op_eT)
                    v.wait_ge(self.sems["sG"], self.counts["sG"])
                    for dh in range(2):
                        ins = v.tensor_tensor(out=t["a_s"][:, dh * 512:(dh + 1) * 512],
                                              in0=pb[dh * 4][:, 0:512],
                                              in1=t["c_bo"][:, dh * 512:(dh + 1) * 512],
                                              op=ALU.add)
                    self.inc(ins, "eV")
                s_eV = self.counts["eV"]

                @block.gpsimd
                def _(g, c=c, seg=seg, s_eV=s_eV):
                    g.wait_ge(self.sems["eV"], s_eV)
                    self.dma(g, t["s_dram"][t0 + c * 128: t0 + c * 128 + 128, :],
                             t["a_s"][:], "sG")

    def ph_routing(self, layer):
        t, block = self.t, self.block
        E = E0 if layer == 0 else E1
        CG = CAP0G if layer == 0 else CAP1G
        LCAP = LCAP0 if layer == 0 else LCAP1
        SLOTS = SLOTS0 if layer == 0 else SLOTS1
        ecap = "c_ecap0" if layer == 0 else "c_ecap1"
        trashf = "c_trash0f" if layer == 0 else "c_trash1f"
        NE = NCHUNK * E
        HUGE = 1.0e6
        for n in ("r_lgtm", "r_m1tm", "r_m2tm", "r_s1tm", "r_s2tm",
                  "r_k1tm", "r_k2tm", "r_dif", "r_ttmp"):
            self.sbl(n, (128, NE), F32)
        for n in ("r_max", "r_sum", "r_g1", "r_max2", "r_g2", "r_den",
                  "r_f1f", "r_gk", "r_f2f"):
            self.sbl(n, (128, NCHUNK), F32)
        self.sbl("r_msk", (128, NCHUNK), U8)
        for n in ("r_m1fm", "r_m2fm", "r_loc1", "r_loc2", "r_k1fm",
                  "r_k2fm", "r_ftmp", "r_lgf"):
            self.sbl(n, (16, T_TOK), F32)
        self.sbl("r_ag", (self.ncores, 64), F32)
        self.sbl("r_pr", (1, 128), F32)
        self.sbl("r_sc", (16, 8), F32)
        self.sbl("r_xn", (128, DM), BF16)
        pba = self.psl("r_pba", (128, 512))
        pbb = self.psl("r_pbb", (128, 512))

        def e3(name):
            return t[name][:].rearrange("p (c e) -> p c e", e=E)

        def bc(name):
            return t[name][:].unsqueeze(2).broadcast_to((128, NCHUNK, E))

        @block.vector
        def _(v):
            ins = v.tensor_copy(out=t["r_lgf"][0:E, :], in_=t["lg_fm"][0:E, :])
            self.inc(ins, "eV")
        lgf_eV = self.counts["eV"]
        tr_v = []
        for c in range(NCHUNK):
            @block.tensor
            def _(tn, c=c):
                bank = pba if c % 2 == 0 else pbb
                tn.wait_ge(self.sems["eV"], tr_v[c - 2] if c >= 2 else lgf_eV)
                ins = tn.transpose(bank[0:128, 0:E],
                                   t["r_lgf"][0:E, c * 128:(c + 1) * 128],
                                   t["c_eye"][0:E, 0:E])
                self.inc(ins, "eT")
            cnt = self.counts["eT"]

            @block.vector
            def _(v, c=c, cnt=cnt):
                bank = pba if c % 2 == 0 else pbb
                v.wait_ge(self.sems["eT"], cnt)
                ins = v.tensor_copy(out=t["r_lgtm"][:, c * E:(c + 1) * E],
                                    in_=bank[0:128, 0:E])
                tr_v.append(self.inc(ins, "eV"))

        def vtree(v, src, scratch, op, out2):
            w = E // 2
            v.tensor_tensor(out=scratch[:, :, 0:w], in0=src[:, :, 0:w],
                            in1=src[:, :, w:2 * w], op=op)
            while w > 2:
                w2 = w // 2
                v.tensor_tensor(out=scratch[:, :, 0:w2], in0=scratch[:, :, 0:w2],
                                in1=scratch[:, :, w2:w], op=op)
                w = w2
            v.tensor_tensor(out=out2.unsqueeze(2), in0=scratch[:, :, 0:1],
                            in1=scratch[:, :, 1:2], op=op)

        @block.vector
        def _(v):
            v.wait_ge(self.sems["eV"], tr_v[-1])
            self.vgap(v)
            vtree(v, e3("r_lgtm"), e3("r_ttmp"), ALU.max, t["r_max"][:])
            self.vgap(v)
            for e in range(E):
                v.tensor_tensor(out=e3("r_m1tm")[:, :, e], in0=e3("r_lgtm")[:, :, e],
                                in1=t["r_max"][:], op=ALU.is_ge)
            self.vgap(v)
            v.scalar_tensor_tensor(out=t["r_ttmp"][:], in0=t["r_m1tm"][:],
                                   scalar=-1.0e30, in1=t["r_lgtm"][:],
                                   op0=ALU.mult, op1=ALU.add)
            vtree(v, e3("r_ttmp"), e3("r_dif"), ALU.max, t["r_max2"][:])
            self.vgap(v)
            for e in range(E):
                v.tensor_tensor(out=e3("r_m2tm")[:, :, e], in0=e3("r_ttmp")[:, :, e],
                                in1=t["r_max2"][:], op=ALU.is_ge)
            v.tensor_tensor(out=e3("r_dif"), in0=e3("r_lgtm"),
                            in1=bc("r_max"), op=ALU.subtract)
            ins = v.tensor_tensor(out=t["r_max2"][:], in0=t["r_max2"][:],
                                  in1=t["r_max"][:], op=ALU.subtract)
            self.inc(ins, "eV")

        @block.scalar
        def _(sc):
            sc.wait_ge(self.sems["eV"], self.counts["eV"])
            sc.activation(out=t["r_dif"][:], in_=t["r_dif"][:], func=ACT.Exp)
            ins = sc.activation(out=t["r_g2"][:], in_=t["r_max2"][:], func=ACT.Exp)
            self.inc(ins, "eS")
        exp_eS = self.counts["eS"]

        @block.vector
        def _(v):
            v.wait_ge(self.sems["eS"], exp_eS)
            vtree(v, e3("r_dif"), e3("r_ttmp"), ALU.add, t["r_sum"][:])
            self.vgap(v)
            v.reciprocal(t["r_g1"][:], t["r_sum"][:])
            self.vgap(v)
            v.tensor_tensor(out=t["r_g2"][:], in0=t["r_g2"][:], in1=t["r_g1"][:],
                            op=ALU.mult)
            if layer == 0:
                self.vgap(v)
                v.tensor_tensor(out=t["r_den"][:], in0=t["r_g1"][:],
                                in1=t["r_g2"][:], op=ALU.add)
                self.vgap(v)
                v.tensor_scalar_max(t["r_den"][:], t["r_den"][:], 1e-9)
                self.vgap(v)
                v.reciprocal(t["r_den"][:], t["r_den"][:])
                self.vgap(v)
                v.tensor_tensor(out=t["r_g1"][:], in0=t["r_g1"][:],
                                in1=t["r_den"][:], op=ALU.mult)
                ins = v.tensor_tensor(out=t["r_g2"][:], in0=t["r_g2"][:],
                                      in1=t["r_den"][:], op=ALU.mult)
            else:
                ins = v.tensor_copy(out=t["r_den"][:], in_=t["r_g1"][:])
            self.inc(ins, "eV")
        gates_eV = self.counts["eV"]

        plan3 = [("r_m1tm", "r_m1fm")] + ([("r_m2tm", "r_m2fm")] if layer == 0 else [])
        m2f_v = []
        for c in range(NCHUNK):
            for wi, (src, dst) in enumerate(plan3):
                idx = c * len(plan3) + wi

                @block.tensor
                def _(tn, c=c, src=src, idx=idx):
                    bank = pba if idx % 2 == 0 else pbb
                    if idx < 2:
                        tn.wait_ge(self.sems["eV"], gates_eV)
                    else:
                        tn.wait_ge(self.sems["eV"], m2f_v[idx - 2])
                    ins = tn.transpose(bank[0:E, 0:128],
                                       t[src][:, c * E:(c + 1) * E],
                                       t["c_eye"][:])
                    self.inc(ins, "eT")
                cnt = self.counts["eT"]

                @block.vector
                def _(v, c=c, dst=dst, idx=idx, cnt=cnt):
                    bank = pba if idx % 2 == 0 else pbb
                    v.wait_ge(self.sems["eT"], cnt)
                    ins = v.tensor_copy(out=t[dst][0:E, c * 128:(c + 1) * 128],
                                        in_=bank[0:E, 0:128])
                    m2f_v.append(self.inc(ins, "eV"))

        @block.vector
        def _(v):
            v.wait_ge(self.sems["eV"], m2f_v[-1])
            v.tensor_tensor_scan(out=t["r_loc1"][0:E, :], data0=t["r_m1fm"][0:E, :],
                                 data1=t["r_m1fm"][0:E, :], initial=0.0,
                                 op0=ALU.add, op1=ALU.bypass)
            if layer == 0:
                v.tensor_tensor_scan(out=t["r_loc2"][0:E, :],
                                     data0=t["r_m2fm"][0:E, :],
                                     data1=t["r_m2fm"][0:E, :], initial=0.0,
                                     op0=ALU.add, op1=ALU.bypass)
            v.tensor_copy(out=t["r_sc"][0:E, 6:7],
                          in_=t["r_loc1"][0:E, T_TOK - 1:T_TOK])
            self.vgap(v)
            ins = v.tensor_scalar_add(t["r_sc"][0:E, 7:8], t["r_sc"][0:E, 6:7], -1.0)
            self.inc(ins, "eV")
        scan_eV = self.counts["eV"]

        @block.gpsimd
        def _(g):
            g.wait_ge(self.sems["eV"], scan_eV)
            self.dma(g, t["cc_in"][0:1, 0:E], t["r_loc1"][0:E, T_TOK - 1:T_TOK], "sG")
            if layer == 0:
                self.dma(g, t["cc_in"][0:1, E:2 * E],
                         t["r_loc2"][0:E, T_TOK - 1:T_TOK], "sG")
            g.wait_ge(self.sems["sG"], self.counts["sG"])
            ins = g.collective_compute(
                "AllGather", ALU.bypass, replica_groups=[list(range(self.ncores))],
                ins=[t["cc_in"].opt()], outs=[t["cc_out"].opt()])
            self.inc(ins, "sCC")
            g.wait_ge(self.sems["sCC"], self.counts["sCC"])
            self.dma(g, t["r_ag"][:], t["cc_out"], "sG")
            g.wait_ge(self.sems["sG"], self.counts["sG"])
            ins = g.iota(t["c_scr"][0:1, 0:1], [[0, 1]], base=0, channel_multiplier=0)
            self.inc(ins, "eG")
        ag_eG = self.counts["eG"]

        @block.tensor
        def _(tn):
            tn.wait_ge(self.sems["eG"], ag_eG)
            tn.matmul(pba[0:1, 0:64], t["c_pref"][:], t["r_ag"][:],
                      start=True, stop=True)
            ins = tn.matmul(pba[0:1, 64:128], t["c_ones8"][:], t["r_ag"][:],
                            start=True, stop=True)
            self.inc(ins, "eT")
        pr_eT = self.counts["eT"]

        @block.vector
        def _(v):
            v.wait_ge(self.sems["eT"], pr_eT)
            ins = v.tensor_copy(out=t["r_pr"][0:1, 0:128], in_=pba[0:1, 0:128])
            self.inc(ins, "eV")
        pr_eV = self.counts["eV"]

        @block.gpsimd
        def _(g):
            g.wait_ge(self.sems["eV"], pr_eV)
            self.dma(g, t["row_dram"][0:2, 0:64], t["r_pr"][0:1, 0:128], "sG")
            g.wait_ge(self.sems["sG"], self.counts["sG"])
            self.dma(g, t["r_sc"][0:E, 0:1], t["row_dram"][0:1, 0:E], "sG")
            if layer == 0:
                self.dma(g, t["r_sc"][0:E, 1:2], t["row_dram"][0:1, E:2 * E], "sG")
            self.dma(g, t["r_sc"][0:E, 2:3], t["row_dram"][1:2, 0:E], "sG")
            g.wait_ge(self.sems["sG"], self.counts["sG"])
            ins = g.iota(t["c_scr"][0:1, 1:2], [[0, 1]], base=0, channel_multiplier=0)
            self.inc(ins, "eG")
        sc_eG = self.counts["eG"]

        @block.vector
        def _(v):
            v.wait_ge(self.sems["eG"], sc_eG)
            sl = t["r_sc"]

            def choice(locn, kfm, prefcols):
                v.tensor_scalar(out=t["r_ftmp"][0:E, :], in0=t[locn][0:E, :],
                                scalar1=prefcols[0], scalar2=None, op0=ALU.add)
                if len(prefcols) > 1:
                    v.tensor_scalar(out=t["r_ftmp"][0:E, :], in0=t["r_ftmp"][0:E, :],
                                    scalar1=prefcols[1], scalar2=None, op0=ALU.add)
                v.tensor_single_scalar(out=t[kfm][0:E, :], in_=t["r_ftmp"][0:E, :],
                                       scalar=CG + 0.5, op=ALU.is_lt)
                if locn == "r_loc2":
                    v.tensor_scalar(out=t[locn][0:E, :], in0=t[locn][0:E, :],
                                    scalar1=sl[0:E, 7:8], scalar2=None, op0=ALU.add)
                v.tensor_scalar_add(t[locn][0:E, :], t[locn][0:E, :], -1.0)
                v.tensor_single_scalar(out=t["r_ftmp"][0:E, :], in_=t[locn][0:E, :],
                                       scalar=LCAP - 0.5, op=ALU.is_lt)
                v.tensor_tensor(out=t[kfm][0:E, :], in0=t[kfm][0:E, :],
                                in1=t["r_ftmp"][0:E, :], op=ALU.mult)
                v.tensor_single_scalar(out=t["r_ftmp"][0:E, :], in_=t[locn][0:E, :],
                                       scalar=LCAP - 0.5, op=ALU.is_ge)
                v.tensor_scalar(out=t[locn][0:E, :], in0=t[locn][0:E, :],
                                scalar1=t[ecap][0:E, 0:1], scalar2=None, op0=ALU.add)
                v.scalar_tensor_tensor(out=t[locn][0:E, :], in0=t["r_ftmp"][0:E, :],
                                       scalar=HUGE, in1=t[locn][0:E, :],
                                       op0=ALU.mult, op1=ALU.add)
            choice("r_loc1", "r_k1fm", [sl[0:E, 0:1]])
            if layer == 0:
                choice("r_loc2", "r_k2fm", [sl[0:E, 1:2], sl[0:E, 2:3]])
            ins = v.tensor_copy(out=t["r_ftmp"][0:1, 0:1], in_=t["r_ftmp"][0:1, 0:1])
            self.inc(ins, "eV")
        fm_eV = self.counts["eV"]

        plan9 = [("r_loc1", "r_s1tm"), ("r_k1fm", "r_k1tm")]
        if layer == 0:
            plan9 += [("r_loc2", "r_s2tm"), ("r_k2fm", "r_k2tm")]
        s2t_v = []
        for c in range(NCHUNK):
            for wi, (src, dst) in enumerate(plan9):
                idx = c * len(plan9) + wi

                @block.tensor
                def _(tn, c=c, src=src, idx=idx):
                    bank = pba if idx % 2 == 0 else pbb
                    if idx < 2:
                        tn.wait_ge(self.sems["eV"], fm_eV)
                    else:
                        tn.wait_ge(self.sems["eV"], s2t_v[idx - 2])
                    ins = tn.transpose(bank[0:128, 0:E],
                                       t[src][0:E, c * 128:(c + 1) * 128],
                                       t["c_eye"][0:E, 0:E])
                    self.inc(ins, "eT")
                cnt = self.counts["eT"]

                @block.vector
                def _(v, c=c, dst=dst, idx=idx, cnt=cnt):
                    bank = pba if idx % 2 == 0 else pbb
                    v.wait_ge(self.sems["eT"], cnt)
                    ins = v.tensor_copy(out=t[dst][:, c * E:(c + 1) * E],
                                        in_=bank[0:128, 0:E])
                    s2t_v.append(self.inc(ins, "eV"))

        @block.vector
        def _(v):
            v.wait_ge(self.sems["eV"], s2t_v[-1])
            self.vgap(v)

            def tok(mtm, stm, ktm, fdst, gsrc, gdst, fcol):
                A, B = e3("r_ttmp"), e3("r_dif")
                v.tensor_tensor(out=t["r_ttmp"][:], in0=t[mtm][:], in1=t[stm][:],
                                op=ALU.mult)
                v.tensor_tensor(out=t["r_dif"][:], in0=t[mtm][:], in1=t[ktm][:],
                                op=ALU.mult)
                w = E // 2
                while w > 1:
                    v.tensor_tensor(out=A[:, :, 0:w], in0=A[:, :, 0:w],
                                    in1=A[:, :, w:2 * w], op=ALU.add)
                    v.tensor_tensor(out=B[:, :, 0:w], in0=B[:, :, 0:w],
                                    in1=B[:, :, w:2 * w], op=ALU.add)
                    w //= 2
                v.tensor_tensor(out=t[fdst][:].unsqueeze(2), in0=A[:, :, 0:1],
                                in1=A[:, :, 1:2], op=ALU.add)
                v.tensor_tensor(out=t["r_gk"][:].unsqueeze(2), in0=B[:, :, 0:1],
                                in1=B[:, :, 1:2], op=ALU.add)
                self.vgap(v)
                v.tensor_tensor(out=self.t[gdst][:], in0=t[gsrc][:],
                                in1=t["r_gk"][:], op=ALU.mult)
                v.tensor_single_scalar(out=t["r_max"][:], in_=t[fdst][:],
                                       scalar=float(SLOTS) - 0.5, op=ALU.is_ge)
                self.vgap(v)
                v.tensor_copy(out=t["r_msk"][:], in_=t["r_max"][:])
                self.vgap(v)
                v.select(out=t[fdst][:], mask=t["r_msk"][:],
                         on_true=t[trashf][:, 0:1].broadcast_to((128, NCHUNK)),
                         on_false=t[fdst][:])
                self.vgap(v)
                v.tensor_copy(out=self.t[fcol][:], in_=t[fdst][:])
            tok("r_m1tm", "r_s1tm", "r_k1tm", "r_f1f", "r_g1", "g1_col", "f1_col")
            if layer == 0:
                tok("r_m2tm", "r_s2tm", "r_k2tm", "r_f2f", "r_g2", "g2_col",
                    "f2_col")
            ins = v.tensor_copy(out=t["r_max"][0:1, 0:1], in_=t["r_max"][0:1, 0:1])
            self.inc(ins, "eV")
        rt_eV = self.counts["eV"]

        xn_ld = []
        for c in range(NCHUNK):
            @block.sync
            def _(sync, c=c):
                if c >= 1:
                    sync.wait_ge(self.sems["sG"], self.counts["sG"])
                xn_ld.append(self.dma(sync, t["r_xn"][:],
                                      t["xn_dram"][c * 128:(c + 1) * 128, :], "sQ"))

            @block.gpsimd
            def _(g, c=c):
                if c == 0:
                    g.wait_ge(self.sems["eV"], rt_eV)
                g.wait_ge(self.sems["sQ"], xn_ld[c])
                self.idma(g, t["disp_dram"], t["r_xn"][:], "sG",
                          out_offset=bass.IndirectOffsetOnAxis(
                              self.t["f1_col"][:, c:c + 1], 0))
                if layer == 0:
                    self.idma(g, t["disp_dram"], t["r_xn"][:], "sG",
                              out_offset=bass.IndirectOffsetOnAxis(
                                  self.t["f2_col"][:, c:c + 1], 0))

    def ph_moe(self, layer):
        t, block = self.t, self.block
        E = E0 if layer == 0 else E1
        LCAP = LCAP0 if layer == 0 else LCAP1
        wname1 = "w_gate0" if layer == 0 else "w1"
        wname2 = "w_down0" if layer == 0 else "w2"
        gfun = ACT.Gelu if layer == 0 else ACT.Gelu_apprx_tanh
        self.sbl("e_dfm", (128, 8 * 512), BF16)
        self.sbl("e_w1", (128, 8 * 512), BF16)
        self.sbl("e_w1b", (128, 8 * 512), BF16)
        self.sbl("e_w1s", (128, 8 * 512), F32)
        self.sbl("e_w1sb", (128, 8 * 512), F32)
        self.sbl("e_h", (128, 32 * 512), BF16)
        self.sbl("e_w2", (128, 1024), BF16)
        self.sbl("e_w2b", (128, 1024), BF16)
        self.sbl("e_w2s", (128, 1024), F32)
        self.sbl("e_w2sb", (128, 1024), F32)
        self.sbl("e_eo", (128, 4 * 1024), BF16)
        pb = [self.psl(f"e_pb{i}") for i in range(8)]
        groups = []
        n = LCAP
        off = 0
        while n > 0:
            g_ = min(512, n)
            groups.append((off, g_))
            off += g_
            n -= g_

        w1cast, w1mm, w2cast, w2mm = {}, {}, {}, {}
        for e in range(E):
            for gi, (goff, gN) in enumerate(groups):
                gg = len(groups) * e + gi
                s0 = e * LCAP + goff
                dld = []

                @block.sync
                def _(sync, s0=s0, gN=gN, e=e, gi=gi):
                    sync.wait_ge(self.sems["eT"], self.counts["eT"])
                    for dtl in range(8):
                        dld.append(self.trdma(
                            sync, t["e_dfm"][:, dtl * 512: dtl * 512 + gN],
                            t["disp_dram"][s0:s0 + gN, dtl * 128:(dtl + 1) * 128],
                            "sQ"))

                for fs in range(8):
                    fq = gg * 8 + fs
                    wbuf = "e_w1" if fs % 2 == 0 else "e_w1b"
                    wstg = "e_w1s" if fs % 2 == 0 else "e_w1sb"

                    @block.sync
                    def _(sync, e=e, fs=fs, fq=fq, wstg=wstg):
                        if fq >= 2:
                            sync.wait_ge(self.sems["eV"], w1cast[fq - 2])
                        for dtl in range(8):
                            self.dma(sync, t[wstg][:, dtl * 512:(dtl + 1) * 512],
                                     t[wname1][e, dtl * 128:(dtl + 1) * 128,
                                               fs * 512:(fs + 1) * 512], "sQ")
                    w_sQ = self.counts["sQ"]

                    @block.vector
                    def _(v, fq=fq, wbuf=wbuf, wstg=wstg, w_sQ=w_sQ):
                        v.wait_ge(self.sems["sQ"], w_sQ)
                        if fq >= 2:
                            v.wait_ge(self.sems["eT"], w1mm[fq - 2])
                        ins = v.tensor_copy(out=t[wbuf][:], in_=t[wstg][:])
                        w1cast[fq] = self.inc(ins, "eV")

                    for fc in range(4):
                        @block.tensor
                        def _(tn, fs=fs, fc=fc, fq=fq, gN=gN, wbuf=wbuf):
                            bank = pb[(fs * 4 + fc) % 4]
                            if fs == 0 and fc == 0:
                                tn.wait_ge(self.sems["sQ"], dld[-1])
                            tn.wait_ge(self.sems["eV"], w1cast[fq])
                            tn.wait_ge(self.sems["eS"], self.counts["eS"])
                            for dtl in range(8):
                                ins = tn.matmul(
                                    bank[:, 0:gN],
                                    t[wbuf][:, dtl * 512 + fc * 128:
                                            dtl * 512 + fc * 128 + 128],
                                    t["e_dfm"][:, dtl * 512: dtl * 512 + gN],
                                    start=(dtl == 0), stop=(dtl == 7))
                            self.inc(ins, "eT")
                        mmc = self.counts["eT"]

                        @block.scalar
                        def _(sc, fs=fs, fc=fc, gN=gN, mmc=mmc):
                            bank = pb[(fs * 4 + fc) % 4]
                            sc.wait_ge(self.sems["eT"], mmc)
                            ins = sc.activation(
                                out=t["e_h"][:, (fs * 4 + fc) * 512:
                                             (fs * 4 + fc) * 512 + gN],
                                in_=bank[:, 0:gN], func=gfun)
                            self.inc(ins, "eS")
                    w1mm[fq] = self.counts["eT"]
                h_eS = self.counts["eS"]

                nsub = (gN + 127) // 128
                for ft in range(32):
                    fq = gg * 32 + ft
                    wbuf = "e_w2" if ft % 2 == 0 else "e_w2b"
                    wstg = "e_w2s" if ft % 2 == 0 else "e_w2sb"

                    @block.scalar
                    def _(sc, e=e, ft=ft, fq=fq, wstg=wstg):
                        if fq >= 2:
                            sc.wait_ge(self.sems["eV"], w2cast[fq - 2])
                        self.dma(sc, t[wstg][:],
                                 t[wname2][e, ft * 128:(ft + 1) * 128, :], "sA")
                    w2_sA = self.counts["sA"]

                    @block.vector
                    def _(v, fq=fq, wbuf=wbuf, wstg=wstg, w2_sA=w2_sA):
                        v.wait_ge(self.sems["sA"], w2_sA)
                        if fq >= 2:
                            v.wait_ge(self.sems["eT"], w2mm[fq - 2])
                        ins = v.tensor_copy(out=t[wbuf][:], in_=t[wstg][:])
                        w2cast[fq] = self.inc(ins, "eV")

                    @block.tensor
                    def _(tn, ft=ft, fq=fq, gN=gN, nsub=nsub, wbuf=wbuf,
                          h_eS=h_eS):
                        tn.wait_ge(self.sems["eV"], w2cast[fq])
                        if ft == 0:
                            tn.wait_ge(self.sems["eS"], h_eS)
                        for sub in range(nsub):
                            sw = min(128, gN - sub * 128)
                            for dh in range(2):
                                ins = tn.matmul(
                                    pb[sub * 2 + dh][0:sw, 0:512],
                                    t["e_h"][:, ft * 512 + sub * 128:
                                             ft * 512 + sub * 128 + sw],
                                    t[wbuf][:, dh * 512:(dh + 1) * 512],
                                    start=(ft == 0), stop=(ft == 31))
                        w2mm[fq] = self.inc(ins, "eT")
                mm2 = self.counts["eT"]

                @block.vector
                def _(v, gN=gN, nsub=nsub, mm2=mm2):
                    v.wait_ge(self.sems["eT"], mm2)
                    for sub in range(nsub):
                        sw = min(128, gN - sub * 128)
                        for dh in range(2):
                            ins = v.tensor_copy(
                                out=t["e_eo"][0:sw, sub * 1024 + dh * 512:
                                              sub * 1024 + dh * 512 + 512],
                                in_=pb[sub * 2 + dh][0:sw, 0:512])
                    self.inc(ins, "eV")
                eo_eV = self.counts["eV"]

                @block.scalar
                def _(sc, s0=s0, gN=gN, nsub=nsub, eo_eV=eo_eV):
                    sc.wait_ge(self.sems["eV"], eo_eV)
                    for sub in range(nsub):
                        sw = min(128, gN - sub * 128)
                        self.dma(sc, t["eo_dram"][s0 + sub * 128:
                                                  s0 + sub * 128 + sw, :],
                                 t["e_eo"][0:sw, sub * 1024: sub * 1024 + 1024],
                                 "sA")

    def ph_combine(self, layer):
        t, block = self.t, self.block
        self.sbl("cb_eoA", (128, DM), BF16)
        self.sbl("cb_eoB", (128, DM), BF16)
        self.sbl("cb_s", (128, DM), BF16)
        self.sbl("cb_x", (128, DM), BF16)
        self.sbl("cb_g", (128, DM), F32)
        self.sbl("cb_gb", (128, DM), BF16)
        self.sbl("cb_o", (128, DM), BF16 if layer == 0 else F32)
        for c in range(NCHUNK):
            ld = {}

            @block.gpsimd
            def _(g, c=c):
                g.wait_ge(self.sems["eV"], self.counts["eV"])
                ld["A"] = self.idma(g, t["cb_eoA"][:], t["eo_dram"], "sG",
                                    in_offset=bass.IndirectOffsetOnAxis(
                                        self.t["f1_col"][:, c:c + 1], 0))
                if layer == 0:
                    ld["B"] = self.idma(g, t["cb_eoB"][:], t["eo_dram"], "sG",
                                        in_offset=bass.IndirectOffsetOnAxis(
                                            self.t["f2_col"][:, c:c + 1], 0))

            @block.sync
            def _(sync, c=c):
                sync.wait_ge(self.sems["eV"], self.counts["eV"])
                ld["s"] = self.dma(sync, t["cb_s"][:],
                                   t["s_dram"][c * 128:(c + 1) * 128, :], "sQ")
                ld["x"] = self.dma(sync, t["cb_x"][:],
                                   t["x_dram"][c * 128:(c + 1) * 128, :], "sQ")

            @block.vector
            def _(v, c=c):
                v.wait_ge(self.sems["sG"], ld["B"] if layer == 0 else ld["A"])
                v.wait_ge(self.sems["eS"], self.counts["eS"])
                ins = v.tensor_scalar(out=t["cb_g"][:], in0=t["cb_eoA"][:],
                                      scalar1=self.t["g1_col"][:, c:c + 1],
                                      scalar2=None, op0=ALU.mult)
                if layer == 0:
                    ins = v.scalar_tensor_tensor(
                        out=t["cb_g"][:], in0=t["cb_eoB"][:],
                        scalar=self.t["g2_col"][:, c:c + 1], in1=t["cb_g"][:],
                        op0=ALU.mult, op1=ALU.add)
                self.inc(ins, "eV")
            g_eV = self.counts["eV"]

            @block.scalar
            def _(sc, c=c, g_eV=g_eV):
                sc.wait_ge(self.sems["eV"], g_eV)
                ins = sc.activation(out=t["cb_gb"][:], in_=t["cb_g"][:],
                                    func=ACT.Silu)
                self.inc(ins, "eS")
            sg_eS = self.counts["eS"]

            @block.vector
            def _(v, c=c, sg_eS=sg_eS):
                v.wait_ge(self.sems["eS"], sg_eS)
                v.wait_ge(self.sems["sQ"], ld["x"])
                v.wait_ge(self.sems["sA"], self.counts["sA"])
                v.tensor_tensor(out=t["cb_g"][:], in0=t["cb_s"][:],
                                in1=t["cb_gb"][:], op=ALU.mult)
                ins = v.tensor_tensor(out=t["cb_o"][:], in0=t["cb_x"][:],
                                      in1=t["cb_g"][:], op=ALU.add)
                self.inc(ins, "eV")
            o_eV = self.counts["eV"]

            @block.scalar
            def _(sc, c=c, o_eV=o_eV):
                sc.wait_ge(self.sems["eV"], o_eV)
                if layer == 0:
                    self.dma(sc, t["x_dram"][c * 128:(c + 1) * 128, :],
                             t["cb_o"][:], "sA")
                else:
                    self.dma(sc, t["out"][c * 128:(c + 1) * 128, :],
                             t["cb_o"][:], "sA")



def kernel(**inputs):
    from concourse.bass_utils import run_bass_kernel_spmd
    x = np.asarray(inputs["L0_attn_x"], dtype=np.float32)
    mk = MK()
    nc = mk.build()
    in_maps = [host_inputs(b, x[b], inputs, ncores=8) for b in range(8)]
    res = run_bass_kernel_spmd(nc, in_maps, core_ids=list(range(8)))
    return np.stack([np.asarray(res.results[b]["out"], dtype=np.float32)
                     for b in range(8)], axis=0)
